# version 01
# baseline (speedup 1.0000x reference)
import numpy as np
import ml_dtypes
from contextlib import ExitStack
import concourse.bass as bass
import concourse.mybir as mybir
from concourse.bass_utils import run_bass_kernel_spmd

F32 = mybir.dt.float32
BF16 = mybir.dt.bfloat16
I32 = mybir.dt.int32
AF = mybir.ActivationFunctionType
ALU = mybir.AluOpType
AX = mybir.AxisListType
BF = ml_dtypes.bfloat16

N_CORES = 8
B, S, D = 2, 8192, 1024
T = S * B // N_CORES
INNER = 2048
HEADS, STATE, HDIM = 8, 64, 256
AH, AHD = 16, 64
FI = 2816
EPS = 1e-5

ENGINES = ("tensor", "vector", "scalar", "gpsimd", "sync")
DMA_POOL = 12


class Prog:
    def __init__(self, nc, es):
        self.nc = nc
        self.ops = []
        self.last_write = {}
        self.readers = {}
        self.sems = {e: es.enter_context(nc.semaphore(f"S_{e}")) for e in ENGINES}
        self.dma_sems = {e: [es.enter_context(nc.semaphore(f"D_{e}_{i}"))
                             for i in range(DMA_POOL)] for e in ENGINES}
        self.dma_count = {e: 0 for e in ENGINES}

    def op(self, engine, builder, reads=(), writes=(), dma=False):
        deps = set()
        for k in reads:
            if k in self.last_write:
                deps.add(self.last_write[k])
        for k in writes:
            if k in self.last_write:
                deps.add(self.last_write[k])
            for r in self.readers.get(k, ()):
                deps.add(r)
        i = len(self.ops)
        o = dict(id=i, engine=engine, builder=builder, deps=deps, dma=dma,
                 inc_no=None)
        if dma:
            n = self.dma_count[engine]
            self.dma_count[engine] = n + 1
            o["dsem"] = self.dma_sems[engine][n % DMA_POOL]
            o["duse"] = n // DMA_POOL + 1
        self.ops.append(o)
        for k in reads:
            self.readers.setdefault(k, []).append(i)
        for k in writes:
            self.last_write[k] = i
            self.readers[k] = []
        return i

    def dma(self, engine, out, in_, reads=(), writes=(), **kw):
        return self.op(engine, lambda e: e.dma_start(out=out, in_=in_, **kw),
                       reads=reads, writes=writes, dma=True)

    def finalize(self, block):
        needed = set()
        for o in self.ops:
            needed.update(o["deps"])
        counts = {e: 0 for e in ENGINES}
        for o in self.ops:
            if not o["dma"] and o["id"] in needed:
                counts[o["engine"]] += 1
                o["inc_no"] = counts[o["engine"]]
        per_engine = {e: [] for e in ENGINES}
        for o in self.ops:
            waits = {}
            for d in o["deps"]:
                p = self.ops[d]
                if (p["engine"] == "tensor" and o["engine"] == "tensor"
                        and not p["dma"] and not o["dma"]):
                    continue
                if p["dma"]:
                    s, v = p["dsem"], 16 * p["duse"]
                else:
                    s, v = self.sems[p["engine"]], p["inc_no"]
                if s.num not in waits or waits[s.num][1] < v:
                    waits[s.num] = (s, v)
            if o["dma"] and o["duse"] > 1:
                s = o["dsem"]
                if s.num not in waits:
                    waits[s.num] = (s, 16 * (o["duse"] - 1))
            o["waits"] = waits
            per_engine[o["engine"]].append(o)

        def mk_body(engine):
            def body(eng):
                satisfied = {}
                for o in per_engine[engine]:
                    for key, (s, v) in sorted(o["waits"].items()):
                        if satisfied.get(key, -1) >= v:
                            continue
                        eng.wait_ge(s, v)
                        satisfied[key] = v
                    inst = o["builder"](eng)
                    if o["dma"]:
                        inst.then_inc(o["dsem"], 16)
                    elif o["inc_no"] is not None:
                        inst.then_inc(self.sems[engine], 1)
                if engine == "sync":
                    for e in ENGINES:
                        n = self.dma_count[e]
                        for i in range(min(n, DMA_POOL)):
                            uses = (n - i - 1) // DMA_POOL + 1
                            eng.wait_ge(self.dma_sems[e][i], 16 * uses)
            return body

        for e in ENGINES:
            getattr(block, e)(mk_body(e))


def make_consts(nc):
    c = {}
    def inl(name, arr):
        c[name] = nc.inline_tensor(np.ascontiguousarray(arr), name=name).ap()
    eye = np.eye(128)
    inl("id_bf", eye.astype(BF))
    inl("id_f32", eye.astype(np.float32))
    inl("ones_row_f32", np.ones((1, 128), np.float32))
    inl("ones_row_bf", np.ones((1, 128), BF))
    inl("ones_col_f32", np.ones((128, 1), np.float32))
    ut = (np.arange(128)[:, None] <= np.arange(128)[None, :]).astype(np.float32)
    inl("ut_f32", ut)
    inl("ones128_f32", np.ones((128, 128), np.float32))
    inl("maskD", np.where(ut > 0, 0.0, -1e30).astype(np.float32))
    kk = np.arange(128)[:, None]
    cc = np.arange(896)[None, :]
    inl("bigmask", np.where(kk + 384 > cc, -1e9, 0.0).astype(np.float32))
    sel = np.zeros((8, 8 * 128), BF)
    for e in range(8):
        sel[e, e * 128:(e + 1) * 128] = 1
    inl("sel8", sel)
    return c


def load_const(p, nc, consts, name, shape, dt):
    t = nc.alloc_sbuf_tensor("c_" + name, shape, dt)
    p.dma("sync", t[:], consts[name][:], writes=["c_" + name])
    return t


def prep_lhsT(wT):
    K, N = wT.shape
    assert K % 128 == 0
    return np.ascontiguousarray(
        wT.reshape(K // 128, 128, N).transpose(1, 0, 2).reshape(128, -1))


def fm_from(tokmajor):
    Tn, F = tokmajor.shape
    return np.ascontiguousarray(
        tokmajor.T.reshape(F // 128, 128, Tn).transpose(1, 0, 2).reshape(128, -1))


def fm_to(arr, Tn, F):
    return np.ascontiguousarray(
        arr.reshape(128, F // 128, Tn).transpose(1, 0, 2).reshape(F, Tn).T)


def build_A():
    nc = bass.Bass("TRN2", target_bir_lowering=False)
    es = ExitStack()
    consts = make_consts(nc)
    NT = 17
    TC = 2176

    xa_d = nc.dram_tensor("xa", [NT * 128, D], F32, kind="ExternalInput").ap()
    wx_d = nc.dram_tensor("wxT", [128, 8 * INNER], F32, kind="ExternalInput").ap()
    wz_d = nc.dram_tensor("wzT", [128, 8 * INNER], F32, kind="ExternalInput").ap()
    wb_d = nc.dram_tensor("wbT", [128, 16 * 512], F32, kind="ExternalInput").ap()
    wc_d = nc.dram_tensor("wcT", [128, 16 * 512], F32, kind="ExternalInput").ap()
    wdt_d = nc.dram_tensor("wdtT", [128, 16 * 8], F32, kind="ExternalInput").ap()
    convw_d = nc.dram_tensor("convw", [128, 64], F32, kind="ExternalInput").ap()
    convb_d = nc.dram_tensor("convb", [128, 16], F32, kind="ExternalInput").ap()
    bdt_d = nc.dram_tensor("bdt", [8, 1], F32, kind="ExternalInput").ap()

    xn_o = nc.dram_tensor("xn_o", [128, 8 * T], BF16, kind="ExternalOutput").ap()
    xlo_o = nc.dram_tensor("xlo_o", [128, 8 * T], BF16, kind="ExternalOutput").ap()
    sz_o = nc.dram_tensor("sz_o", [128, 16 * T], BF16, kind="ExternalOutput").ap()
    xc_o = nc.dram_tensor("xc_o", [128, 16 * T], BF16, kind="ExternalOutput").ap()
    b_o = nc.dram_tensor("b_o", [64, 8 * T], BF16, kind="ExternalOutput").ap()
    c_o = nc.dram_tensor("c_o", [64, 8 * T], BF16, kind="ExternalOutput").ap()
    dt_o = nc.dram_tensor("dt_o", [8, T], F32, kind="ExternalOutput").ap()

    sb = nc.alloc_sbuf_tensor
    ps = nc.alloc_psum_tensor
    p = Prog(nc, es)
    idb = load_const(p, nc, consts, "id_bf", [128, 128], BF16)

    xn = sb("xn", [128, 8 * TC], BF16)
    wbig = sb("wbig", [128, 8 * INNER], BF16)
    xt = sb("xt", [128, D], F32)
    xn32 = sb("xn32", [128, D], F32)
    xnb = sb("xnb", [128, D], BF16)
    xlo_t = sb("xlo_t", [128, D], BF16)
    xlo_st = sb("xlo_st", [128, D], BF16)
    mcol = sb("mcol", [128, 1], F32)
    vcol = sb("vcol", [128, 4], F32)
    sq = sb("sq", [128, D], F32)
    big = sb("big", [128, 16 * T], BF16)
    xpb = sb("xpb", [128, 16 * 515], BF16)
    wb2 = wbig[:, 0:16 * 512]
    wc2 = wbig[:, 16 * 512:16 * 1024]
    wdt2 = sb("wdt2", [128, 16 * 8], BF16)
    cw = sb("cw", [128, 64], F32)
    cb = sb("cb", [128, 16], F32)
    bdt = sb("bdt_s", [8, 1], F32)
    bstage = sb("bstage", [64, 8 * 512], BF16)
    cstage = sb("cstage", [64, 8 * 512], BF16)
    dts = sb("dts", [8, T], F32)
    dtt = sb("dtt", [8, 512], F32)
    spa = sb("spa", [8, 512], F32)
    cvt = sb("cvt", [128, 512], F32)

    pml = [ps(f"pm{i}", [128, 512], F32) for i in range(2)]
    ptr = ps("ptr", [128, 128], BF16)
    pb = ps("pb", [64, 512], F32)
    pdt = ps("pdt", [8, 512], F32)

    p.dma("gpsimd", cw[:], convw_d[:], writes=["cw"])
    p.dma("gpsimd", cb[:], convb_d[:], writes=["cb"])
    p.dma("gpsimd", bdt[:], bdt_d[:], writes=["bdt"])

    for t in range(NT):
        p.dma("sync", xt[:], xa_d[t * 128:(t + 1) * 128, :], writes=["xt"])
        p.op("vector", lambda e, xt=xt: e.reduce_sum(mcol[:], xt[:], axis=AX.X),
             reads=["xt"], writes=["mcol"])
        p.op("scalar", lambda e, xt=xt: e.activation(sq[:], xt[:], AF.Square),
             reads=["xt"], writes=["sq"])
        p.op("vector", lambda e: e.reduce_sum(vcol[:, 0:1], sq[:], axis=AX.X),
             reads=["sq"], writes=["vcol"])
        p.op("vector", lambda e: e.tensor_scalar(mcol[:], mcol[:], 1.0 / D, None, ALU.mult),
             reads=["mcol"], writes=["mcol"])
        p.op("scalar", lambda e: e.activation(vcol[:, 1:2], mcol[:], AF.Square),
             reads=["mcol"], writes=["vcol2"])
        p.op("vector", lambda e: e.scalar_tensor_tensor(
            vcol[:, 2:3], vcol[:, 0:1], 1.0 / D, vcol[:, 1:2], ALU.mult, ALU.subtract),
             reads=["vcol", "vcol2"], writes=["vcol3"])
        p.op("vector", lambda e: e.tensor_scalar(vcol[:, 2:3], vcol[:, 2:3], EPS, None, ALU.add),
             reads=["vcol3"], writes=["vcol3"])
        p.op("scalar", lambda e: e.activation(vcol[:, 3:4], vcol[:, 2:3], AF.Ln),
             reads=["vcol3"], writes=["vcol4"])
        p.op("scalar", lambda e: e.activation(vcol[:, 3:4], vcol[:, 3:4], AF.Exp, scale=-0.5),
             reads=["vcol4"], writes=["vcol4"])
        p.op("vector", lambda e: e.tensor_scalar(
            xn32[:], xt[:], mcol[:], vcol[:, 3:4], ALU.subtract, ALU.mult),
             reads=["xt", "mcol", "vcol4"], writes=["xn32"])
        p.op("vector", lambda e: e.tensor_copy(xnb[:], xn32[:]),
             reads=["xn32"], writes=["xnb"])
        p.op("vector", lambda e: e.tensor_sub(xlo_t[:], xn32[:], xnb[:]),
             reads=["xn32", "xnb"], writes=["xlo_t"])
        for kc in range(8):
            p.op("tensor", lambda e, kc=kc: e.transpose(
                ptr[:], xnb[:, kc * 128:(kc + 1) * 128], idb[:]),
                 reads=["xnb", "c_id_bf"], writes=["ptr"])
            p.op("vector", lambda e, kc=kc, t=t: e.tensor_copy(
                xn[:, kc * TC + t * 128: kc * TC + (t + 1) * 128], ptr[:]),
                 reads=["ptr"], writes=["xn"])
        if t >= 1:
            for kc in range(8):
                p.op("tensor", lambda e, kc=kc: e.transpose(
                    ptr[:], xlo_t[:, kc * 128:(kc + 1) * 128], idb[:]),
                     reads=["xlo_t", "c_id_bf"], writes=["ptr"])
                p.op("vector", lambda e, kc=kc: e.tensor_copy(
                    xlo_st[:, kc * 128:(kc + 1) * 128], ptr[:]),
                     reads=["ptr"], writes=["xlo_st"])
            p.dma("sync",
                  xlo_o[:].rearrange("p (a b) -> p a b", b=T)[:, :, (t - 1) * 128: t * 128],
                  xlo_st[:].rearrange("p (a b) -> p a b", b=128),
                  reads=["xlo_st"])

    p.dma("gpsimd", wbig[:], wz_d[:], writes=["wbig"])
    for j in range(4):
        off = 128 + j * 512
        for ic in range(16):
            pm = pml[ic % 2]
            for kc in range(8):
                p.op("tensor", lambda e, ic=ic, kc=kc, off=off, pm=pm: e.matmul(
                    pm[:], wbig[:, kc * INNER + ic * 128: kc * INNER + (ic + 1) * 128],
                    xn[:, kc * TC + off: kc * TC + off + 512],
                    start=(kc == 0), stop=(kc == 7)),
                     reads=["wbig", "xn"], writes=[f"pm{ic % 2}"])
            p.op("scalar", lambda e, ic=ic, j=j, pm=pm: e.activation(
                big[:, ic * T + j * 512: ic * T + (j + 1) * 512], pm[:], AF.Silu),
                 reads=[f"pm{ic % 2}"], writes=["big"])
    p.dma("sync", sz_o[:], big[:], reads=["big"])

    p.dma("gpsimd", wbig[:], wx_d[:], reads=["wbig"], writes=["wbig"])
    for j in range(4):
        off = 128 + j * 512
        for ic in range(16):
            pm = pml[ic % 2]
            for kc in range(8):
                p.op("tensor", lambda e, ic=ic, kc=kc, off=off, pm=pm: e.matmul(
                    pm[:], wbig[:, kc * INNER + ic * 128: kc * INNER + (ic + 1) * 128],
                    xn[:, kc * TC + off: kc * TC + off + 512],
                    start=(kc == 0), stop=(kc == 7)),
                     reads=["wbig", "xn"], writes=[f"pm{ic % 2}"])
            p.op("vector", lambda e, ic=ic, pm=pm: e.tensor_copy(
                xpb[:, ic * 515 + 3: (ic + 1) * 515], pm[:]),
                 reads=[f"pm{ic % 2}"], writes=["xpb" + str(ic)])
        if j == 0:
            for ic in range(16):
                pm = pml[ic % 2]
                for kc in range(8):
                    p.op("tensor", lambda e, ic=ic, kc=kc, pm=pm: e.matmul(
                        pm[:, 0:3], wbig[:, kc * INNER + ic * 128: kc * INNER + (ic + 1) * 128],
                        xn[:, kc * TC + 125: kc * TC + 128],
                        start=(kc == 0), stop=(kc == 7)),
                         reads=["wbig", "xn"], writes=[f"pm{ic % 2}"])
                p.op("vector", lambda e, ic=ic, pm=pm: e.tensor_copy(
                    xpb[:, ic * 515: ic * 515 + 3], pm[:, 0:3]),
                     reads=[f"pm{ic % 2}"], writes=["xpb" + str(ic)])
        for ic in range(16):
            p.op("vector", lambda e, ic=ic: e.tensor_scalar(
                cvt[:], xpb[:, ic * 515: ic * 515 + 512], cw[:, ic * 4: ic * 4 + 1],
                None, ALU.mult),
                 reads=["xpb" + str(ic), "cw"], writes=["cvt"])
            for k in range(1, 4):
                p.op("vector", lambda e, ic=ic, k=k: e.scalar_tensor_tensor(
                    cvt[:], xpb[:, ic * 515 + k: ic * 515 + k + 512],
                    cw[:, ic * 4 + k: ic * 4 + k + 1], cvt[:], ALU.mult, ALU.add),
                     reads=["xpb" + str(ic), "cw", "cvt"],
                     writes=["cvt"])
            p.op("scalar", lambda e, ic=ic, j=j: e.activation(
                big[:, ic * T + j * 512: ic * T + (j + 1) * 512], cvt[:],
                AF.Silu, bias=cb[:, ic: ic + 1]),
                 reads=["cvt", "cb"], writes=["big"])
        if j < 3:
            for ic in range(16):
                p.op("vector", lambda e, ic=ic: e.tensor_copy(
                    xpb[:, ic * 515: ic * 515 + 3],
                    xpb[:, ic * 515 + 512: ic * 515 + 515]),
                     reads=["xpb" + str(ic)], writes=["xpb" + str(ic)])
    p.dma("sync", xc_o[:], big[:], reads=["big"])
    p.dma("sync", xn_o[:], xn[:].rearrange("p (a b) -> p a b", b=TC)[:, :, 128:TC],
          reads=["xn"])

    p.dma("gpsimd", wb2, wb_d[:], writes=["wbig", "wb2"])
    p.dma("gpsimd", wc2, wc_d[:], writes=["wbig", "wc2"])
    p.dma("gpsimd", wdt2[:], wdt_d[:], writes=["wdt2"])
    for j in range(4):
        for h in range(8):
            for (wname, wt, dkey) in (("wb2", wb2, "bstage"),
                                      ("wc2", wc2, "cstage")):
                dst = bstage if dkey == "bstage" else cstage
                for kc in range(16):
                    p.op("tensor", lambda e, h=h, kc=kc, j=j, wt=wt: e.matmul(
                        pb[:], wt[:, kc * 512 + h * 64: kc * 512 + (h + 1) * 64],
                        big[:, kc * T + j * 512: kc * T + (j + 1) * 512],
                        start=(kc == 0), stop=(kc == 15)),
                         reads=[wname, "big"], writes=["pb"])
                p.op("vector", lambda e, h=h, dst=dst: e.tensor_copy(
                    dst[:, h * 512: (h + 1) * 512], pb[:]),
                     reads=["pb"], writes=[dkey])
        p.dma("sync", b_o[:].rearrange("p (a b) -> p a b", b=T)[:, :, j * 512:(j + 1) * 512],
              bstage[:].rearrange("p (a b) -> p a b", b=512), reads=["bstage"])
        p.dma("sync", c_o[:].rearrange("p (a b) -> p a b", b=T)[:, :, j * 512:(j + 1) * 512],
              cstage[:].rearrange("p (a b) -> p a b", b=512), reads=["cstage"])
        for kc in range(16):
            p.op("tensor", lambda e, kc=kc, j=j: e.matmul(
                pdt[:], wdt2[:, kc * 8:(kc + 1) * 8],
                big[:, kc * T + j * 512: kc * T + (j + 1) * 512],
                start=(kc == 0), stop=(kc == 15)),
                 reads=["wdt2", "big"], writes=["pdt"])
        p.op("vector", lambda e: e.tensor_scalar(dtt[:], pdt[:], bdt[:], None, ALU.add),
             reads=["pdt", "bdt"], writes=["dtt"])
        p.op("scalar", lambda e: e.activation(spa[:], dtt[:], AF.Abs),
             reads=["dtt"], writes=["spa"])
        p.op("scalar", lambda e: e.activation(spa[:], spa[:], AF.Exp, scale=-1.0),
             reads=["spa"], writes=["spa"])
        p.op("scalar", lambda e: e.activation(spa[:], spa[:], AF.Ln, bias=1.0),
             reads=["spa"], writes=["spa"])
        p.op("vector", lambda e: e.tensor_scalar_max(dtt[:], dtt[:], 0.0),
             reads=["dtt"], writes=["dtt"])
        p.op("vector", lambda e, j=j: e.tensor_add(
            dts[:, j * 512:(j + 1) * 512], dtt[:], spa[:]),
             reads=["dtt", "spa"], writes=["dts"])
    p.dma("sync", dt_o[:], dts[:], reads=["dts"])

    with nc.Block() as block:
        p.finalize(block)
    nc.finalize()
    return nc


def host_A(inputs):
    x = np.asarray(inputs["L0_attn_input_tensor"])
    wx = prep_lhsT(np.ascontiguousarray(np.asarray(inputs["L0_attn_Wx"]).T))
    wz = prep_lhsT(np.ascontiguousarray(np.asarray(inputs["L0_attn_Wz"]).T))
    wb = prep_lhsT(np.ascontiguousarray(np.asarray(inputs["L0_attn_Wb"]).T))
    wc = prep_lhsT(np.ascontiguousarray(np.asarray(inputs["L0_attn_Wc"]).T))
    wdt = prep_lhsT(np.ascontiguousarray(np.asarray(inputs["L0_attn_Wdt"]).T))
    cwf = np.asarray(inputs["L0_attn_conv_w"])[:, 0, :]
    convw = np.ascontiguousarray(cwf.reshape(16, 128, 4).transpose(1, 0, 2).reshape(128, 64))
    convb = np.ascontiguousarray(np.asarray(inputs["L0_attn_conv_b"]).reshape(16, 128).T)
    bdt = np.asarray(inputs["L0_attn_b_dt"]).reshape(8, 1)
    maps = []
    for c in range(N_CORES):
        b_, k = c // 4, c % 4
        xa = np.zeros((17 * 128, D), np.float32)
        if k > 0:
            xa[125:128] = x[b_, k * T - 3: k * T]
        xa[128:] = x[b_, k * T: (k + 1) * T]
        maps.append({"xa": xa, "wxT": wx, "wzT": wz, "wbT": wb, "wcT": wc,
                     "wdtT": wdt, "convw": convw, "convb": convb, "bdt": bdt})
    return maps


def build_B():
    nc = bass.Bass("TRN2", target_bir_lowering=False)
    es = ExitStack()
    consts = make_consts(nc)
    NCH = 64

    xu_d = [nc.dram_tensor(f"xu{u}", [128, NCH * HDIM], BF16, kind="ExternalInput").ap() for u in range(2)]
    bu_d = [nc.dram_tensor(f"bu{u}", [64, S], BF16, kind="ExternalInput").ap() for u in range(2)]
    cu_d = [nc.dram_tensor(f"cu{u}", [64, S], BF16, kind="ExternalInput").ap() for u in range(2)]
    dtu_d = [nc.dram_tensor(f"dtu{u}", [128, NCH], F32, kind="ExternalInput").ap() for u in range(2)]
    la_d = nc.dram_tensor("la", [2, 1], F32, kind="ExternalInput").ap()
    dp_d = nc.dram_tensor("dp", [2, 1], F32, kind="ExternalInput").ap()
    gnw_d = nc.dram_tensor("gnw", [2, HDIM], F32, kind="ExternalInput").ap()
    gnb_d = nc.dram_tensor("gnb", [2, HDIM], F32, kind="ExternalInput").ap()
    y_o = nc.dram_tensor("y_o", [2, 128, NCH * HDIM], BF16, kind="ExternalOutput").ap()

    sb = nc.alloc_sbuf_tensor
    ps = nc.alloc_psum_tensor
    p = Prog(nc, es)
    idb = load_const(p, nc, consts, "id_bf", [128, 128], BF16)
    onr = load_const(p, nc, consts, "ones_row_f32", [1, 128], F32)
    onc = load_const(p, nc, consts, "ones_col_f32", [128, 1], F32)
    ut = load_const(p, nc, consts, "ut_f32", [128, 128], F32)
    on128 = load_const(p, nc, consts, "ones128_f32", [128, 128], F32)
    mkd = load_const(p, nc, consts, "maskD", [128, 128], F32)

    Xu = sb("Xu", [128, NCH * HDIM], BF16)
    Bu = sb("Bu", [64, S], BF16)
    Cu = sb("Cu", [64, S], BF16)
    dtu = sb("dtu", [128, NCH], F32)
    y_sb = sb("y_sb", [128, NCH * HDIM], BF16)
    dta = sb("dta", [128, NCH], F32)
    small = sb("small", [128, 8], F32)
    acol = sb("acol", [128, 1], F32)
    aend = sb("aend", [128, 1], F32)
    arow = sb("arow", [1, 128], F32)
    a_col = sb("a_col", [128, 1], F32)
    d_col = sb("d_col", [128, 1], F32)
    d_sb = sb("d_sb", [128, 128], F32)
    m_bf = sb("m_bf", [128, 128], BF16)
    ea_bf = sb("ea_bf", [64, 128], BF16)
    cbar = sb("cbar", [64, 128], BF16)
    p_f = sb("p_f", [128, 128], F32)
    p_bf = sb("p_bf", [128, 128], BF16)
    f_col = sb("f_col", [128, 1], F32)
    bt_bf = sb("bt_bf", [128, 64], BF16)
    e64 = sb("e64", [64, 1], F32)
    Ssb = sb("Ssb", [64, HDIM], F32)
    S_bf = sb("S_bf", [64, HDIM], BF16)
    sum_cols = sb("sum_cols", [128, NCH], F32)
    sq_cols = sb("sq_cols", [128, NCH], F32)
    sq_scr = sb("sq_scr", [128, HDIM], F32)
    gnw_row = sb("gnw_row", [1, HDIM], F32)
    gnb_row = sb("gnb_row", [1, HDIM], F32)
    k1b = sb("k1b", [128, HDIM], F32)
    k2b = sb("k2b", [128, HDIM], F32)
    t256 = sb("t256", [128, HDIM], F32)

    pc = ps("pc", [128, 4], F32)
    parow = ps("parow", [1, 128], F32)
    pbca = ps("pbca", [128, 128], F32)
    pg = ps("pg", [128, 128], F32)
    py = ps("py", [128, HDIM], F32)
    pbt = ps("pbt", [128, 64], BF16)
    pds = ps("pds", [64, HDIM], F32)

    for u in range(2):
        p.dma("sync", Xu[:], xu_d[u][:], writes=["Xu"])
        p.dma("sync", Bu[:], bu_d[u][:], writes=["Bu"])
        p.dma("sync", Cu[:], cu_d[u][:], writes=["Cu"])
        p.dma("sync", dtu[:], dtu_d[u][:], writes=["dtu"])
        p.dma("sync", small[0:1, 0:1], la_d[u:u + 1, :], writes=["la"])
        p.dma("sync", small[0:1, 1:2], dp_d[u:u + 1, :], writes=["dp"])
        p.dma("sync", gnw_row[:], gnw_d[u:u + 1, :], writes=["gnw_row"])
        p.dma("sync", gnb_row[:], gnb_d[u:u + 1, :], writes=["gnb_row"])
        p.op("scalar", lambda e: e.activation(small[0:1, 2:3], small[0:1, 0:1], AF.Exp),
             reads=["la"], writes=["a11"])
        p.op("vector", lambda e: e.tensor_scalar(small[0:1, 2:3], small[0:1, 2:3], -1.0, None, ALU.mult),
             reads=["a11"], writes=["a11"])
        p.op("tensor", lambda e: e.matmul(pc[:, 0:1], onr[:], small[0:1, 2:3], start=True, stop=True),
             reads=["a11", "c_ones_row_f32"], writes=["pcb"])
        p.op("vector", lambda e: e.tensor_copy(a_col[:], pc[:, 0:1]), reads=["pcb"], writes=["a_col"])
        p.op("tensor", lambda e: e.matmul(pc[:, 1:2], onr[:], small[0:1, 1:2], start=True, stop=True),
             reads=["dp", "c_ones_row_f32"], writes=["pcb"])
        p.op("vector", lambda e: e.tensor_copy(d_col[:], pc[:, 1:2]), reads=["pcb"], writes=["d_col"])
        p.op("vector", lambda e: e.tensor_scalar(dta[:], dtu[:], a_col[:], None, ALU.mult),
             reads=["dtu", "a_col"], writes=["dta"])
        p.op("vector", lambda e: e.memset(Ssb[:], 0.0), reads=[], writes=["Ssb"])

        for i in range(NCH):
            xsl = Xu[:, i * HDIM:(i + 1) * HDIM]
            bsl = Bu[:, i * 128:(i + 1) * 128]
            csl = Cu[:, i * 128:(i + 1) * 128]
            dcol = dtu[:, i:i + 1]
            dtacol = dta[:, i:i + 1]
            p.op("tensor", lambda e, dtacol=dtacol: e.matmul(pc[:, 2:3], ut[:], dtacol, start=True, stop=True),
                 reads=["dta", "c_ut_f32"], writes=["pcb"])
            p.op("vector", lambda e: e.tensor_copy(acol[:], pc[:, 2:3]), reads=["pcb"], writes=["acol"])
            p.op("tensor", lambda e, dtacol=dtacol: e.matmul(pc[:, 3:4], on128[:], dtacol, start=True, stop=True),
                 reads=["dta", "c_ones128_f32"], writes=["pcb"])
            p.op("vector", lambda e: e.tensor_copy(aend[:], pc[:, 3:4]), reads=["pcb"], writes=["aend"])
            p.op("tensor", lambda e, dtacol=dtacol: e.matmul(parow[:], dtacol, ut[:], start=True, stop=True),
                 reads=["dta", "c_ut_f32"], writes=["parow"])
            p.op("vector", lambda e: e.tensor_copy(arow[:], parow[:]), reads=["parow"], writes=["arow"])
            p.op("tensor", lambda e: e.matmul(pbca[:], onr[:], arow[:], start=True, stop=True),
                 reads=["arow", "c_ones_row_f32"], writes=["pbca"])
            p.op("vector", lambda e: e.scalar_tensor_tensor(
                d_sb[:], pbca[:], acol[:], mkd[:], ALU.subtract, ALU.add),
                 reads=["pbca", "acol", "c_maskD"], writes=["d_sb", "bca_tok"])
            p.op("scalar", lambda e: e.activation(m_bf[:], d_sb[:], AF.Exp),
                 reads=["d_sb"], writes=["m_bf"])
            p.op("scalar", lambda e: e.activation(ea_bf[:], pbca[0:64, :], AF.Exp),
                 reads=["pbca", "bca_tok"], writes=["ea_bf"])
            if i > 0:
                p.op("vector", lambda e, csl=csl: e.tensor_tensor(cbar[:], csl, ea_bf[:], ALU.mult),
                     reads=["Cu", "ea_bf"], writes=["cbar"])
            p.op("tensor", lambda e, bsl=bsl, csl=csl: e.matmul(pg[:], bsl, csl, start=True, stop=True),
                 reads=["Bu", "Cu"], writes=["pg"])
            p.op("vector", lambda e: e.tensor_tensor(p_f[:], pg[:], m_bf[:], ALU.mult),
                 reads=["pg", "m_bf"], writes=["p_f"])
            p.op("vector", lambda e, dcol=dcol: e.tensor_scalar(p_bf[:], p_f[:], dcol, None, ALU.mult),
                 reads=["p_f", "dtu"], writes=["p_bf"])
            p.op("tensor", lambda e, xsl=xsl, i=i: e.matmul(py[:], p_bf[:], xsl,
                 start=True, stop=(i == 0)), reads=["p_bf", "Xu"], writes=["py"])
            if i > 0:
                p.op("tensor", lambda e: e.matmul(py[:], cbar[:], S_bf[:], start=False, stop=True),
                     reads=["cbar", "S_bf"], writes=["py"])
            ysl = y_sb[:, i * HDIM:(i + 1) * HDIM]
            p.op("vector", lambda e, xsl=xsl, ysl=ysl: e.scalar_tensor_tensor(
                ysl, xsl, d_col[:], py[:], ALU.mult, ALU.add),
                 reads=["Xu", "d_col", "py"], writes=["y_sb" + str(i % 4)])
            p.op("vector", lambda e, ysl=ysl, i=i: e.reduce_sum(
                sum_cols[:, i:i + 1], ysl, axis=AX.X),
                 reads=["y_sb" + str(i % 4)], writes=["sum_cols"])
            p.op("scalar", lambda e, ysl=ysl, i=i: e.activation(
                sq_scr[:], ysl, AF.Square, accum_out=sq_cols[:, i:i + 1]),
                 reads=["y_sb" + str(i % 4)], writes=["sq_cols", "sq_scr"])
            p.op("vector", lambda e: e.tensor_sub(f_col[:], aend[:], acol[:]),
                 reads=["aend", "acol"], writes=["f_col"])
            p.op("scalar", lambda e: e.activation(f_col[:], f_col[:], AF.Exp),
                 reads=["f_col"], writes=["f_col"])
            p.op("vector", lambda e, dcol=dcol: e.tensor_scalar(f_col[:], f_col[:], dcol, None, ALU.mult),
                 reads=["f_col", "dtu"], writes=["f_col"])
            p.op("tensor", lambda e, bsl=bsl: e.transpose(pbt[:], bsl, idb[0:64, 0:64]),
                 reads=["Bu", "c_id_bf"], writes=["pbt"])
            p.op("vector", lambda e: e.tensor_scalar(bt_bf[:], pbt[:], f_col[:], None, ALU.mult),
                 reads=["pbt", "f_col"], writes=["bt_bf"])
            p.op("tensor", lambda e, xsl=xsl: e.matmul(pds[:], bt_bf[:], xsl, start=True, stop=True),
                 reads=["bt_bf", "Xu"], writes=["pds"])
            p.op("scalar", lambda e: e.activation(e64[:], aend[0:64, :], AF.Exp),
                 reads=["aend"], writes=["e64"])
            p.op("vector", lambda e: e.scalar_tensor_tensor(
                Ssb[:], Ssb[:], e64[:], pds[:], ALU.mult, ALU.add),
                 reads=["Ssb", "e64", "pds"], writes=["Ssb"])
            p.op("vector", lambda e: e.tensor_copy(S_bf[:], Ssb[:]),
                 reads=["Ssb"], writes=["S_bf"])

        p.op("vector", lambda e: e.reduce_sum(acol[:], sum_cols[:], axis=AX.X),
             reads=["sum_cols"], writes=["acol"])
        p.op("vector", lambda e: e.reduce_sum(aend[:], sq_cols[:], axis=AX.X),
             reads=["sq_cols"], writes=["aend"])
        p.op("tensor", lambda e: e.matmul(pc[0:1, 0:1], acol[:], onc[:], start=True, stop=True),
             reads=["acol", "c_ones_col_f32"], writes=["pcb"])
        p.op("vector", lambda e: e.tensor_scalar(small[0:1, 3:4], pc[0:1, 0:1],
             1.0 / (S * HDIM), None, ALU.mult), reads=["pcb"], writes=["mean"])
        p.op("tensor", lambda e: e.matmul(pc[0:1, 1:2], aend[:], onc[:], start=True, stop=True),
             reads=["aend", "c_ones_col_f32"], writes=["pcb"])
        p.op("scalar", lambda e: e.activation(small[0:1, 4:5], small[0:1, 3:4], AF.Square),
             reads=["mean"], writes=["msq"])
        p.op("vector", lambda e: e.scalar_tensor_tensor(
            small[0:1, 5:6], pc[0:1, 1:2], 1.0 / (S * HDIM), small[0:1, 4:5],
            ALU.mult, ALU.subtract), reads=["pcb", "msq"], writes=["var"])
        p.op("vector", lambda e: e.tensor_scalar(small[0:1, 5:6], small[0:1, 5:6],
             EPS, None, ALU.add), reads=["var"], writes=["var"])
        p.op("scalar", lambda e: e.activation(small[0:1, 5:6], small[0:1, 5:6], AF.Ln),
             reads=["var"], writes=["var"])
        p.op("scalar", lambda e: e.activation(small[0:1, 5:6], small[0:1, 5:6], AF.Exp, scale=-0.5),
             reads=["var"], writes=["var"])
        p.op("vector", lambda e: e.tensor_scalar(small[0:1, 6:7], small[0:1, 3:4], -1.0, None, ALU.mult),
             reads=["mean"], writes=["negmean"])
        p.op("vector", lambda e: e.tensor_scalar(gnw_row[:], gnw_row[:], small[0:1, 5:6], None, ALU.mult),
             reads=["gnw_row", "var"], writes=["gnw_row"])
        p.op("vector", lambda e: e.scalar_tensor_tensor(
            gnb_row[:], gnw_row[:], small[0:1, 6:7], gnb_row[:], ALU.mult, ALU.add),
             reads=["gnw_row", "negmean", "gnb_row"], writes=["gnb_row"])
        p.op("tensor", lambda e: e.matmul(py[:], onr[:], gnw_row[:], start=True, stop=True),
             reads=["gnw_row", "c_ones_row_f32"], writes=["py"])
        p.op("vector", lambda e: e.tensor_copy(k1b[:], py[:]), reads=["py"], writes=["k1b"])
        p.op("tensor", lambda e: e.matmul(py[:], onr[:], gnb_row[:], start=True, stop=True),
             reads=["gnb_row", "c_ones_row_f32"], writes=["py"])
        p.op("vector", lambda e: e.tensor_copy(k2b[:], py[:]), reads=["py"], writes=["k2b"])
        for i in range(NCH):
            ysl = y_sb[:, i * HDIM:(i + 1) * HDIM]
            p.op("vector", lambda e, ysl=ysl: e.tensor_tensor(t256[:], ysl, k1b[:], ALU.mult),
                 reads=["y_sb" + str(i % 4), "k1b"], writes=["t256"])
            p.op("vector", lambda e, ysl=ysl: e.tensor_tensor(ysl, t256[:], k2b[:], ALU.add),
                 reads=["t256", "k2b"], writes=["y_sb" + str(i % 4)])
        p.dma("sync", y_o[u], y_sb[:],
              reads=["y_sb0", "y_sb1", "y_sb2", "y_sb3"])
    with nc.Block() as block:
        p.finalize(block)
    nc.finalize()
    return nc


def host_B(xc_g, b_g, c_g, dt_g, inputs):
    la = np.asarray(inputs["L0_attn_log_a"], np.float32)
    dp = np.asarray(inputs["L0_attn_d_param"], np.float32)
    gnw = np.asarray(inputs["L0_attn_gn_w"], np.float32).reshape(HEADS, HDIM)
    gnb = np.asarray(inputs["L0_attn_gn_b"], np.float32).reshape(HEADS, HDIM)
    maps = []
    for c in range(N_CORES):
        m = {}
        us = [divmod(2 * c + u, HEADS) for u in range(2)]
        for u, (bb, hh) in enumerate(us):
            X = np.ascontiguousarray(xc_g[bb, :, hh * HDIM:(hh + 1) * HDIM]
                                     .reshape(64, 128, HDIM).transpose(1, 0, 2).reshape(128, -1))
            m[f"xu{u}"] = X
            m[f"bu{u}"] = np.ascontiguousarray(b_g[bb, :, hh * 64:(hh + 1) * 64].T)
            m[f"cu{u}"] = np.ascontiguousarray(c_g[bb, :, hh * 64:(hh + 1) * 64].T)
            m[f"dtu{u}"] = np.ascontiguousarray(dt_g[bb, :, hh].reshape(64, 128).T)
        m["la"] = np.stack([la[h] for _, h in us]).reshape(2, 1)
        m["dp"] = np.stack([dp[h] for _, h in us]).reshape(2, 1)
        m["gnw"] = np.stack([gnw[h] for _, h in us])
        m["gnb"] = np.stack([gnb[h] for _, h in us])
        maps.append(m)
    return maps


def build_M():
    nc = bass.Bass("TRN2", target_bir_lowering=False)
    es = ExitStack()
    consts = make_consts(nc)

    xn_d = nc.dram_tensor("xn", [128, 8 * T], BF16, kind="ExternalInput").ap()
    xlo_d = nc.dram_tensor("xlo", [128, 8 * T], BF16, kind="ExternalInput").ap()
    gw_d = nc.dram_tensor("gw", [128, 8 * 8], F32, kind="ExternalInput").ap()
    shg_d = nc.dram_tensor("shg", [128, 8 * INNER], F32, kind="ExternalInput").ap()
    shu_d = nc.dram_tensor("shu", [128, 8 * INNER], F32, kind="ExternalInput").ap()
    shd_d = nc.dram_tensor("shd", [128, 16 * D], F32, kind="ExternalInput").ap()
    rtg_d = nc.dram_tensor("rtg", [8, 128, 8 * INNER], F32, kind="ExternalInput").ap()
    rtu_d = nc.dram_tensor("rtu", [8, 128, 8 * INNER], F32, kind="ExternalInput").ap()
    rtd_d = nc.dram_tensor("rtd", [8, 128, 16 * D], F32, kind="ExternalInput").ap()
    g_o = nc.dram_tensor("g_o", [128, 8 * T], BF16, kind="ExternalOutput").ap()

    sb = nc.alloc_sbuf_tensor
    ps = nc.alloc_psum_tensor
    p = Prog(nc, es)
    idb = load_const(p, nc, consts, "id_bf", [128, 128], BF16)
    sel8 = load_const(p, nc, consts, "sel8", [8, 8 * 128], BF16)

    xnf = sb("xnf", [128, 8 * T], BF16)
    gwl = sb("gwl", [128, 8 * 8], BF16)
    gw32 = sb("gw32", [128, 8 * 8], F32)
    gwlo = sb("gwlo", [128, 8 * 8], BF16)
    wg = sb("wg", [128, 8 * INNER], BF16)
    wu = sb("wu", [128, 8 * INNER], BF16)
    wd = sb("wd", [128, 16 * D], BF16)
    xets = sb("xets", [128, 8 * 512], BF16)
    actb = sb("actb", [128, 16 * 512], BF16)
    tb_bf = sb("tb_bf", [128, 512], BF16)
    y_sb = sb("y_sb2", [128, 8 * T], BF16)
    mt = sb("mt", [128, 8], BF16)
    tv = sb("tv", [128, 2], F32)
    mhm = sb("mhm", [8, T], BF16)
    msk = sb("msk", [128, T], BF16)

    plog = ps("plog", [128, 8], F32)
    ptr8 = ps("ptr8", [8, 128], BF16)
    pg = ps("pg", [128, 512], F32)
    pu = ps("pu", [128, 512], F32)
    pd = ps("pd", [128, 512], F32)
    pmsk = ps("pmsk", [128, 512], F32)

    p.dma("sync", xnf[:], xn_d[:], writes=["xnf"])
    p.dma("sync", wg[:], xlo_d[:], writes=["wg"])
    p.dma("gpsimd", gwl[:], gw_d[:], writes=["gwl"])
    p.dma("sync", gw32[:], gw_d[:], writes=["gw32"])
    p.op("vector", lambda e: e.tensor_sub(gwlo[:], gw32[:], gwl[:]),
         reads=["gw32", "gwl"], writes=["gwlo"])

    for t in range(16):
        for pi, (xb_, xkey, wb_, wkey) in enumerate(
                ((xnf, "xnf", gwl, "gwl"), (wg, "wg", gwl, "gwl"),
                 (xnf, "xnf", gwlo, "gwlo"))):
            for kc in range(8):
                p.op("tensor", lambda e, t=t, kc=kc, xb_=xb_, wb_=wb_, pi=pi: e.matmul(
                    plog[:], xb_[:, kc * T + t * 128: kc * T + (t + 1) * 128],
                    wb_[:, kc * 8:(kc + 1) * 8],
                    start=(pi == 0 and kc == 0), stop=(pi == 2 and kc == 7)),
                     reads=[xkey, wkey], writes=["plog"])
        p.op("vector", lambda e: e.reduce_max(tv[:, 0:1], plog[:], axis=AX.X),
             reads=["plog"], writes=["tv0"])
        p.op("scalar", lambda e: e.activation(tv[:, 1:2], tv[:, 0:1], AF.Sigmoid),
             reads=["tv0"], writes=["tv1"])
        p.op("vector", lambda e: e.tensor_scalar(
            mt[:], plog[:], tv[:, 0:1], tv[:, 1:2], ALU.is_equal, ALU.mult),
             reads=["plog", "tv0", "tv1"], writes=["mt"])
        p.op("tensor", lambda e: e.transpose(ptr8[:], mt[:], idb[:]),
             reads=["mt", "c_id_bf"], writes=["ptr8"])
        p.op("vector", lambda e, t=t: e.tensor_copy(mhm[:, t * 128:(t + 1) * 128], ptr8[:]),
             reads=["ptr8"], writes=["mhm"])

    def ffn_branch(first, rhs_fn, pre_tb=None):
        for tb in range(4):
            if pre_tb is not None:
                pre_tb(tb)
            for ic in range(16):
                for kc in range(8):
                    rap, rkey = rhs_fn(kc, tb)
                    p.op("tensor", lambda e, ic=ic, kc=kc, rap=rap: e.matmul(
                        pg[:], wg[:, kc * INNER + ic * 128: kc * INNER + (ic + 1) * 128],
                        rap, start=(kc == 0), stop=(kc == 7)),
                         reads=["wg", rkey], writes=["pg"])
                for kc in range(8):
                    rap, rkey = rhs_fn(kc, tb)
                    p.op("tensor", lambda e, ic=ic, kc=kc, rap=rap: e.matmul(
                        pu[:], wu[:, kc * INNER + ic * 128: kc * INNER + (ic + 1) * 128],
                        rap, start=(kc == 0), stop=(kc == 7)),
                         reads=["wu", rkey], writes=["pu"])
                p.op("scalar", lambda e: e.activation(tb_bf[:], pg[:], AF.Silu),
                     reads=["pg"], writes=["tb_bf"])
                p.op("vector", lambda e, ic=ic: e.tensor_tensor(
                    actb[:, ic * 512:(ic + 1) * 512], tb_bf[:], pu[:], ALU.mult),
                     reads=["tb_bf", "pu"], writes=["actb" + str(ic % 4)])
            for dmc in range(8):
                for ic in range(16):
                    p.op("tensor", lambda e, dmc=dmc, ic=ic: e.matmul(
                        pd[:], wd[:, ic * D + dmc * 128: ic * D + (dmc + 1) * 128],
                        actb[:, ic * 512:(ic + 1) * 512],
                        start=(ic == 0), stop=(ic == 15)),
                         reads=["wd", "actb0", "actb1", "actb2", "actb3"],
                         writes=["pd"])
                ysl = y_sb[:, dmc * T + tb * 512: dmc * T + (tb + 1) * 512]
                if first:
                    p.op("vector", lambda e, ysl=ysl: e.tensor_copy(ysl, pd[:]),
                         reads=["pd"], writes=[f"y{dmc}_{tb}"])
                else:
                    p.op("vector", lambda e, ysl=ysl: e.scalar_tensor_tensor(
                        ysl, pd[:], 1.0, ysl, ALU.mult, ALU.add),
                         reads=["pd", f"y{dmc}_{tb}"], writes=[f"y{dmc}_{tb}"])

    p.dma("gpsimd", wg[:], shg_d[:], writes=["wg"])
    p.dma("gpsimd", wu[:], shu_d[:], writes=["wu"])
    p.dma("gpsimd", wd[:], shd_d[:], writes=["wd"])
    ffn_branch(True, lambda kc, tb: (xnf[:, kc * T + tb * 512: kc * T + (tb + 1) * 512], "xnf"))

    for ex in range(8):
        p.dma("gpsimd", wg[:], rtg_d[ex], writes=["wg"])
        p.dma("gpsimd", wu[:], rtu_d[ex], writes=["wu"])
        p.dma("gpsimd", wd[:], rtd_d[ex], writes=["wd"])
        for tb in range(4):
            p.op("tensor", lambda e, ex=ex, tb=tb: e.matmul(
                pmsk[:], sel8[:, ex * 128:(ex + 1) * 128],
                mhm[:, tb * 512:(tb + 1) * 512], start=True, stop=True),
                 reads=["c_sel8", "mhm"], writes=["pmsk"])
            p.op("vector", lambda e, tb=tb: e.tensor_copy(
                msk[:, tb * 512:(tb + 1) * 512], pmsk[:]),
                 reads=["pmsk"], writes=["msk"])
        def pre_tb(tb):
            for kc in range(8):
                p.op("vector", lambda e, kc=kc, tb=tb: e.tensor_tensor(
                    xets[:, kc * 512:(kc + 1) * 512],
                    xnf[:, kc * T + tb * 512: kc * T + (tb + 1) * 512],
                    msk[:, tb * 512:(tb + 1) * 512], ALU.mult),
                     reads=["xnf", "msk"], writes=["xets"])
        ffn_branch(False, lambda kc, tb: (xets[:, kc * 512:(kc + 1) * 512], "xets"),
                   pre_tb)

    for dmc in range(8):
        p.op("scalar", lambda e, dmc=dmc: e.activation(
            y_sb[:, dmc * T:(dmc + 1) * T], y_sb[:, dmc * T:(dmc + 1) * T], AF.Silu),
             reads=[f"y{dmc}_0", f"y{dmc}_1", f"y{dmc}_2", f"y{dmc}_3"],
             writes=[f"y{dmc}_0", f"y{dmc}_1", f"y{dmc}_2", f"y{dmc}_3"])
        p.dma("sync", g_o[:].rearrange("p (a b) -> p a b", b=T)[:, dmc, :],
              y_sb[:, dmc * T:(dmc + 1) * T],
              reads=[f"y{dmc}_0", f"y{dmc}_1", f"y{dmc}_2", f"y{dmc}_3"])
    with nc.Block() as block:
        p.finalize(block)
    nc.finalize()
    return nc


def host_M(xn_cores, inputs):
    gw = prep_lhsT(np.asarray(inputs["L0_ffn_gate_w"], np.float32))
    shg = prep_lhsT(np.asarray(inputs["L0_ffn_sh_gate"], np.float32))
    shu = prep_lhsT(np.asarray(inputs["L0_ffn_sh_up"], np.float32))
    shd = prep_lhsT(np.asarray(inputs["L0_ffn_sh_down"], np.float32))
    rtg = np.stack([prep_lhsT(np.asarray(inputs["L0_ffn_rt_gate"][e], np.float32)) for e in range(8)])
    rtu = np.stack([prep_lhsT(np.asarray(inputs["L0_ffn_rt_up"][e], np.float32)) for e in range(8)])
    rtd = np.stack([prep_lhsT(np.asarray(inputs["L0_ffn_rt_down"][e], np.float32)) for e in range(8)])
    return [{"xn": xn_cores[c], "gw": gw, "shg": shg, "shu": shu, "shd": shd,
             "rtg": rtg, "rtu": rtu, "rtd": rtd} for c in range(N_CORES)]


def build_C():
    nc = bass.Bass("TRN2", target_bir_lowering=False)
    es = ExitStack()
    consts = make_consts(nc)

    yn_d = nc.dram_tensor("yn", [128, 16 * T], BF16, kind="ExternalInput").ap()
    sz_d = nc.dram_tensor("sz", [128, 16 * T], BF16, kind="ExternalInput").ap()
    g_d = nc.dram_tensor("g", [128, 8 * T], BF16, kind="ExternalInput").ap()
    x_d = nc.dram_tensor("x", [T, D], F32, kind="ExternalInput").ap()
    wo_d = nc.dram_tensor("woT", [128, 16 * D], F32, kind="ExternalInput").ap()
    x1_o = nc.dram_tensor("x1_o", [T, D], F32, kind="ExternalOutput").ap()

    sb = nc.alloc_sbuf_tensor
    ps = nc.alloc_psum_tensor
    p = Prog(nc, es)
    idb = load_const(p, nc, consts, "id_bf", [128, 128], BF16)

    gf = sb("gf", [128, 8 * T], BF16)
    wo = sb("wo", [128, 16 * D], BF16)
    ytb = sb("ytb", [128, 16 * 512], BF16)
    ztb = sb("ztb", [128, 16 * 512], BF16)
    ob = sb("ob", [128, 16 * 512], BF16)
    sg = sb("sg", [128, 512], BF16)
    xin = sb("xin", [128, 4 * D], F32)
    pd = ps("pd", [128, 512], F32)
    ptr = ps("ptr", [128, 128], BF16)

    p.dma("sync", gf[:], g_d[:], writes=["gf"])
    p.dma("gpsimd", wo[:], wo_d[:], writes=["wo"])
    for tb in range(4):
        p.dma("sync", ytb[:].rearrange("p (a b) -> p a b", b=512),
              yn_d[:].rearrange("p (a b) -> p a b", b=T)[:, :, tb * 512:(tb + 1) * 512],
              writes=["ytb"])
        p.dma("sync", ztb[:].rearrange("p (a b) -> p a b", b=512),
              sz_d[:].rearrange("p (a b) -> p a b", b=T)[:, :, tb * 512:(tb + 1) * 512],
              writes=["ztb"])
        p.dma("sync", xin[:].rearrange("p (a b) -> p a b", b=D),
              x_d[:].rearrange("(s p) d -> p s d", p=128)[:, tb * 4:(tb + 1) * 4, :],
              writes=["xin"])
        for ic in range(16):
            p.op("vector", lambda e, ic=ic: e.tensor_tensor(
                ob[:, ic * 512:(ic + 1) * 512], ytb[:, ic * 512:(ic + 1) * 512],
                ztb[:, ic * 512:(ic + 1) * 512], ALU.mult),
                 reads=["ytb", "ztb"], writes=["ob" + str(ic % 4)])
        for dmc in range(8):
            for ic in range(16):
                p.op("tensor", lambda e, dmc=dmc, ic=ic: e.matmul(
                    pd[:], wo[:, ic * D + dmc * 128: ic * D + (dmc + 1) * 128],
                    ob[:, ic * 512:(ic + 1) * 512],
                    start=(ic == 0), stop=(ic == 15)),
                     reads=["wo", "ob0", "ob1", "ob2", "ob3"], writes=["pd"])
            p.op("vector", lambda e, dmc=dmc, tb=tb: e.tensor_tensor(
                sg[:], pd[:], gf[:, dmc * T + tb * 512: dmc * T + (tb + 1) * 512],
                ALU.mult), reads=["pd", "gf"], writes=["sg"])
            for st in range(4):
                p.op("tensor", lambda e, st=st: e.transpose(
                    ptr[:], sg[:, st * 128:(st + 1) * 128], idb[:]),
                     reads=["sg", "c_id_bf"], writes=["ptr"])
                p.op("vector", lambda e, st=st, dmc=dmc: e.tensor_tensor(
                    xin[:, st * D + dmc * 128: st * D + (dmc + 1) * 128],
                    xin[:, st * D + dmc * 128: st * D + (dmc + 1) * 128],
                    ptr[:], ALU.add),
                     reads=["ptr", "xin"], writes=["xin"])
        p.dma("sync", x1_o[:].rearrange("(s p) d -> p s d", p=128)[:, tb * 4:(tb + 1) * 4, :],
              xin[:].rearrange("p (a b) -> p a b", b=D), reads=["xin"])
    with nc.Block() as block:
        p.finalize(block)
    nc.finalize()
    return nc


def host_C(yn_cores, sz_cores, g_cores, inputs):
    woT = prep_lhsT(np.ascontiguousarray(np.asarray(inputs["L0_attn_Wout"], np.float32).T))
    x = np.asarray(inputs["L0_attn_input_tensor"])
    maps = []
    for c in range(N_CORES):
        b_, k = c // 4, c % 4
        maps.append({"yn": yn_cores[c], "sz": sz_cores[c], "g": g_cores[c],
                     "x": np.ascontiguousarray(x[b_, k * T:(k + 1) * T]), "woT": woT})
    return maps


_prog_cache = {}

def _get(name, builder):
    if name not in _prog_cache:
        _prog_cache[name] = builder()
    return _prog_cache[name]


def _run(nc, maps):
    res = run_bass_kernel_spmd(nc, maps, core_ids=list(range(N_CORES)))
    return res.results


def run_layer0(inputs):
    rA = _run(_get("A", build_A), host_A(inputs))
    xc_g = np.zeros((B, S, INNER), BF)
    b_g = np.zeros((B, S, 512), BF)
    c_g = np.zeros((B, S, 512), BF)
    dt_g = np.zeros((B, S, 8), np.float32)
    xn_cores, xlo_cores, sz_cores = [], [], []
    for c in range(N_CORES):
        b_, k = c // 4, c % 4
        rows = slice(k * T, (k + 1) * T)
        r = rA[c]
        xc_g[b_, rows] = fm_to(np.asarray(r["xc_o"]), T, INNER)
        b_g[b_, rows] = np.asarray(r["b_o"]).reshape(64, 8, T).transpose(2, 1, 0).reshape(T, 512)
        c_g[b_, rows] = np.asarray(r["c_o"]).reshape(64, 8, T).transpose(2, 1, 0).reshape(T, 512)
        dt_g[b_, rows] = np.asarray(r["dt_o"]).T
        xn_cores.append(np.asarray(r["xn_o"]))
        xlo_cores.append(np.asarray(r["xlo_o"]))
        sz_cores.append(np.asarray(r["sz_o"]))

    rB = _run(_get("B", build_B), host_B(xc_g, b_g, c_g, dt_g, inputs))
    yn_g = np.zeros((B, S, INNER), BF)
    for c in range(N_CORES):
        r = np.asarray(rB[c]["y_o"])
        for u in range(2):
            bb, hh = divmod(2 * c + u, HEADS)
            yn_g[bb, :, hh * HDIM:(hh + 1) * HDIM] =                 r[u].reshape(128, 64, HDIM).transpose(1, 0, 2).reshape(S, HDIM)

    mapsM = host_M(xn_cores, inputs)
    for c in range(N_CORES):
        mapsM[c]["xlo"] = xlo_cores[c]
    rM = _run(_get("M", build_M), mapsM)
    g_cores = [np.asarray(rM[c]["g_o"]) for c in range(N_CORES)]

    yn_cores = []
    for c in range(N_CORES):
        b_, k = c // 4, c % 4
        yn_cores.append(fm_from(yn_g[b_, k * T:(k + 1) * T]))
    rC = _run(_get("C", build_C), host_C(yn_cores, sz_cores, g_cores, inputs))
    x1 = np.zeros((B, S, D), np.float32)
    for c in range(N_CORES):
        b_, k = c // 4, c % 4
        x1[b_, k * T:(k + 1) * T] = np.asarray(rC[c]["x1_o"])
    return {"x1": x1}


def build_D():
    nc = bass.Bass("TRN2", target_bir_lowering=False)
    es = ExitStack()
    consts = make_consts(nc)

    x1_d = nc.dram_tensor("x1", [T, D], F32, kind="ExternalInput").ap()
    wq_d = nc.dram_tensor("wq", [128, 8 * D], F32, kind="ExternalInput").ap()
    wk_d = nc.dram_tensor("wk", [128, 8 * D], F32, kind="ExternalInput").ap()
    wv_d = nc.dram_tensor("wv", [128, 8 * D], F32, kind="ExternalInput").ap()
    q_o = nc.dram_tensor("q_o", [128, 8 * T], BF16, kind="ExternalOutput").ap()
    k_o = nc.dram_tensor("k_o", [128, 8 * T], BF16, kind="ExternalOutput").ap()
    v_o = nc.dram_tensor("v_o", [128, 16 * D], BF16, kind="ExternalOutput").ap()
    xn1_o = nc.dram_tensor("xn1_o", [T, D], F32, kind="ExternalOutput").ap()

    sb = nc.alloc_sbuf_tensor
    ps = nc.alloc_psum_tensor
    p = Prog(nc, es)
    idb = load_const(p, nc, consts, "id_bf", [128, 128], BF16)

    xnf = sb("xnf", [128, 8 * T], BF16)
    wql = sb("wql", [128, 8 * D], BF16)
    wkl = sb("wkl", [128, 8 * D], BF16)
    wvl = sb("wvl", [128, 8 * D], BF16)
    qf = sb("qf", [128, 8 * T], BF16)
    kf = sb("kf", [128, 8 * T], BF16)
    vt = sb("vt", [128, 16 * D], BF16)
    xt = sb("xt", [128, D], F32)
    xn32 = sb("xn32", [128, D], F32)
    xnb = sb("xnb", [128, D], BF16)
    mcol = sb("mcol", [128, 1], F32)
    vcol = sb("vcol", [128, 4], F32)
    sq = sb("sq", [128, D], F32)
    pm = ps("pm", [128, 512], F32)
    ptr = ps("ptr", [128, 128], BF16)

    p.dma("gpsimd", wql[:], wq_d[:], writes=["wql"])
    p.dma("gpsimd", wkl[:], wk_d[:], writes=["wkl"])
    p.dma("gpsimd", wvl[:], wv_d[:], writes=["wvl"])

    for t in range(16):
        p.dma("sync", xt[:], x1_d[t * 128:(t + 1) * 128, :], writes=["xt"])
        p.op("vector", lambda e: e.reduce_sum(mcol[:], xt[:], axis=AX.X),
             reads=["xt"], writes=["mcol"])
        p.op("scalar", lambda e: e.activation(sq[:], xt[:], AF.Square),
             reads=["xt"], writes=["sq"])
        p.op("vector", lambda e: e.reduce_sum(vcol[:, 0:1], sq[:], axis=AX.X),
             reads=["sq"], writes=["vcol"])
        p.op("vector", lambda e: e.tensor_scalar(mcol[:], mcol[:], 1.0 / D, None, ALU.mult),
             reads=["mcol"], writes=["mcol"])
        p.op("scalar", lambda e: e.activation(vcol[:, 1:2], mcol[:], AF.Square),
             reads=["mcol"], writes=["vcol2"])
        p.op("vector", lambda e: e.scalar_tensor_tensor(
            vcol[:, 2:3], vcol[:, 0:1], 1.0 / D, vcol[:, 1:2], ALU.mult, ALU.subtract),
             reads=["vcol", "vcol2"], writes=["vcol3"])
        p.op("vector", lambda e: e.tensor_scalar(vcol[:, 2:3], vcol[:, 2:3], EPS, None, ALU.add),
             reads=["vcol3"], writes=["vcol3"])
        p.op("scalar", lambda e: e.activation(vcol[:, 3:4], vcol[:, 2:3], AF.Ln),
             reads=["vcol3"], writes=["vcol4"])
        p.op("scalar", lambda e: e.activation(vcol[:, 3:4], vcol[:, 3:4], AF.Exp, scale=-0.5),
             reads=["vcol4"], writes=["vcol4"])
        p.op("vector", lambda e: e.tensor_scalar(
            xn32[:], xt[:], mcol[:], vcol[:, 3:4], ALU.subtract, ALU.mult),
             reads=["xt", "mcol", "vcol4"], writes=["xn32"])
        p.dma("sync", xn1_o[t * 128:(t + 1) * 128, :], xn32[:], reads=["xn32"])
        p.op("vector", lambda e: e.tensor_copy(xnb[:], xn32[:]),
             reads=["xn32"], writes=["xnb"])
        for kc in range(8):
            p.op("tensor", lambda e, kc=kc: e.transpose(
                ptr[:], xnb[:, kc * 128:(kc + 1) * 128], idb[:]),
                 reads=["xnb", "c_id_bf"], writes=["ptr"])
            p.op("vector", lambda e, kc=kc, t=t: e.tensor_copy(
                xnf[:, kc * T + t * 128: kc * T + (t + 1) * 128], ptr[:]),
                 reads=["ptr"], writes=["xnf"])
    for (wl, wkey, dst, dkey, scale) in ((wql, "wql", qf, "qf", 0.125),
                                         (wkl, "wkl", kf, "kf", None)):
        for ic in range(8):
            for tb in range(4):
                for kc in range(8):
                    p.op("tensor", lambda e, ic=ic, tb=tb, kc=kc, wl=wl: e.matmul(
                        pm[:], wl[:, kc * D + ic * 128: kc * D + (ic + 1) * 128],
                        xnf[:, kc * T + tb * 512: kc * T + (tb + 1) * 512],
                        start=(kc == 0), stop=(kc == 7)),
                         reads=[wkey, "xnf"], writes=["pm"])
                dsl = dst[:, ic * T + tb * 512: ic * T + (tb + 1) * 512]
                if scale is None:
                    p.op("vector", lambda e, dsl=dsl: e.tensor_copy(dsl, pm[:]),
                         reads=["pm"], writes=[dkey])
                else:
                    p.op("vector", lambda e, dsl=dsl, scale=scale: e.tensor_scalar(
                        dsl, pm[:], scale, None, ALU.mult),
                         reads=["pm"], writes=[dkey])
    for t in range(16):
        for vb in range(2):
            for kc in range(8):
                p.op("tensor", lambda e, t=t, vb=vb, kc=kc: e.matmul(
                    pm[:], xnf[:, kc * T + t * 128: kc * T + (t + 1) * 128],
                    wvl[:, kc * D + vb * 512: kc * D + (vb + 1) * 512],
                    start=(kc == 0), stop=(kc == 7)),
                     reads=["xnf", "wvl"], writes=["pm"])
            p.op("vector", lambda e, t=t, vb=vb: e.tensor_copy(
                vt[:, t * D + vb * 512: t * D + (vb + 1) * 512], pm[:]),
                 reads=["pm"], writes=["vt"])
    p.dma("sync", q_o[:], qf[:], reads=["qf"])
    p.dma("sync", k_o[:], kf[:], reads=["kf"])
    p.dma("sync", v_o[:], vt[:], reads=["vt"])
    with nc.Block() as block:
        p.finalize(block)
    nc.finalize()
    return nc


def host_D(x1, inputs):
    wq = prep_lhsT(np.asarray(inputs["L1_attn_Wq"], np.float32))
    wk = prep_lhsT(np.asarray(inputs["L1_attn_Wk"], np.float32))
    wv = prep_lhsT(np.asarray(inputs["L1_attn_Wv"], np.float32))
    maps = []
    for c in range(N_CORES):
        b_, k = c // 4, c % 4
        maps.append({"x1": np.ascontiguousarray(x1[b_, k * T:(k + 1) * T]),
                     "wq": wq, "wk": wk, "wv": wv})
    return maps


_MIDS = (0.25, 0.75, 1.25, 1.75, 2.5, 3.5, 5.0)
_DELT = (0.5, 0.5, 0.5, 0.5, 1.0, 1.0, 2.0)

def emit_fp4(p, src, dst, bufs, P, n, key):
    nb = n // 16
    sc, th, ax, sg, cb1, cb2, accv, accg = (bufs[x] for x in
        ("sc", "th", "ax", "sg", "cb1", "cb2", "accv", "accg"))
    k_src, k_dst = key + "_src", key + "_dst"
    ksc, kax, ksg = key + "_sc", key + "_ax", key + "_sg"
    kc1, kc2, kav, kag = key + "_c1", key + "_c2", key + "_av", key + "_ag"
    def v3(ap):
        return ap.rearrange("p (a b) -> p a b", b=16)
    p.op("vector", lambda e: e.reduce_max(sc[0:P, 0:nb], v3(src), axis=AX.X,
         apply_absolute_value=True), reads=[k_src], writes=[ksc])
    p.op("vector", lambda e: e.tensor_scalar(sc[0:P, 0:nb], sc[0:P, 0:nb],
         1.0 / 6.0, 1e-12, ALU.mult, ALU.max), reads=[ksc], writes=[ksc])
    p.op("scalar", lambda e: e.activation(ax[0:P, 0:n], src, AF.Abs),
         reads=[k_src], writes=[kax])
    p.op("scalar", lambda e: e.activation(sg[0:P, 0:n], src, AF.Sign),
         reads=[k_src], writes=[ksg])
    for i in range(7):
        p.op("vector", lambda e, i=i: e.tensor_scalar(
            th[0:P, i * nb:(i + 1) * nb], sc[0:P, 0:nb], _MIDS[i], None, ALU.mult),
             reads=[ksc], writes=[key + "_th" + str(i)])
    def cmp(i, dstbuf, dkey):
        p.op("vector", lambda e, i=i, dstbuf=dstbuf: e.tensor_tensor(
            v3(dstbuf[0:P, 0:n]), v3(ax[0:P, 0:n]),
            th[0:P, i * nb:(i + 1) * nb].unsqueeze(2).broadcast_to([P, nb, 16]),
            ALU.is_gt), reads=[kax, key + "_th" + str(i)], writes=[dkey])
    for j in range(3):
        cmp(2 * j, cb1, kc1)
        cmp(2 * j + 1, cb2, kc2)
        acc, kacc = (accv, kav) if j == 2 else (accg, kag)
        if j == 0:
            p.op("gpsimd", lambda e: e.tensor_add(accg[0:P, 0:n], cb1[0:P, 0:n],
                 cb2[0:P, 0:n]), reads=[kc1, kc2], writes=[kag])
        elif j == 1:
            p.op("gpsimd", lambda e: e.tensor_add(accg[0:P, 0:n], accg[0:P, 0:n],
                 cb1[0:P, 0:n]), reads=[kag, kc1], writes=[kag])
            p.op("gpsimd", lambda e: e.tensor_add(accg[0:P, 0:n], accg[0:P, 0:n],
                 cb2[0:P, 0:n]), reads=[kag, kc2], writes=[kag])
        else:
            p.op("gpsimd", lambda e: e.tensor_scalar_mul(accg[0:P, 0:n],
                 accg[0:P, 0:n], 0.5), reads=[kag], writes=[kag])
            p.op("vector", lambda e: e.tensor_add(accv[0:P, 0:n], cb1[0:P, 0:n],
                 cb2[0:P, 0:n]), reads=[kc1, kc2], writes=[kav])
    cmp(6, cb1, kc1)
    p.op("vector", lambda e: e.scalar_tensor_tensor(accv[0:P, 0:n], cb1[0:P, 0:n],
         2.0, accv[0:P, 0:n], ALU.mult, ALU.add), reads=[kc1, kav], writes=[kav])
    p.op("vector", lambda e: e.tensor_add(accv[0:P, 0:n], accv[0:P, 0:n],
         accg[0:P, 0:n]), reads=[kav, kag], writes=[kav])
    p.op("gpsimd", lambda e: e.tensor_mul(accv[0:P, 0:n], accv[0:P, 0:n],
         sg[0:P, 0:n]), reads=[kav, ksg], writes=[kav])
    p.op("vector", lambda e: e.tensor_tensor(
        v3(dst), v3(accv[0:P, 0:n]),
        sc[0:P, 0:nb].unsqueeze(2).broadcast_to([P, nb, 16]), ALU.mult),
         reads=[kav, ksc], writes=[k_dst])


def mk_fp4_bufs(nc, n, tag=""):
    sb = nc.alloc_sbuf_tensor
    return {"sc": sb("q_sc" + tag, [128, n // 16], F32),
            "th": sb("q_th" + tag, [128, 7 * (n // 16)], F32),
            "ax": sb("q_ax" + tag, [128, n], F32),
            "sg": sb("q_sg" + tag, [128, n], BF16),
            "cb1": sb("q_c1" + tag, [128, n], BF16),
            "cb2": sb("q_c2" + tag, [128, n], BF16),
            "accv": sb("q_av" + tag, [128, n], BF16),
            "accg": sb("q_ag" + tag, [128, n], BF16)}


def build_W():
    nc = bass.Bass("TRN2", target_bir_lowering=False)
    es = ExitStack()
    wg_d = nc.dram_tensor("wgs", [352, 1024], F32, kind="ExternalInput").ap()
    wu_d = nc.dram_tensor("wus", [352, 1024], F32, kind="ExternalInput").ap()
    wd_d = nc.dram_tensor("wds", [128, FI], F32, kind="ExternalInput").ap()
    wgq_o = nc.dram_tensor("wgq_o", [352, 1024], BF16, kind="ExternalOutput").ap()
    wuq_o = nc.dram_tensor("wuq_o", [352, 1024], BF16, kind="ExternalOutput").ap()
    wdq_o = nc.dram_tensor("wdq_o", [128, FI], BF16, kind="ExternalOutput").ap()
    sb = nc.alloc_sbuf_tensor
    p = Prog(nc, es)
    bufs = mk_fp4_bufs(nc, FI)
    srcb = sb("srcb", [128, FI], F32)
    dstb = sb("dstb", [128, FI], BF16)
    ji = 0
    for (src_d, dst_d, rows, n) in ((wg_d, wgq_o, 352, 1024), (wu_d, wuq_o, 352, 1024),
                                    (wd_d, wdq_o, 128, FI)):
        r0 = 0
        while r0 < rows:
            P = min(128, rows - r0)
            p.dma("sync", srcb[0:P, 0:n], src_d[r0:r0 + P, :], writes=["q0_src"])
            emit_fp4(p, srcb[0:P, 0:n], dstb[0:P, 0:n], bufs, P, n, "q0")
            p.dma("sync", dst_d[r0:r0 + P, :], dstb[0:P, 0:n], reads=["q0_dst"])
            r0 += P
            ji += 1
    with nc.Block() as block:
        p.finalize(block)
    nc.finalize()
    return nc


def host_W(inputs):
    wg = np.asarray(inputs["L1_ffn_w_gate"], np.float32)
    wu = np.asarray(inputs["L1_ffn_w_up"], np.float32)
    wd = np.asarray(inputs["L1_ffn_w_down"], np.float32)
    maps = []
    for c in range(N_CORES):
        maps.append({"wgs": np.ascontiguousarray(wg[c * 352:(c + 1) * 352]),
                     "wus": np.ascontiguousarray(wu[c * 352:(c + 1) * 352]),
                     "wds": np.ascontiguousarray(wd[c * 128:(c + 1) * 128])})
    return maps


def build_E():
    nc = bass.Bass("TRN2", target_bir_lowering=False)
    es = ExitStack()
    consts = make_consts(nc)
    NKT = 64

    qu_d = [nc.dram_tensor(f"qu{u}", [64, S], BF16, kind="ExternalInput").ap() for u in range(4)]
    ku_d = [nc.dram_tensor(f"ku{u}", [64, S], BF16, kind="ExternalInput").ap() for u in range(4)]
    vu_d = [nc.dram_tensor(f"vu{u}", [128, NKT * AHD], BF16, kind="ExternalInput").ap() for u in range(4)]
    o_o = nc.dram_tensor("o_o", [4, 128, NKT * AHD], BF16, kind="ExternalOutput").ap()

    sb = nc.alloc_sbuf_tensor
    ps = nc.alloc_psum_tensor
    p = Prog(nc, es)
    bm = load_const(p, nc, consts, "bigmask", [128, 896], F32)

    qu = sb("qu", [64, S], BF16)
    ku = sb("ku", [64, S], BF16)
    vsb = sb("vsb", [128, NKT * 65], BF16)
    pbf = sb("pbf", [128, 512], BF16)
    ob = sb("ob", [128, NKT * AHD], BF16)
    rc = sb("rc", [128, 1], F32)
    psl = [ps(f"ps{i}", [128, 512], F32) for i in range(2)]
    pacc = [ps(f"pa{i}", [128, 65], F32) for i in range(4)]

    for u in range(4):
        p.dma("sync", qu[:], qu_d[u][:], writes=["qu"])
        p.dma("sync", ku[:], ku_d[u][:], writes=["ku"])
        p.op("vector", lambda e: e.memset(vsb[:], 1.0), reads=[], writes=["vsb"])
        p.dma("sync", vsb[:].rearrange("p (a b) -> p a b", b=65)[:, :, 0:64],
              vu_d[u][:].rearrange("p (a b) -> p a b", b=AHD), writes=["vsb"])
        for qb in range(16):
            nkt = 4 * qb + 4
            for kt in range(nkt):
                sp = psl[kt % 2]
                skey = f"ps{kt % 2}"
                p.op("tensor", lambda e, kt=kt, qb=qb, sp=sp: e.matmul(
                    sp[:], ku[:, kt * 128:(kt + 1) * 128],
                    qu[:, qb * 512:(qb + 1) * 512], start=True, stop=True),
                     reads=["ku", "qu"], writes=[skey])
                a = kt - 4 * qb
                if a >= 0:
                    off = 384 - 128 * a
                    p.op("vector", lambda e, sp=sp, off=off: e.tensor_tensor(
                        sp[:], sp[:], bm[:, off:off + 512], ALU.add),
                         reads=[skey, "c_bigmask"], writes=[skey])
                p.op("scalar", lambda e, sp=sp: e.activation(pbf[:], sp[:], AF.Exp),
                     reads=[skey], writes=["pbf"])
                for qs in range(4):
                    p.op("tensor", lambda e, qs=qs, kt=kt, nkt=nkt: e.matmul(
                        pacc[qs][:], pbf[:, qs * 128:(qs + 1) * 128],
                        vsb[:, kt * 65:(kt + 1) * 65],
                        start=(kt == 0), stop=(kt == nkt - 1)),
                         reads=["pbf", "vsb"], writes=[f"pa{qs}"])
            for qs in range(4):
                p.op("vector", lambda e, qs=qs: e.reciprocal(rc[:], pacc[qs][:, 64:65]),
                     reads=[f"pa{qs}"], writes=["rc"])
                ti = qb * 4 + qs
                p.op("vector", lambda e, qs=qs, ti=ti: e.tensor_scalar(
                    ob[:, ti * AHD:(ti + 1) * AHD], pacc[qs][:, 0:64], rc[:],
                    None, ALU.mult), reads=[f"pa{qs}", "rc"], writes=["ob"])
        p.dma("sync", o_o[u], ob[:], reads=["ob"])
    with nc.Block() as block:
        p.finalize(block)
    nc.finalize()
    return nc


def host_E(q_cores, k_cores, v_cores):
    qg = np.zeros((B, 2 * S, 512), BF)
    qf = np.zeros((B, D, S), BF)
    kf = np.zeros((B, D, S), BF)
    vt = np.zeros((B, S, D), BF)
    for c in range(N_CORES):
        b_, k = c // 4, c % 4
        rows = slice(k * T, (k + 1) * T)
        qf[b_, :, rows] = q_cores[c].reshape(128, 8, T).transpose(1, 0, 2).reshape(D, T)
        kf[b_, :, rows] = k_cores[c].reshape(128, 8, T).transpose(1, 0, 2).reshape(D, T)
        vt[b_, rows, :] = v_cores[c].reshape(128, 16, D).transpose(1, 0, 2).reshape(T, D)
    maps = []
    for c in range(N_CORES):
        m = {}
        for u in range(4):
            bb, hh = divmod(4 * c + u, AH)
            m[f"qu{u}"] = np.ascontiguousarray(qf[bb, hh * AHD:(hh + 1) * AHD, :])
            m[f"ku{u}"] = np.ascontiguousarray(kf[bb, hh * AHD:(hh + 1) * AHD, :])
            m[f"vu{u}"] = np.ascontiguousarray(
                vt[bb, :, hh * AHD:(hh + 1) * AHD].reshape(64, 128, AHD)
                .transpose(1, 0, 2).reshape(128, -1))
        maps.append(m)
    return maps


def build_F1a():
    nc = bass.Bass("TRN2", target_bir_lowering=False)
    es = ExitStack()
    consts = make_consts(nc)
    IB = (512, 512, 512, 512, 512, 256)

    xn1_d = nc.dram_tensor("xn1", [T, D], F32, kind="ExternalInput").ap()
    wg_d = nc.dram_tensor("wgqT", [128, 8 * FI], BF16, kind="ExternalInput").ap()
    wu_d = nc.dram_tensor("wuqT", [128, 8 * FI], BF16, kind="ExternalInput").ap()
    hq_o = nc.dram_tensor("hq_o", [128, 16 * FI], BF16, kind="ExternalOutput").ap()

    sb = nc.alloc_sbuf_tensor
    ps = nc.alloc_psum_tensor
    p = Prog(nc, es)
    idb = load_const(p, nc, consts, "id_bf", [128, 128], BF16)

    wgq = sb("wgq", [128, 8 * FI], BF16)
    wuq = sb("wuq", [128, 8 * FI], BF16)
    bufs = mk_fp4_bufs(nc, FI)
    xts = sb("xts", [128, D], F32)
    xq = sb("xq", [128, D], BF16)
    xqf = sb("xqf", [128, D], BF16)
    hf = sb("hf", [128, FI], F32)
    hq = sb("hq", [128, FI], BF16)
    sil = sb("sil", [128, 512], BF16)
    pg = ps("pg", [128, 512], F32)
    pu = ps("pu", [128, 512], F32)
    ptr = ps("ptr", [128, 128], BF16)

    p.dma("sync", wgq[:], wg_d[:], writes=["wgq"])
    p.dma("sync", wuq[:], wu_d[:], writes=["wuq"])
    for t in range(16):
        p.dma("sync", xts[:], xn1_d[t * 128:(t + 1) * 128, :], writes=["q0_src"])
        emit_fp4(p, xts[:], xq[:], bufs, 128, D, "q0")
        for kc in range(8):
            p.op("tensor", lambda e, kc=kc: e.transpose(
                ptr[:], xq[:, kc * 128:(kc + 1) * 128], idb[:]),
                 reads=["q0_dst", "c_id_bf"], writes=["ptr"])
            p.op("vector", lambda e, kc=kc: e.tensor_copy(
                xqf[:, kc * 128:(kc + 1) * 128], ptr[:]),
                 reads=["ptr"], writes=["xqf"])
        off = 0
        for ib, w in enumerate(IB):
            for kc in range(8):
                p.op("tensor", lambda e, kc=kc, off=off, w=w: e.matmul(
                    pg[:, 0:w], xqf[:, kc * 128:(kc + 1) * 128],
                    wgq[:, kc * FI + off: kc * FI + off + w],
                    start=(kc == 0), stop=(kc == 7)),
                     reads=["xqf", "wgq"], writes=["pg"])
            for kc in range(8):
                p.op("tensor", lambda e, kc=kc, off=off, w=w: e.matmul(
                    pu[:, 0:w], xqf[:, kc * 128:(kc + 1) * 128],
                    wuq[:, kc * FI + off: kc * FI + off + w],
                    start=(kc == 0), stop=(kc == 7)),
                     reads=["xqf", "wuq"], writes=["pu"])
            p.op("scalar", lambda e, w=w: e.activation(sil[:, 0:w], pg[:, 0:w], AF.Silu),
                 reads=["pg"], writes=["sil"])
            p.op("vector", lambda e, off=off, w=w: e.tensor_tensor(
                hf[:, off:off + w], sil[:, 0:w], pu[:, 0:w], ALU.mult),
                 reads=["sil", "pu"], writes=["q1_src"])
            off += w
        emit_fp4(p, hf[:], hq[:], bufs, 128, FI, "q1")
        p.dma("sync", hq_o[:, t * FI:(t + 1) * FI], hq[:], reads=["q1_dst"])
    with nc.Block() as block:
        p.finalize(block)
    nc.finalize()
    return nc


def build_F1b():
    nc = bass.Bass("TRN2", target_bir_lowering=False)
    es = ExitStack()
    NIC = 22
    hq_d = nc.dram_tensor("hqf", [128, NIC * T], BF16, kind="ExternalInput").ap()
    wd_d = nc.dram_tensor("wdqT", [128, NIC * D], BF16, kind="ExternalInput").ap()
    g1_o = nc.dram_tensor("g1_o", [128, 8 * T], BF16, kind="ExternalOutput").ap()
    sb = nc.alloc_sbuf_tensor
    ps = nc.alloc_psum_tensor
    p = Prog(nc, es)
    wdq = sb("wdq", [128, NIC * D], BF16)
    htb = sb("htb", [128, NIC * 512], BF16)
    g1 = sb("g1", [128, 8 * T], BF16)
    pd = ps("pd2", [128, 512], F32)
    p.dma("sync", wdq[:], wd_d[:], writes=["wdq"])
    for tb in range(4):
        p.dma("sync", htb[:].rearrange("p (a b) -> p a b", b=512),
              hq_d[:].rearrange("p (a b) -> p a b", b=T)[:, :, tb * 512:(tb + 1) * 512],
              writes=["htb"])
        for dmc in range(8):
            for ic in range(NIC):
                p.op("tensor", lambda e, dmc=dmc, ic=ic: e.matmul(
                    pd[:], wdq[:, ic * D + dmc * 128: ic * D + (dmc + 1) * 128],
                    htb[:, ic * 512:(ic + 1) * 512],
                    start=(ic == 0), stop=(ic == NIC - 1)),
                     reads=["wdq", "htb"], writes=["pd2"])
            p.op("scalar", lambda e, dmc=dmc, tb=tb: e.activation(
                g1[:, dmc * T + tb * 512: dmc * T + (tb + 1) * 512], pd[:], AF.Silu),
                 reads=["pd2"], writes=["g1"])
    p.dma("sync", g1_o[:], g1[:], reads=["g1"])
    with nc.Block() as block:
        p.finalize(block)
    nc.finalize()
    return nc


def build_F2():
    nc = bass.Bass("TRN2", target_bir_lowering=False)
    es = ExitStack()
    consts = make_consts(nc)

    o_d = nc.dram_tensor("of", [128, 8 * T], BF16, kind="ExternalInput").ap()
    g1_d = nc.dram_tensor("g1f", [128, 8 * T], BF16, kind="ExternalInput").ap()
    x1_d = nc.dram_tensor("x1", [T, D], F32, kind="ExternalInput").ap()
    wo_d = nc.dram_tensor("wo", [128, 8 * D], F32, kind="ExternalInput").ap()
    bo_d = nc.dram_tensor("bo", [128, 8], F32, kind="ExternalInput").ap()
    x2_o = nc.dram_tensor("x2_o", [T, D], F32, kind="ExternalOutput").ap()

    sb = nc.alloc_sbuf_tensor
    ps = nc.alloc_psum_tensor
    p = Prog(nc, es)
    idb = load_const(p, nc, consts, "id_bf", [128, 128], BF16)

    of = sb("of_s", [128, 8 * T], BF16)
    g1f = sb("g1f_s", [128, 8 * T], BF16)
    wol = sb("wol", [128, 8 * D], BF16)
    bos = sb("bos", [128, 8], F32)
    sg = sb("sg", [128, 512], BF16)
    xin = sb("xin", [128, 4 * D], F32)
    pd = ps("pd", [128, 512], F32)
    ptr = ps("ptr", [128, 128], BF16)

    p.dma("sync", of[:], o_d[:], writes=["of"])
    p.dma("sync", g1f[:], g1_d[:], writes=["g1f"])
    p.dma("gpsimd", wol[:], wo_d[:], writes=["wol"])
    p.dma("sync", bos[:], bo_d[:], writes=["bos"])
    for tb in range(4):
        p.dma("sync", xin[:].rearrange("p (a b) -> p a b", b=D),
              x1_d[:].rearrange("(s p) d -> p s d", p=128)[:, tb * 4:(tb + 1) * 4, :],
              writes=["xin"])
        for dmc in range(8):
            for ic in range(8):
                p.op("tensor", lambda e, dmc=dmc, ic=ic, tb=tb: e.matmul(
                    pd[:], wol[:, ic * D + dmc * 128: ic * D + (dmc + 1) * 128],
                    of[:, ic * T + tb * 512: ic * T + (tb + 1) * 512],
                    start=(ic == 0), stop=(ic == 7)),
                     reads=["wol", "of"], writes=["pd"])
            p.op("vector", lambda e, dmc=dmc, tb=tb: e.scalar_tensor_tensor(
                sg[:], pd[:], bos[:, dmc:dmc + 1],
                g1f[:, dmc * T + tb * 512: dmc * T + (tb + 1) * 512],
                ALU.add, ALU.mult), reads=["pd", "bos", "g1f"], writes=["sg"])
            for st in range(4):
                p.op("tensor", lambda e, st=st: e.transpose(
                    ptr[:], sg[:, st * 128:(st + 1) * 128], idb[:]),
                     reads=["sg", "c_id_bf"], writes=["ptr"])
                p.op("vector", lambda e, st=st, dmc=dmc: e.tensor_tensor(
                    xin[:, st * D + dmc * 128: st * D + (dmc + 1) * 128],
                    xin[:, st * D + dmc * 128: st * D + (dmc + 1) * 128],
                    ptr[:], ALU.add),
                     reads=["ptr", "xin"], writes=["xin"])
        p.dma("sync", x2_o[:].rearrange("(s p) d -> p s d", p=128)[:, tb * 4:(tb + 1) * 4, :],
              xin[:].rearrange("p (a b) -> p a b", b=D), reads=["xin"])
    with nc.Block() as block:
        p.finalize(block)
    nc.finalize()
    return nc


def run_layer1(x1, inputs):
    rD = _run(_get("D", build_D), host_D(x1, inputs))
    q_cores = [np.asarray(rD[c]["q_o"]) for c in range(N_CORES)]
    k_cores = [np.asarray(rD[c]["k_o"]) for c in range(N_CORES)]
    v_cores = [np.asarray(rD[c]["v_o"]) for c in range(N_CORES)]
    xn1_cores = [np.asarray(rD[c]["xn1_o"]) for c in range(N_CORES)]

    rW = _run(_get("W", build_W), host_W(inputs))
    wgq = np.concatenate([np.asarray(rW[c]["wgq_o"]) for c in range(N_CORES)], 0)
    wuq = np.concatenate([np.asarray(rW[c]["wuq_o"]) for c in range(N_CORES)], 0)
    wdq = np.concatenate([np.asarray(rW[c]["wdq_o"]) for c in range(N_CORES)], 0)

    rE = _run(_get("E", build_E), host_E(q_cores, k_cores, v_cores))
    og = np.zeros((B, S, D), BF)
    for c in range(N_CORES):
        r = np.asarray(rE[c]["o_o"])
        for u in range(4):
            bb, hh = divmod(4 * c + u, AH)
            og[bb, :, hh * AHD:(hh + 1) * AHD] =                 r[u].reshape(128, 64, AHD).transpose(1, 0, 2).reshape(S, AHD)

    wgqT = prep_lhsT(np.ascontiguousarray(wgq.T)).astype(BF)
    wuqT = prep_lhsT(np.ascontiguousarray(wuq.T)).astype(BF)
    wdqT = prep_lhsT(np.ascontiguousarray(wdq.T)).astype(BF)
    mapsF1a = [{"xn1": xn1_cores[c], "wgqT": wgqT, "wuqT": wuqT}
               for c in range(N_CORES)]
    rF1a = _run(_get("F1a", build_F1a), mapsF1a)
    mapsF1b = []
    for c in range(N_CORES):
        hq = np.asarray(rF1a[c]["hq_o"]).reshape(128, 16, FI).transpose(1, 0, 2).reshape(T, FI)
        mapsF1b.append({"hqf": fm_from(hq), "wdqT": wdqT})
    rF1b = _run(_get("F1b", build_F1b), mapsF1b)

    wo = prep_lhsT(np.asarray(inputs["L1_attn_Wo"], np.float32))
    bo = np.ascontiguousarray(np.asarray(inputs["L1_attn_bo"], np.float32).reshape(8, 128).T)
    mapsF2 = []
    for c in range(N_CORES):
        b_, k = c // 4, c % 4
        mapsF2.append({"of": fm_from(og[b_, k * T:(k + 1) * T]),
                       "g1f": np.asarray(rF1b[c]["g1_o"]),
                       "x1": np.ascontiguousarray(x1[b_, k * T:(k + 1) * T]),
                       "wo": wo, "bo": bo})
    rF2 = _run(_get("F2", build_F2), mapsF2)
    x2 = np.zeros((B, S, D), np.float32)
    for c in range(N_CORES):
        b_, k = c // 4, c % 4
        x2[b_, k * T:(k + 1) * T] = np.asarray(rF2[c]["x2_o"])
    return x2


def kernel(**inputs):
    st = run_layer0(inputs)
    x2 = run_layer1(st["x1"], inputs)
    return x2


# version 02
# speedup vs baseline: 1.1763x; 1.1763x over previous
import numpy as np
import ml_dtypes
from contextlib import ExitStack
import concourse.bass as bass
import concourse.mybir as mybir
from concourse.bass_utils import run_bass_kernel_spmd

F32 = mybir.dt.float32
BF16 = mybir.dt.bfloat16
I32 = mybir.dt.int32
AF = mybir.ActivationFunctionType
ALU = mybir.AluOpType
AX = mybir.AxisListType
BF = ml_dtypes.bfloat16

N_CORES = 8
B, S, D = 2, 8192, 1024
T = S * B // N_CORES
INNER = 2048
HEADS, STATE, HDIM = 8, 64, 256
AH, AHD = 16, 64
FI = 2816
EPS = 1e-5

ENGINES = ("tensor", "vector", "scalar", "gpsimd", "sync")
DMA_POOL = 12


class Prog:
    def __init__(self, nc, es):
        self.nc = nc
        self.ops = []
        self.last_write = {}
        self.readers = {}
        self.sems = {e: es.enter_context(nc.semaphore(f"S_{e}")) for e in ENGINES}
        self.dma_sems = {e: [es.enter_context(nc.semaphore(f"D_{e}_{i}"))
                             for i in range(DMA_POOL)] for e in ENGINES}
        self.dma_count = {e: 0 for e in ENGINES}

    def op(self, engine, builder, reads=(), writes=(), dma=False):
        deps = set()
        for k in reads:
            if k in self.last_write:
                deps.add(self.last_write[k])
        for k in writes:
            if k in self.last_write:
                deps.add(self.last_write[k])
            for r in self.readers.get(k, ()):
                deps.add(r)
        i = len(self.ops)
        o = dict(id=i, engine=engine, builder=builder, deps=deps, dma=dma,
                 inc_no=None)
        if dma:
            n = self.dma_count[engine]
            self.dma_count[engine] = n + 1
            o["dsem"] = self.dma_sems[engine][n % DMA_POOL]
            o["duse"] = n // DMA_POOL + 1
        self.ops.append(o)
        for k in reads:
            self.readers.setdefault(k, []).append(i)
        for k in writes:
            self.last_write[k] = i
            self.readers[k] = []
        return i

    def dma(self, engine, out, in_, reads=(), writes=(), **kw):
        return self.op(engine, lambda e: e.dma_start(out=out, in_=in_, **kw),
                       reads=reads, writes=writes, dma=True)

    def finalize(self, block):
        needed = set()
        for o in self.ops:
            needed.update(o["deps"])
        counts = {e: 0 for e in ENGINES}
        for o in self.ops:
            if not o["dma"] and o["id"] in needed:
                counts[o["engine"]] += 1
                o["inc_no"] = counts[o["engine"]]
        per_engine = {e: [] for e in ENGINES}
        for o in self.ops:
            waits = {}
            for d in o["deps"]:
                p = self.ops[d]
                if (p["engine"] == "tensor" and o["engine"] == "tensor"
                        and not p["dma"] and not o["dma"]):
                    continue
                if p["dma"]:
                    s, v = p["dsem"], 16 * p["duse"]
                else:
                    s, v = self.sems[p["engine"]], p["inc_no"]
                if s.num not in waits or waits[s.num][1] < v:
                    waits[s.num] = (s, v)
            if o["dma"] and o["duse"] > 1:
                s = o["dsem"]
                if s.num not in waits:
                    waits[s.num] = (s, 16 * (o["duse"] - 1))
            o["waits"] = waits
            per_engine[o["engine"]].append(o)

        def mk_body(engine):
            def body(eng):
                satisfied = {}
                for o in per_engine[engine]:
                    for key, (s, v) in sorted(o["waits"].items()):
                        if satisfied.get(key, -1) >= v:
                            continue
                        eng.wait_ge(s, v)
                        satisfied[key] = v
                    inst = o["builder"](eng)
                    if o["dma"]:
                        inst.then_inc(o["dsem"], 16)
                    elif o["inc_no"] is not None:
                        inst.then_inc(self.sems[engine], 1)
                if engine == "sync":
                    for e in ENGINES:
                        n = self.dma_count[e]
                        for i in range(min(n, DMA_POOL)):
                            uses = (n - i - 1) // DMA_POOL + 1
                            eng.wait_ge(self.dma_sems[e][i], 16 * uses)
            return body

        for e in ENGINES:
            getattr(block, e)(mk_body(e))


def make_consts(nc):
    c = {}
    def inl(name, arr):
        c[name] = nc.inline_tensor(np.ascontiguousarray(arr), name=name).ap()
    eye = np.eye(128)
    inl("id_bf", eye.astype(BF))
    inl("id_f32", eye.astype(np.float32))
    inl("ones_row_f32", np.ones((1, 128), np.float32))
    inl("ones_row_bf", np.ones((1, 128), BF))
    inl("ones_col_f32", np.ones((128, 1), np.float32))
    ut = (np.arange(128)[:, None] <= np.arange(128)[None, :]).astype(np.float32)
    inl("ut_f32", ut)
    inl("ones128_f32", np.ones((128, 128), np.float32))
    inl("maskD", np.where(ut > 0, 0.0, -1e30).astype(np.float32))
    kk = np.arange(128)[:, None]
    cc = np.arange(896)[None, :]
    inl("bigmask", np.where(kk + 384 > cc, -1e9, 0.0).astype(np.float32))
    sel = np.zeros((8, 8 * 128), BF)
    for e in range(8):
        sel[e, e * 128:(e + 1) * 128] = 1
    inl("sel8", sel)
    return c


def load_const(p, nc, consts, name, shape, dt):
    t = nc.alloc_sbuf_tensor("c_" + name, shape, dt)
    p.dma("sync", t[:], consts[name][:], writes=["c_" + name])
    return t


def prep_lhsT(wT):
    K, N = wT.shape
    assert K % 128 == 0
    return np.ascontiguousarray(
        wT.reshape(K // 128, 128, N).transpose(1, 0, 2).reshape(128, -1))


def fm_from(tokmajor):
    Tn, F = tokmajor.shape
    return np.ascontiguousarray(
        tokmajor.T.reshape(F // 128, 128, Tn).transpose(1, 0, 2).reshape(128, -1))


def fm_to(arr, Tn, F):
    return np.ascontiguousarray(
        arr.reshape(128, F // 128, Tn).transpose(1, 0, 2).reshape(F, Tn).T)


def build_A():
    nc = bass.Bass("TRN2", target_bir_lowering=False)
    es = ExitStack()
    consts = make_consts(nc)
    NT = 17
    TC = 2176

    xa_d = nc.dram_tensor("xa", [NT * 128, D], F32, kind="ExternalInput").ap()
    wx_d = nc.dram_tensor("wxT", [128, 8 * INNER], F32, kind="ExternalInput").ap()
    wz_d = nc.dram_tensor("wzT", [128, 8 * INNER], F32, kind="ExternalInput").ap()
    wb_d = nc.dram_tensor("wbT", [128, 16 * 512], F32, kind="ExternalInput").ap()
    wc_d = nc.dram_tensor("wcT", [128, 16 * 512], F32, kind="ExternalInput").ap()
    wdt_d = nc.dram_tensor("wdtT", [128, 16 * 8], F32, kind="ExternalInput").ap()
    convw_d = nc.dram_tensor("convw", [128, 64], F32, kind="ExternalInput").ap()
    convb_d = nc.dram_tensor("convb", [128, 16], F32, kind="ExternalInput").ap()
    bdt_d = nc.dram_tensor("bdt", [8, 1], F32, kind="ExternalInput").ap()

    xn_o = nc.dram_tensor("xn_o", [128, 8 * T], BF16, kind="ExternalOutput").ap()
    xlo_o = nc.dram_tensor("xlo_o", [128, 8 * T], BF16, kind="ExternalOutput").ap()
    sz_o = nc.dram_tensor("sz_o", [128, 16 * T], BF16, kind="ExternalOutput").ap()
    xc_o = nc.dram_tensor("xc_o", [128, 16 * T], BF16, kind="ExternalOutput").ap()
    b_o = nc.dram_tensor("b_o", [64, 8 * T], BF16, kind="ExternalOutput").ap()
    c_o = nc.dram_tensor("c_o", [64, 8 * T], BF16, kind="ExternalOutput").ap()
    dt_o = nc.dram_tensor("dt_o", [8, T], F32, kind="ExternalOutput").ap()

    sb = nc.alloc_sbuf_tensor
    ps = nc.alloc_psum_tensor
    p = Prog(nc, es)
    idb = load_const(p, nc, consts, "id_bf", [128, 128], BF16)

    xn = sb("xn", [128, 8 * TC], BF16)
    wbig = sb("wbig", [128, 8 * INNER], BF16)
    xt = [sb(f"xt{i}", [128, D], F32) for i in range(2)]
    xn32 = sb("xn32", [128, D], F32)
    xnb = sb("xnb", [128, D], BF16)
    xlo_t = sb("xlo_t", [128, D], BF16)
    xlo_st = sb("xlo_st", [128, D], BF16)
    mcol = sb("mcol", [128, 1], F32)
    vcol = sb("vcol", [128, 4], F32)
    sq = sb("sq", [128, D], F32)
    big = sb("big", [128, 16 * T], BF16)
    xpb = sb("xpb", [128, 16 * 515], BF16)
    wb2 = wbig[:, 0:16 * 512]
    wc2 = wbig[:, 16 * 512:16 * 1024]
    wdt2 = sb("wdt2", [128, 16 * 8], BF16)
    cw = sb("cw", [128, 64], F32)
    cb = sb("cb", [128, 16], F32)
    bdt = sb("bdt_s", [8, 1], F32)
    bstage = sb("bstage", [64, 8 * 512], BF16)
    cstage = sb("cstage", [64, 8 * 512], BF16)
    dts = sb("dts", [8, T], F32)
    dtt = sb("dtt", [8, 512], F32)
    spa = sb("spa", [8, 512], F32)
    cvt = sb("cvt", [128, 512], F32)

    pml = [ps(f"pm{i}", [128, 512], F32) for i in range(2)]
    ptr = ps("ptr", [128, 128], BF16)
    pb = ps("pb", [64, 512], F32)
    pdt = ps("pdt", [8, 512], F32)

    p.dma("gpsimd", cw[:], convw_d[:], writes=["cw"])
    p.dma("gpsimd", cb[:], convb_d[:], writes=["cb"])
    p.dma("gpsimd", bdt[:], bdt_d[:], writes=["bdt"])

    p.dma("gpsimd", wbig[:], wz_d[:], writes=["wbA", "wbB"])

    for t in range(NT):
        xtw, kxt = xt[t % 2], f"xt{t % 2}"
        p.dma("sync" if t % 2 == 0 else "scalar", xtw[:],
              xa_d[t * 128:(t + 1) * 128, :], writes=[kxt])
        p.op("vector", lambda e, xtw=xtw: e.reduce_sum(mcol[:], xtw[:], axis=AX.X),
             reads=[kxt], writes=["mcol"])
        p.op("scalar", lambda e, xtw=xtw: e.activation(sq[:], xtw[:], AF.Square),
             reads=[kxt], writes=["sq"])
        p.op("vector", lambda e: e.reduce_sum(vcol[:, 0:1], sq[:], axis=AX.X),
             reads=["sq"], writes=["vcol"])
        p.op("vector", lambda e: e.tensor_scalar(mcol[:], mcol[:], 1.0 / D, None, ALU.mult),
             reads=["mcol"], writes=["mcol"])
        p.op("scalar", lambda e: e.activation(vcol[:, 1:2], mcol[:], AF.Square),
             reads=["mcol"], writes=["vcol2"])
        p.op("vector", lambda e: e.scalar_tensor_tensor(
            vcol[:, 2:3], vcol[:, 0:1], 1.0 / D, vcol[:, 1:2], ALU.mult, ALU.subtract),
             reads=["vcol", "vcol2"], writes=["vcol3"])
        p.op("vector", lambda e: e.tensor_scalar(vcol[:, 2:3], vcol[:, 2:3], EPS, None, ALU.add),
             reads=["vcol3"], writes=["vcol3"])
        p.op("scalar", lambda e: e.activation(vcol[:, 3:4], vcol[:, 2:3], AF.Ln),
             reads=["vcol3"], writes=["vcol4"])
        p.op("scalar", lambda e: e.activation(vcol[:, 3:4], vcol[:, 3:4], AF.Exp, scale=-0.5),
             reads=["vcol4"], writes=["vcol4"])
        p.op("vector", lambda e, xtw=xtw: e.tensor_scalar(
            xn32[:], xtw[:], mcol[:], vcol[:, 3:4], ALU.subtract, ALU.mult),
             reads=[kxt, "mcol", "vcol4"], writes=["xn32"])
        p.op("vector", lambda e: e.tensor_copy(xnb[:], xn32[:]),
             reads=["xn32"], writes=["xnb"])
        p.op("vector", lambda e: e.tensor_sub(xlo_t[:], xn32[:], xnb[:]),
             reads=["xn32", "xnb"], writes=["xlo_t"])
        for kc in range(8):
            p.op("tensor", lambda e, kc=kc: e.transpose(
                ptr[:], xnb[:, kc * 128:(kc + 1) * 128], idb[:]),
                 reads=["xnb", "c_id_bf"], writes=["ptr"])
            p.op("vector", lambda e, kc=kc, t=t: e.tensor_copy(
                xn[:, kc * TC + t * 128: kc * TC + (t + 1) * 128], ptr[:]),
                 reads=["ptr"], writes=["xn"])
        if t >= 1:
            for kc in range(8):
                p.op("tensor", lambda e, kc=kc: e.transpose(
                    ptr[:], xlo_t[:, kc * 128:(kc + 1) * 128], idb[:]),
                     reads=["xlo_t", "c_id_bf"], writes=["ptr"])
                p.op("vector", lambda e, kc=kc: e.tensor_copy(
                    xlo_st[:, kc * 128:(kc + 1) * 128], ptr[:]),
                     reads=["ptr"], writes=["xlo_st"])
            p.dma("sync",
                  xlo_o[:].rearrange("p (a b) -> p a b", b=T)[:, :, (t - 1) * 128: t * 128],
                  xlo_st[:].rearrange("p (a b) -> p a b", b=128),
                  reads=["xlo_st"])

    for j in range(4):
        off = 128 + j * 512
        for ic in range(16):
            pm = pml[ic % 2]
            for kc in range(8):
                p.op("tensor", lambda e, ic=ic, kc=kc, off=off, pm=pm: e.matmul(
                    pm[:], wbig[:, kc * INNER + ic * 128: kc * INNER + (ic + 1) * 128],
                    xn[:, kc * TC + off: kc * TC + off + 512],
                    start=(kc == 0), stop=(kc == 7)),
                     reads=["wbA" if kc < 4 else "wbB", "xn"], writes=[f"pm{ic % 2}"])
            p.op("scalar", lambda e, ic=ic, j=j, pm=pm: e.activation(
                big[:, ic * T + j * 512: ic * T + (j + 1) * 512], pm[:], AF.Silu),
                 reads=[f"pm{ic % 2}"], writes=["big"])
    p.dma("sync", sz_o[:], big[:], reads=["big"])

    p.dma("gpsimd", wbig[:], wx_d[:], writes=["wbA", "wbB"])
    for j in range(4):
        off = 128 + j * 512
        for ic in range(16):
            pm = pml[ic % 2]
            for kc in range(8):
                p.op("tensor", lambda e, ic=ic, kc=kc, off=off, pm=pm: e.matmul(
                    pm[:], wbig[:, kc * INNER + ic * 128: kc * INNER + (ic + 1) * 128],
                    xn[:, kc * TC + off: kc * TC + off + 512],
                    start=(kc == 0), stop=(kc == 7)),
                     reads=["wbA" if kc < 4 else "wbB", "xn"], writes=[f"pm{ic % 2}"])
            p.op("vector", lambda e, ic=ic, pm=pm: e.tensor_copy(
                xpb[:, ic * 515 + 3: (ic + 1) * 515], pm[:]),
                 reads=[f"pm{ic % 2}"], writes=["xpb" + str(ic)])
        if j == 0:
            for ic in range(16):
                pm = pml[ic % 2]
                for kc in range(8):
                    p.op("tensor", lambda e, ic=ic, kc=kc, pm=pm: e.matmul(
                        pm[:, 0:3], wbig[:, kc * INNER + ic * 128: kc * INNER + (ic + 1) * 128],
                        xn[:, kc * TC + 125: kc * TC + 128],
                        start=(kc == 0), stop=(kc == 7)),
                         reads=["wbA" if kc < 4 else "wbB", "xn"], writes=[f"pm{ic % 2}"])
                p.op("vector", lambda e, ic=ic, pm=pm: e.tensor_copy(
                    xpb[:, ic * 515: ic * 515 + 3], pm[:, 0:3]),
                     reads=[f"pm{ic % 2}"], writes=["xpb" + str(ic)])
        for ic in range(16):
            p.op("vector", lambda e, ic=ic: e.tensor_scalar(
                cvt[:], xpb[:, ic * 515: ic * 515 + 512], cw[:, ic * 4: ic * 4 + 1],
                None, ALU.mult),
                 reads=["xpb" + str(ic), "cw"], writes=["cvt"])
            for k in range(1, 4):
                p.op("vector", lambda e, ic=ic, k=k: e.scalar_tensor_tensor(
                    cvt[:], xpb[:, ic * 515 + k: ic * 515 + k + 512],
                    cw[:, ic * 4 + k: ic * 4 + k + 1], cvt[:], ALU.mult, ALU.add),
                     reads=["xpb" + str(ic), "cw", "cvt"],
                     writes=["cvt"])
            p.op("scalar", lambda e, ic=ic, j=j: e.activation(
                big[:, ic * T + j * 512: ic * T + (j + 1) * 512], cvt[:],
                AF.Silu, bias=cb[:, ic: ic + 1]),
                 reads=["cvt", "cb"], writes=["big"])
        if j < 3:
            for ic in range(16):
                p.op("vector", lambda e, ic=ic: e.tensor_copy(
                    xpb[:, ic * 515: ic * 515 + 3],
                    xpb[:, ic * 515 + 512: ic * 515 + 515]),
                     reads=["xpb" + str(ic)], writes=["xpb" + str(ic)])
    p.dma("sync", xc_o[:], big[:], reads=["big"])
    p.dma("sync", xn_o[:], xn[:].rearrange("p (a b) -> p a b", b=TC)[:, :, 128:TC],
          reads=["xn"])

    p.dma("gpsimd", wb2, wb_d[:], writes=["wbA", "wb2"])
    p.dma("gpsimd", wc2, wc_d[:], writes=["wbB", "wc2"])
    p.dma("gpsimd", wdt2[:], wdt_d[:], writes=["wdt2"])
    for j in range(4):
        for h in range(8):
            for (wname, wt, dkey) in (("wb2", wb2, "bstage"),
                                      ("wc2", wc2, "cstage")):
                dst = bstage if dkey == "bstage" else cstage
                for kc in range(16):
                    p.op("tensor", lambda e, h=h, kc=kc, j=j, wt=wt: e.matmul(
                        pb[:], wt[:, kc * 512 + h * 64: kc * 512 + (h + 1) * 64],
                        big[:, kc * T + j * 512: kc * T + (j + 1) * 512],
                        start=(kc == 0), stop=(kc == 15)),
                         reads=[wname, "big"], writes=["pb"])
                p.op("vector", lambda e, h=h, dst=dst: e.tensor_copy(
                    dst[:, h * 512: (h + 1) * 512], pb[:]),
                     reads=["pb"], writes=[dkey])
        p.dma("sync", b_o[:].rearrange("p (a b) -> p a b", b=T)[:, :, j * 512:(j + 1) * 512],
              bstage[:].rearrange("p (a b) -> p a b", b=512), reads=["bstage"])
        p.dma("sync", c_o[:].rearrange("p (a b) -> p a b", b=T)[:, :, j * 512:(j + 1) * 512],
              cstage[:].rearrange("p (a b) -> p a b", b=512), reads=["cstage"])
        for kc in range(16):
            p.op("tensor", lambda e, kc=kc, j=j: e.matmul(
                pdt[:], wdt2[:, kc * 8:(kc + 1) * 8],
                big[:, kc * T + j * 512: kc * T + (j + 1) * 512],
                start=(kc == 0), stop=(kc == 15)),
                 reads=["wdt2", "big"], writes=["pdt"])
        p.op("vector", lambda e: e.tensor_scalar(dtt[:], pdt[:], bdt[:], None, ALU.add),
             reads=["pdt", "bdt"], writes=["dtt"])
        p.op("scalar", lambda e: e.activation(spa[:], dtt[:], AF.Abs),
             reads=["dtt"], writes=["spa"])
        p.op("scalar", lambda e: e.activation(spa[:], spa[:], AF.Exp, scale=-1.0),
             reads=["spa"], writes=["spa"])
        p.op("scalar", lambda e: e.activation(spa[:], spa[:], AF.Ln, bias=1.0),
             reads=["spa"], writes=["spa"])
        p.op("vector", lambda e: e.tensor_scalar_max(dtt[:], dtt[:], 0.0),
             reads=["dtt"], writes=["dtt"])
        p.op("vector", lambda e, j=j: e.tensor_add(
            dts[:, j * 512:(j + 1) * 512], dtt[:], spa[:]),
             reads=["dtt", "spa"], writes=["dts"])
    p.dma("sync", dt_o[:], dts[:], reads=["dts"])

    with nc.Block() as block:
        p.finalize(block)
    nc.finalize()
    return nc


def host_A(inputs):
    x = np.asarray(inputs["L0_attn_input_tensor"])
    wx = prep_lhsT(np.ascontiguousarray(np.asarray(inputs["L0_attn_Wx"]).T))
    wz = prep_lhsT(np.ascontiguousarray(np.asarray(inputs["L0_attn_Wz"]).T))
    wb = prep_lhsT(np.ascontiguousarray(np.asarray(inputs["L0_attn_Wb"]).T))
    wc = prep_lhsT(np.ascontiguousarray(np.asarray(inputs["L0_attn_Wc"]).T))
    wdt = prep_lhsT(np.ascontiguousarray(np.asarray(inputs["L0_attn_Wdt"]).T))
    cwf = np.asarray(inputs["L0_attn_conv_w"])[:, 0, :]
    convw = np.ascontiguousarray(cwf.reshape(16, 128, 4).transpose(1, 0, 2).reshape(128, 64))
    convb = np.ascontiguousarray(np.asarray(inputs["L0_attn_conv_b"]).reshape(16, 128).T)
    bdt = np.asarray(inputs["L0_attn_b_dt"]).reshape(8, 1)
    maps = []
    for c in range(N_CORES):
        b_, k = c // 4, c % 4
        xa = np.zeros((17 * 128, D), np.float32)
        if k > 0:
            xa[125:128] = x[b_, k * T - 3: k * T]
        xa[128:] = x[b_, k * T: (k + 1) * T]
        maps.append({"xa": xa, "wxT": wx, "wzT": wz, "wbT": wb, "wcT": wc,
                     "wdtT": wdt, "convw": convw, "convb": convb, "bdt": bdt})
    return maps


def build_B():
    nc = bass.Bass("TRN2", target_bir_lowering=False)
    es = ExitStack()
    consts = make_consts(nc)
    NCH = 64

    xu_d = [nc.dram_tensor(f"xu{u}", [128, NCH * HDIM], BF16, kind="ExternalInput").ap() for u in range(2)]
    bu_d = [nc.dram_tensor(f"bu{u}", [64, S], BF16, kind="ExternalInput").ap() for u in range(2)]
    cu_d = [nc.dram_tensor(f"cu{u}", [64, S], BF16, kind="ExternalInput").ap() for u in range(2)]
    dtu_d = [nc.dram_tensor(f"dtu{u}", [128, NCH], F32, kind="ExternalInput").ap() for u in range(2)]
    la_d = nc.dram_tensor("la", [2, 1], F32, kind="ExternalInput").ap()
    dp_d = nc.dram_tensor("dp", [2, 1], F32, kind="ExternalInput").ap()
    gnw_d = nc.dram_tensor("gnw", [2, HDIM], F32, kind="ExternalInput").ap()
    gnb_d = nc.dram_tensor("gnb", [2, HDIM], F32, kind="ExternalInput").ap()
    y_o = nc.dram_tensor("y_o", [2, 128, NCH * HDIM], BF16, kind="ExternalOutput").ap()

    sb = nc.alloc_sbuf_tensor
    ps = nc.alloc_psum_tensor
    p = Prog(nc, es)
    idb = load_const(p, nc, consts, "id_bf", [128, 128], BF16)
    onr = load_const(p, nc, consts, "ones_row_f32", [1, 128], F32)
    onc = load_const(p, nc, consts, "ones_col_f32", [128, 1], F32)
    ut = load_const(p, nc, consts, "ut_f32", [128, 128], F32)
    on128 = load_const(p, nc, consts, "ones128_f32", [128, 128], F32)
    mkd = load_const(p, nc, consts, "maskD", [128, 128], F32)

    Xu = sb("Xu", [128, NCH * HDIM], BF16)
    Bu = sb("Bu", [64, S], BF16)
    Cu = sb("Cu", [64, S], BF16)
    dtu = sb("dtu", [128, NCH], F32)
    y_sb = sb("y_sb", [128, NCH * HDIM], BF16)
    dta = sb("dta", [128, NCH], F32)
    small = sb("small", [128, 8], F32)
    acol = sb("acol", [128, 1], F32)
    aend = sb("aend", [128, 1], F32)
    arow = sb("arow", [1, 128], F32)
    a_col = sb("a_col", [128, 1], F32)
    d_col = sb("d_col", [128, 1], F32)
    d_sb = sb("d_sb", [128, 128], F32)
    m_bf = sb("m_bf", [128, 128], BF16)
    ea_bf = sb("ea_bf", [64, 128], BF16)
    cbar = sb("cbar", [64, 128], BF16)
    p_f = sb("p_f", [128, 128], F32)
    p_bf = sb("p_bf", [128, 128], BF16)
    f_col = sb("f_col", [128, 1], F32)
    bt_bf = sb("bt_bf", [128, 64], BF16)
    e64 = sb("e64", [64, 1], F32)
    Ssb = sb("Ssb", [64, HDIM], F32)
    S_bf = sb("S_bf", [64, HDIM], BF16)
    sum_cols = sb("sum_cols", [128, NCH], F32)
    sq_cols = sb("sq_cols", [128, NCH], F32)
    sq_scr = sb("sq_scr", [128, HDIM], F32)
    gnw_row = sb("gnw_row", [1, HDIM], F32)
    gnb_row = sb("gnb_row", [1, HDIM], F32)
    k1b = sb("k1b", [128, HDIM], F32)
    k2b = sb("k2b", [128, HDIM], F32)
    t256 = sb("t256", [128, HDIM], F32)

    pc = ps("pc", [128, 4], F32)
    parow = ps("parow", [1, 128], F32)
    pbca = ps("pbca", [128, 128], F32)
    pg = ps("pg", [128, 128], F32)
    py = ps("py", [128, HDIM], F32)
    pbt = ps("pbt", [128, 64], BF16)
    pds = ps("pds", [64, HDIM], F32)

    for u in range(2):
        p.dma("sync", Xu[:], xu_d[u][:], writes=["Xu"])
        p.dma("sync", Bu[:], bu_d[u][:], writes=["Bu"])
        p.dma("sync", Cu[:], cu_d[u][:], writes=["Cu"])
        p.dma("sync", dtu[:], dtu_d[u][:], writes=["dtu"])
        p.dma("sync", small[0:1, 0:1], la_d[u:u + 1, :], writes=["la"])
        p.dma("sync", small[0:1, 1:2], dp_d[u:u + 1, :], writes=["dp"])
        p.dma("sync", gnw_row[:], gnw_d[u:u + 1, :], writes=["gnw_row"])
        p.dma("sync", gnb_row[:], gnb_d[u:u + 1, :], writes=["gnb_row"])
        p.op("scalar", lambda e: e.activation(small[0:1, 2:3], small[0:1, 0:1], AF.Exp),
             reads=["la"], writes=["a11"])
        p.op("vector", lambda e: e.tensor_scalar(small[0:1, 2:3], small[0:1, 2:3], -1.0, None, ALU.mult),
             reads=["a11"], writes=["a11"])
        p.op("tensor", lambda e: e.matmul(pc[:, 0:1], onr[:], small[0:1, 2:3], start=True, stop=True),
             reads=["a11", "c_ones_row_f32"], writes=["pcb"])
        p.op("vector", lambda e: e.tensor_copy(a_col[:], pc[:, 0:1]), reads=["pcb"], writes=["a_col"])
        p.op("tensor", lambda e: e.matmul(pc[:, 1:2], onr[:], small[0:1, 1:2], start=True, stop=True),
             reads=["dp", "c_ones_row_f32"], writes=["pcb"])
        p.op("vector", lambda e: e.tensor_copy(d_col[:], pc[:, 1:2]), reads=["pcb"], writes=["d_col"])
        p.op("vector", lambda e: e.tensor_scalar(dta[:], dtu[:], a_col[:], None, ALU.mult),
             reads=["dtu", "a_col"], writes=["dta"])
        p.op("vector", lambda e: e.memset(Ssb[:], 0.0), reads=[], writes=["Ssb"])

        for i in range(NCH):
            xsl = Xu[:, i * HDIM:(i + 1) * HDIM]
            bsl = Bu[:, i * 128:(i + 1) * 128]
            csl = Cu[:, i * 128:(i + 1) * 128]
            dcol = dtu[:, i:i + 1]
            dtacol = dta[:, i:i + 1]
            p.op("tensor", lambda e, dtacol=dtacol: e.matmul(pc[:, 2:3], ut[:], dtacol, start=True, stop=True),
                 reads=["dta", "c_ut_f32"], writes=["pcb"])
            p.op("vector", lambda e: e.tensor_copy(acol[:], pc[:, 2:3]), reads=["pcb"], writes=["acol"])
            p.op("tensor", lambda e, dtacol=dtacol: e.matmul(pc[:, 3:4], on128[:], dtacol, start=True, stop=True),
                 reads=["dta", "c_ones128_f32"], writes=["pcb"])
            p.op("vector", lambda e: e.tensor_copy(aend[:], pc[:, 3:4]), reads=["pcb"], writes=["aend"])
            p.op("tensor", lambda e, dtacol=dtacol: e.matmul(parow[:], dtacol, ut[:], start=True, stop=True),
                 reads=["dta", "c_ut_f32"], writes=["parow"])
            p.op("vector", lambda e: e.tensor_copy(arow[:], parow[:]), reads=["parow"], writes=["arow"])
            p.op("tensor", lambda e: e.matmul(pbca[:], onr[:], arow[:], start=True, stop=True),
                 reads=["arow", "c_ones_row_f32"], writes=["pbca"])
            p.op("vector", lambda e: e.scalar_tensor_tensor(
                d_sb[:], pbca[:], acol[:], mkd[:], ALU.subtract, ALU.add),
                 reads=["pbca", "acol", "c_maskD"], writes=["d_sb", "bca_tok"])
            p.op("scalar", lambda e: e.activation(m_bf[:], d_sb[:], AF.Exp),
                 reads=["d_sb"], writes=["m_bf"])
            p.op("scalar", lambda e: e.activation(ea_bf[:], pbca[0:64, :], AF.Exp),
                 reads=["pbca", "bca_tok"], writes=["ea_bf"])
            if i > 0:
                p.op("vector", lambda e, csl=csl: e.tensor_tensor(cbar[:], csl, ea_bf[:], ALU.mult),
                     reads=["Cu", "ea_bf"], writes=["cbar"])
            p.op("tensor", lambda e, bsl=bsl, csl=csl: e.matmul(pg[:], bsl, csl, start=True, stop=True),
                 reads=["Bu", "Cu"], writes=["pg"])
            p.op("vector", lambda e: e.tensor_tensor(p_f[:], pg[:], m_bf[:], ALU.mult),
                 reads=["pg", "m_bf"], writes=["p_f"])
            p.op("vector", lambda e, dcol=dcol: e.tensor_scalar(p_bf[:], p_f[:], dcol, None, ALU.mult),
                 reads=["p_f", "dtu"], writes=["p_bf"])
            p.op("tensor", lambda e, xsl=xsl, i=i: e.matmul(py[:], p_bf[:], xsl,
                 start=True, stop=(i == 0)), reads=["p_bf", "Xu"], writes=["py"])
            if i > 0:
                p.op("tensor", lambda e: e.matmul(py[:], cbar[:], S_bf[:], start=False, stop=True),
                     reads=["cbar", "S_bf"], writes=["py"])
            ysl = y_sb[:, i * HDIM:(i + 1) * HDIM]
            p.op("vector", lambda e, xsl=xsl, ysl=ysl: e.scalar_tensor_tensor(
                ysl, xsl, d_col[:], py[:], ALU.mult, ALU.add),
                 reads=["Xu", "d_col", "py"], writes=["y_sb" + str(i % 4)])
            p.op("vector", lambda e, ysl=ysl, i=i: e.reduce_sum(
                sum_cols[:, i:i + 1], ysl, axis=AX.X),
                 reads=["y_sb" + str(i % 4)], writes=["sum_cols"])
            p.op("scalar", lambda e, ysl=ysl, i=i: e.activation(
                sq_scr[:], ysl, AF.Square, accum_out=sq_cols[:, i:i + 1]),
                 reads=["y_sb" + str(i % 4)], writes=["sq_cols", "sq_scr"])
            p.op("vector", lambda e: e.tensor_sub(f_col[:], aend[:], acol[:]),
                 reads=["aend", "acol"], writes=["f_col"])
            p.op("scalar", lambda e: e.activation(f_col[:], f_col[:], AF.Exp),
                 reads=["f_col"], writes=["f_col"])
            p.op("vector", lambda e, dcol=dcol: e.tensor_scalar(f_col[:], f_col[:], dcol, None, ALU.mult),
                 reads=["f_col", "dtu"], writes=["f_col"])
            p.op("tensor", lambda e, bsl=bsl: e.transpose(pbt[:], bsl, idb[0:64, 0:64]),
                 reads=["Bu", "c_id_bf"], writes=["pbt"])
            p.op("vector", lambda e: e.tensor_scalar(bt_bf[:], pbt[:], f_col[:], None, ALU.mult),
                 reads=["pbt", "f_col"], writes=["bt_bf"])
            p.op("tensor", lambda e, xsl=xsl: e.matmul(pds[:], bt_bf[:], xsl, start=True, stop=True),
                 reads=["bt_bf", "Xu"], writes=["pds"])
            p.op("scalar", lambda e: e.activation(e64[:], aend[0:64, :], AF.Exp),
                 reads=["aend"], writes=["e64"])
            p.op("vector", lambda e: e.scalar_tensor_tensor(
                Ssb[:], Ssb[:], e64[:], pds[:], ALU.mult, ALU.add),
                 reads=["Ssb", "e64", "pds"], writes=["Ssb"])
            p.op("vector", lambda e: e.tensor_copy(S_bf[:], Ssb[:]),
                 reads=["Ssb"], writes=["S_bf"])

        p.op("vector", lambda e: e.reduce_sum(acol[:], sum_cols[:], axis=AX.X),
             reads=["sum_cols"], writes=["acol"])
        p.op("vector", lambda e: e.reduce_sum(aend[:], sq_cols[:], axis=AX.X),
             reads=["sq_cols"], writes=["aend"])
        p.op("tensor", lambda e: e.matmul(pc[0:1, 0:1], acol[:], onc[:], start=True, stop=True),
             reads=["acol", "c_ones_col_f32"], writes=["pcb"])
        p.op("vector", lambda e: e.tensor_scalar(small[0:1, 3:4], pc[0:1, 0:1],
             1.0 / (S * HDIM), None, ALU.mult), reads=["pcb"], writes=["mean"])
        p.op("tensor", lambda e: e.matmul(pc[0:1, 1:2], aend[:], onc[:], start=True, stop=True),
             reads=["aend", "c_ones_col_f32"], writes=["pcb"])
        p.op("scalar", lambda e: e.activation(small[0:1, 4:5], small[0:1, 3:4], AF.Square),
             reads=["mean"], writes=["msq"])
        p.op("vector", lambda e: e.scalar_tensor_tensor(
            small[0:1, 5:6], pc[0:1, 1:2], 1.0 / (S * HDIM), small[0:1, 4:5],
            ALU.mult, ALU.subtract), reads=["pcb", "msq"], writes=["var"])
        p.op("vector", lambda e: e.tensor_scalar(small[0:1, 5:6], small[0:1, 5:6],
             EPS, None, ALU.add), reads=["var"], writes=["var"])
        p.op("scalar", lambda e: e.activation(small[0:1, 5:6], small[0:1, 5:6], AF.Ln),
             reads=["var"], writes=["var"])
        p.op("scalar", lambda e: e.activation(small[0:1, 5:6], small[0:1, 5:6], AF.Exp, scale=-0.5),
             reads=["var"], writes=["var"])
        p.op("vector", lambda e: e.tensor_scalar(small[0:1, 6:7], small[0:1, 3:4], -1.0, None, ALU.mult),
             reads=["mean"], writes=["negmean"])
        p.op("vector", lambda e: e.tensor_scalar(gnw_row[:], gnw_row[:], small[0:1, 5:6], None, ALU.mult),
             reads=["gnw_row", "var"], writes=["gnw_row"])
        p.op("vector", lambda e: e.scalar_tensor_tensor(
            gnb_row[:], gnw_row[:], small[0:1, 6:7], gnb_row[:], ALU.mult, ALU.add),
             reads=["gnw_row", "negmean", "gnb_row"], writes=["gnb_row"])
        p.op("tensor", lambda e: e.matmul(py[:], onr[:], gnw_row[:], start=True, stop=True),
             reads=["gnw_row", "c_ones_row_f32"], writes=["py"])
        p.op("vector", lambda e: e.tensor_copy(k1b[:], py[:]), reads=["py"], writes=["k1b"])
        p.op("tensor", lambda e: e.matmul(py[:], onr[:], gnb_row[:], start=True, stop=True),
             reads=["gnb_row", "c_ones_row_f32"], writes=["py"])
        p.op("vector", lambda e: e.tensor_copy(k2b[:], py[:]), reads=["py"], writes=["k2b"])
        for i in range(NCH):
            ysl = y_sb[:, i * HDIM:(i + 1) * HDIM]
            p.op("vector", lambda e, ysl=ysl: e.tensor_tensor(t256[:], ysl, k1b[:], ALU.mult),
                 reads=["y_sb" + str(i % 4), "k1b"], writes=["t256"])
            p.op("vector", lambda e, ysl=ysl: e.tensor_tensor(ysl, t256[:], k2b[:], ALU.add),
                 reads=["t256", "k2b"], writes=["y_sb" + str(i % 4)])
        p.dma("sync", y_o[u], y_sb[:],
              reads=["y_sb0", "y_sb1", "y_sb2", "y_sb3"])
    with nc.Block() as block:
        p.finalize(block)
    nc.finalize()
    return nc


def host_B(xc_g, b_g, c_g, dt_g, inputs):
    la = np.asarray(inputs["L0_attn_log_a"], np.float32)
    dp = np.asarray(inputs["L0_attn_d_param"], np.float32)
    gnw = np.asarray(inputs["L0_attn_gn_w"], np.float32).reshape(HEADS, HDIM)
    gnb = np.asarray(inputs["L0_attn_gn_b"], np.float32).reshape(HEADS, HDIM)
    maps = []
    for c in range(N_CORES):
        m = {}
        us = [divmod(2 * c + u, HEADS) for u in range(2)]
        for u, (bb, hh) in enumerate(us):
            X = np.ascontiguousarray(xc_g[bb, :, hh * HDIM:(hh + 1) * HDIM]
                                     .reshape(64, 128, HDIM).transpose(1, 0, 2).reshape(128, -1))
            m[f"xu{u}"] = X
            m[f"bu{u}"] = np.ascontiguousarray(b_g[bb, :, hh * 64:(hh + 1) * 64].T)
            m[f"cu{u}"] = np.ascontiguousarray(c_g[bb, :, hh * 64:(hh + 1) * 64].T)
            m[f"dtu{u}"] = np.ascontiguousarray(dt_g[bb, :, hh].reshape(64, 128).T)
        m["la"] = np.stack([la[h] for _, h in us]).reshape(2, 1)
        m["dp"] = np.stack([dp[h] for _, h in us]).reshape(2, 1)
        m["gnw"] = np.stack([gnw[h] for _, h in us])
        m["gnb"] = np.stack([gnb[h] for _, h in us])
        maps.append(m)
    return maps


def build_M():
    nc = bass.Bass("TRN2", target_bir_lowering=False)
    es = ExitStack()
    consts = make_consts(nc)

    xn_d = nc.dram_tensor("xn", [128, 8 * T], BF16, kind="ExternalInput").ap()
    xlo_d = nc.dram_tensor("xlo", [128, 8 * T], BF16, kind="ExternalInput").ap()
    gw_d = nc.dram_tensor("gw", [128, 8 * 8], F32, kind="ExternalInput").ap()
    shg_d = nc.dram_tensor("shg", [128, 8 * INNER], F32, kind="ExternalInput").ap()
    shu_d = nc.dram_tensor("shu", [128, 8 * INNER], F32, kind="ExternalInput").ap()
    shd_d = nc.dram_tensor("shd", [128, 16 * D], F32, kind="ExternalInput").ap()
    rtg_d = nc.dram_tensor("rtg", [8, 128, 8 * INNER], F32, kind="ExternalInput").ap()
    rtu_d = nc.dram_tensor("rtu", [8, 128, 8 * INNER], F32, kind="ExternalInput").ap()
    rtd_d = nc.dram_tensor("rtd", [8, 128, 16 * D], F32, kind="ExternalInput").ap()
    g_o = nc.dram_tensor("g_o", [128, 8 * T], BF16, kind="ExternalOutput").ap()

    sb = nc.alloc_sbuf_tensor
    ps = nc.alloc_psum_tensor
    p = Prog(nc, es)
    idb = load_const(p, nc, consts, "id_bf", [128, 128], BF16)
    sel8 = load_const(p, nc, consts, "sel8", [8, 8 * 128], BF16)

    xnf = sb("xnf", [128, 8 * T], BF16)
    gwl = sb("gwl", [128, 8 * 8], BF16)
    gw32 = sb("gw32", [128, 8 * 8], F32)
    gwlo = sb("gwlo", [128, 8 * 8], BF16)
    wg = sb("wg", [128, 8 * INNER], BF16)
    wu = sb("wu", [128, 8 * INNER], BF16)
    wd = sb("wd", [128, 16 * D], BF16)
    xets = sb("xets", [128, 8 * 512], BF16)
    actb = sb("actb", [128, 16 * 512], BF16)
    tb_bf = sb("tb_bf", [128, 512], BF16)
    y_sb = sb("y_sb2", [128, 8 * T], BF16)
    mt = sb("mt", [128, 8], BF16)
    tv = sb("tv", [128, 2], F32)
    mhm = sb("mhm", [8, T], BF16)
    msk = sb("msk", [128, T], BF16)

    plog = ps("plog", [128, 8], F32)
    ptr8 = ps("ptr8", [8, 128], BF16)
    pg = ps("pg", [128, 512], F32)
    pu = ps("pu", [128, 512], F32)
    pd = ps("pd", [128, 512], F32)
    pmsk = ps("pmsk", [128, 512], F32)

    p.dma("sync", xnf[:], xn_d[:], writes=["xnf"])
    p.dma("sync", wg[:], xlo_d[:], writes=["wg"])
    p.dma("gpsimd", gwl[:], gw_d[:], writes=["gwl"])
    p.dma("sync", gw32[:], gw_d[:], writes=["gw32"])
    p.op("vector", lambda e: e.tensor_sub(gwlo[:], gw32[:], gwl[:]),
         reads=["gw32", "gwl"], writes=["gwlo"])

    for t in range(16):
        for pi, (xb_, xkey, wb_, wkey) in enumerate(
                ((xnf, "xnf", gwl, "gwl"), (wg, "wg", gwl, "gwl"),
                 (xnf, "xnf", gwlo, "gwlo"))):
            for kc in range(8):
                p.op("tensor", lambda e, t=t, kc=kc, xb_=xb_, wb_=wb_, pi=pi: e.matmul(
                    plog[:], xb_[:, kc * T + t * 128: kc * T + (t + 1) * 128],
                    wb_[:, kc * 8:(kc + 1) * 8],
                    start=(pi == 0 and kc == 0), stop=(pi == 2 and kc == 7)),
                     reads=[xkey, wkey], writes=["plog"])
        p.op("vector", lambda e: e.reduce_max(tv[:, 0:1], plog[:], axis=AX.X),
             reads=["plog"], writes=["tv0"])
        p.op("scalar", lambda e: e.activation(tv[:, 1:2], tv[:, 0:1], AF.Sigmoid),
             reads=["tv0"], writes=["tv1"])
        p.op("vector", lambda e: e.tensor_scalar(
            mt[:], plog[:], tv[:, 0:1], tv[:, 1:2], ALU.is_equal, ALU.mult),
             reads=["plog", "tv0", "tv1"], writes=["mt"])
        p.op("tensor", lambda e: e.transpose(ptr8[:], mt[:], idb[:]),
             reads=["mt", "c_id_bf"], writes=["ptr8"])
        p.op("vector", lambda e, t=t: e.tensor_copy(mhm[:, t * 128:(t + 1) * 128], ptr8[:]),
             reads=["ptr8"], writes=["mhm"])

    def ffn_branch(first, rhs_fn, pre_tb=None):
        for tb in range(4):
            if pre_tb is not None:
                pre_tb(tb)
            for ic in range(16):
                for kc in range(8):
                    rap, rkey = rhs_fn(kc, tb)
                    p.op("tensor", lambda e, ic=ic, kc=kc, rap=rap: e.matmul(
                        pg[:], wg[:, kc * INNER + ic * 128: kc * INNER + (ic + 1) * 128],
                        rap, start=(kc == 0), stop=(kc == 7)),
                         reads=["wg", rkey], writes=["pg"])
                for kc in range(8):
                    rap, rkey = rhs_fn(kc, tb)
                    p.op("tensor", lambda e, ic=ic, kc=kc, rap=rap: e.matmul(
                        pu[:], wu[:, kc * INNER + ic * 128: kc * INNER + (ic + 1) * 128],
                        rap, start=(kc == 0), stop=(kc == 7)),
                         reads=["wu", rkey], writes=["pu"])
                p.op("scalar", lambda e: e.activation(tb_bf[:], pg[:], AF.Silu),
                     reads=["pg"], writes=["tb_bf"])
                p.op("vector", lambda e, ic=ic: e.tensor_tensor(
                    actb[:, ic * 512:(ic + 1) * 512], tb_bf[:], pu[:], ALU.mult),
                     reads=["tb_bf", "pu"], writes=["actb" + str(ic % 4)])
            for dmc in range(8):
                for ic in range(16):
                    p.op("tensor", lambda e, dmc=dmc, ic=ic: e.matmul(
                        pd[:], wd[:, ic * D + dmc * 128: ic * D + (dmc + 1) * 128],
                        actb[:, ic * 512:(ic + 1) * 512],
                        start=(ic == 0), stop=(ic == 15)),
                         reads=["wd", "actb0", "actb1", "actb2", "actb3"],
                         writes=["pd"])
                ysl = y_sb[:, dmc * T + tb * 512: dmc * T + (tb + 1) * 512]
                if first:
                    p.op("vector", lambda e, ysl=ysl: e.tensor_copy(ysl, pd[:]),
                         reads=["pd"], writes=[f"y{dmc}_{tb}"])
                else:
                    p.op("vector", lambda e, ysl=ysl: e.scalar_tensor_tensor(
                        ysl, pd[:], 1.0, ysl, ALU.mult, ALU.add),
                         reads=["pd", f"y{dmc}_{tb}"], writes=[f"y{dmc}_{tb}"])

    p.dma("gpsimd", wg[:], shg_d[:], writes=["wg"])
    p.dma("gpsimd", wu[:], shu_d[:], writes=["wu"])
    p.dma("gpsimd", wd[:], shd_d[:], writes=["wd"])
    ffn_branch(True, lambda kc, tb: (xnf[:, kc * T + tb * 512: kc * T + (tb + 1) * 512], "xnf"))

    for ex in range(8):
        p.dma("gpsimd", wg[:], rtg_d[ex], writes=["wg"])
        p.dma("gpsimd", wu[:], rtu_d[ex], writes=["wu"])
        p.dma("gpsimd", wd[:], rtd_d[ex], writes=["wd"])
        for tb in range(4):
            p.op("tensor", lambda e, ex=ex, tb=tb: e.matmul(
                pmsk[:], sel8[:, ex * 128:(ex + 1) * 128],
                mhm[:, tb * 512:(tb + 1) * 512], start=True, stop=True),
                 reads=["c_sel8", "mhm"], writes=["pmsk"])
            p.op("vector", lambda e, tb=tb: e.tensor_copy(
                msk[:, tb * 512:(tb + 1) * 512], pmsk[:]),
                 reads=["pmsk"], writes=["msk"])
        def pre_tb(tb):
            for kc in range(8):
                p.op("vector", lambda e, kc=kc, tb=tb: e.tensor_tensor(
                    xets[:, kc * 512:(kc + 1) * 512],
                    xnf[:, kc * T + tb * 512: kc * T + (tb + 1) * 512],
                    msk[:, tb * 512:(tb + 1) * 512], ALU.mult),
                     reads=["xnf", "msk"], writes=["xets"])
        ffn_branch(False, lambda kc, tb: (xets[:, kc * 512:(kc + 1) * 512], "xets"),
                   pre_tb)

    for dmc in range(8):
        p.op("scalar", lambda e, dmc=dmc: e.activation(
            y_sb[:, dmc * T:(dmc + 1) * T], y_sb[:, dmc * T:(dmc + 1) * T], AF.Silu),
             reads=[f"y{dmc}_0", f"y{dmc}_1", f"y{dmc}_2", f"y{dmc}_3"],
             writes=[f"y{dmc}_0", f"y{dmc}_1", f"y{dmc}_2", f"y{dmc}_3"])
        p.dma("sync", g_o[:].rearrange("p (a b) -> p a b", b=T)[:, dmc, :],
              y_sb[:, dmc * T:(dmc + 1) * T],
              reads=[f"y{dmc}_0", f"y{dmc}_1", f"y{dmc}_2", f"y{dmc}_3"])
    with nc.Block() as block:
        p.finalize(block)
    nc.finalize()
    return nc


def host_M(xn_cores, inputs):
    gw = prep_lhsT(np.asarray(inputs["L0_ffn_gate_w"], np.float32))
    shg = prep_lhsT(np.asarray(inputs["L0_ffn_sh_gate"], np.float32))
    shu = prep_lhsT(np.asarray(inputs["L0_ffn_sh_up"], np.float32))
    shd = prep_lhsT(np.asarray(inputs["L0_ffn_sh_down"], np.float32))
    rtg = np.stack([prep_lhsT(np.asarray(inputs["L0_ffn_rt_gate"][e], np.float32)) for e in range(8)])
    rtu = np.stack([prep_lhsT(np.asarray(inputs["L0_ffn_rt_up"][e], np.float32)) for e in range(8)])
    rtd = np.stack([prep_lhsT(np.asarray(inputs["L0_ffn_rt_down"][e], np.float32)) for e in range(8)])
    return [{"xn": xn_cores[c], "gw": gw, "shg": shg, "shu": shu, "shd": shd,
             "rtg": rtg, "rtu": rtu, "rtd": rtd} for c in range(N_CORES)]


def build_C():
    nc = bass.Bass("TRN2", target_bir_lowering=False)
    es = ExitStack()
    consts = make_consts(nc)

    yn_d = nc.dram_tensor("yn", [128, 16 * T], BF16, kind="ExternalInput").ap()
    sz_d = nc.dram_tensor("sz", [128, 16 * T], BF16, kind="ExternalInput").ap()
    g_d = nc.dram_tensor("g", [128, 8 * T], BF16, kind="ExternalInput").ap()
    x_d = nc.dram_tensor("x", [T, D], F32, kind="ExternalInput").ap()
    wo_d = nc.dram_tensor("woT", [128, 16 * D], F32, kind="ExternalInput").ap()
    x1_o = nc.dram_tensor("x1_o", [T, D], F32, kind="ExternalOutput").ap()

    sb = nc.alloc_sbuf_tensor
    ps = nc.alloc_psum_tensor
    p = Prog(nc, es)
    idb = load_const(p, nc, consts, "id_bf", [128, 128], BF16)

    gf = sb("gf", [128, 8 * T], BF16)
    wo = sb("wo", [128, 16 * D], BF16)
    ytb = sb("ytb", [128, 16 * 512], BF16)
    ztb = sb("ztb", [128, 16 * 512], BF16)
    ob = sb("ob", [128, 16 * 512], BF16)
    sg = sb("sg", [128, 512], BF16)
    xin = sb("xin", [128, 4 * D], F32)
    pd = ps("pd", [128, 512], F32)
    ptr = ps("ptr", [128, 128], BF16)

    p.dma("sync", gf[:], g_d[:], writes=["gf"])
    p.dma("gpsimd", wo[:], wo_d[:], writes=["wo"])
    for tb in range(4):
        p.dma("sync", ytb[:].rearrange("p (a b) -> p a b", b=512),
              yn_d[:].rearrange("p (a b) -> p a b", b=T)[:, :, tb * 512:(tb + 1) * 512],
              writes=["ytb"])
        p.dma("sync", ztb[:].rearrange("p (a b) -> p a b", b=512),
              sz_d[:].rearrange("p (a b) -> p a b", b=T)[:, :, tb * 512:(tb + 1) * 512],
              writes=["ztb"])
        p.dma("sync", xin[:].rearrange("p (a b) -> p a b", b=D),
              x_d[:].rearrange("(s p) d -> p s d", p=128)[:, tb * 4:(tb + 1) * 4, :],
              writes=["xin"])
        for ic in range(16):
            p.op("vector", lambda e, ic=ic: e.tensor_tensor(
                ob[:, ic * 512:(ic + 1) * 512], ytb[:, ic * 512:(ic + 1) * 512],
                ztb[:, ic * 512:(ic + 1) * 512], ALU.mult),
                 reads=["ytb", "ztb"], writes=["ob" + str(ic % 4)])
        for dmc in range(8):
            for ic in range(16):
                p.op("tensor", lambda e, dmc=dmc, ic=ic: e.matmul(
                    pd[:], wo[:, ic * D + dmc * 128: ic * D + (dmc + 1) * 128],
                    ob[:, ic * 512:(ic + 1) * 512],
                    start=(ic == 0), stop=(ic == 15)),
                     reads=["wo", "ob0", "ob1", "ob2", "ob3"], writes=["pd"])
            p.op("vector", lambda e, dmc=dmc, tb=tb: e.tensor_tensor(
                sg[:], pd[:], gf[:, dmc * T + tb * 512: dmc * T + (tb + 1) * 512],
                ALU.mult), reads=["pd", "gf"], writes=["sg"])
            for st in range(4):
                p.op("tensor", lambda e, st=st: e.transpose(
                    ptr[:], sg[:, st * 128:(st + 1) * 128], idb[:]),
                     reads=["sg", "c_id_bf"], writes=["ptr"])
                p.op("vector", lambda e, st=st, dmc=dmc: e.tensor_tensor(
                    xin[:, st * D + dmc * 128: st * D + (dmc + 1) * 128],
                    xin[:, st * D + dmc * 128: st * D + (dmc + 1) * 128],
                    ptr[:], ALU.add),
                     reads=["ptr", "xin"], writes=["xin"])
        p.dma("sync", x1_o[:].rearrange("(s p) d -> p s d", p=128)[:, tb * 4:(tb + 1) * 4, :],
              xin[:].rearrange("p (a b) -> p a b", b=D), reads=["xin"])
    with nc.Block() as block:
        p.finalize(block)
    nc.finalize()
    return nc


def host_C(yn_cores, sz_cores, g_cores, inputs):
    woT = prep_lhsT(np.ascontiguousarray(np.asarray(inputs["L0_attn_Wout"], np.float32).T))
    x = np.asarray(inputs["L0_attn_input_tensor"])
    maps = []
    for c in range(N_CORES):
        b_, k = c // 4, c % 4
        maps.append({"yn": yn_cores[c], "sz": sz_cores[c], "g": g_cores[c],
                     "x": np.ascontiguousarray(x[b_, k * T:(k + 1) * T]), "woT": woT})
    return maps


_prog_cache = {}

def _get(name, builder):
    if name not in _prog_cache:
        _prog_cache[name] = builder()
    return _prog_cache[name]


def _run(nc, maps):
    res = run_bass_kernel_spmd(nc, maps, core_ids=list(range(N_CORES)))
    return res.results


def run_layer0(inputs):
    rA = _run(_get("A", build_A), host_A(inputs))
    xc_g = np.zeros((B, S, INNER), BF)
    b_g = np.zeros((B, S, 512), BF)
    c_g = np.zeros((B, S, 512), BF)
    dt_g = np.zeros((B, S, 8), np.float32)
    xn_cores, xlo_cores, sz_cores = [], [], []
    for c in range(N_CORES):
        b_, k = c // 4, c % 4
        rows = slice(k * T, (k + 1) * T)
        r = rA[c]
        xc_g[b_, rows] = fm_to(np.asarray(r["xc_o"]), T, INNER)
        b_g[b_, rows] = np.asarray(r["b_o"]).reshape(64, 8, T).transpose(2, 1, 0).reshape(T, 512)
        c_g[b_, rows] = np.asarray(r["c_o"]).reshape(64, 8, T).transpose(2, 1, 0).reshape(T, 512)
        dt_g[b_, rows] = np.asarray(r["dt_o"]).T
        xn_cores.append(np.asarray(r["xn_o"]))
        xlo_cores.append(np.asarray(r["xlo_o"]))
        sz_cores.append(np.asarray(r["sz_o"]))

    rB = _run(_get("B", build_B), host_B(xc_g, b_g, c_g, dt_g, inputs))
    yn_g = np.zeros((B, S, INNER), BF)
    for c in range(N_CORES):
        r = np.asarray(rB[c]["y_o"])
        for u in range(2):
            bb, hh = divmod(2 * c + u, HEADS)
            yn_g[bb, :, hh * HDIM:(hh + 1) * HDIM] =                 r[u].reshape(128, 64, HDIM).transpose(1, 0, 2).reshape(S, HDIM)

    mapsM = host_M(xn_cores, inputs)
    for c in range(N_CORES):
        mapsM[c]["xlo"] = xlo_cores[c]
    rM = _run(_get("M", build_M), mapsM)
    g_cores = [np.asarray(rM[c]["g_o"]) for c in range(N_CORES)]

    yn_cores = []
    for c in range(N_CORES):
        b_, k = c // 4, c % 4
        yn_cores.append(fm_from(yn_g[b_, k * T:(k + 1) * T]))
    rC = _run(_get("C", build_C), host_C(yn_cores, sz_cores, g_cores, inputs))
    x1 = np.zeros((B, S, D), np.float32)
    for c in range(N_CORES):
        b_, k = c // 4, c % 4
        x1[b_, k * T:(k + 1) * T] = np.asarray(rC[c]["x1_o"])
    return {"x1": x1}


def build_D():
    nc = bass.Bass("TRN2", target_bir_lowering=False)
    es = ExitStack()
    consts = make_consts(nc)

    x1_d = nc.dram_tensor("x1", [T, D], F32, kind="ExternalInput").ap()
    wq_d = nc.dram_tensor("wq", [128, 8 * D], F32, kind="ExternalInput").ap()
    wk_d = nc.dram_tensor("wk", [128, 8 * D], F32, kind="ExternalInput").ap()
    wv_d = nc.dram_tensor("wv", [128, 8 * D], F32, kind="ExternalInput").ap()
    q_o = nc.dram_tensor("q_o", [128, 8 * T], BF16, kind="ExternalOutput").ap()
    k_o = nc.dram_tensor("k_o", [128, 8 * T], BF16, kind="ExternalOutput").ap()
    v_o = nc.dram_tensor("v_o", [128, 16 * D], BF16, kind="ExternalOutput").ap()
    xn1_o = nc.dram_tensor("xn1_o", [T, D], F32, kind="ExternalOutput").ap()

    sb = nc.alloc_sbuf_tensor
    ps = nc.alloc_psum_tensor
    p = Prog(nc, es)
    idb = load_const(p, nc, consts, "id_bf", [128, 128], BF16)

    xnf = sb("xnf", [128, 8 * T], BF16)
    wql = sb("wql", [128, 8 * D], BF16)
    wkl = sb("wkl", [128, 8 * D], BF16)
    wvl = sb("wvl", [128, 8 * D], BF16)
    qf = sb("qf", [128, 8 * T], BF16)
    kf = sb("kf", [128, 8 * T], BF16)
    vt = sb("vt", [128, 16 * D], BF16)
    xt = sb("xt", [128, D], F32)
    xn32 = sb("xn32", [128, D], F32)
    xnb = sb("xnb", [128, D], BF16)
    mcol = sb("mcol", [128, 1], F32)
    vcol = sb("vcol", [128, 4], F32)
    sq = sb("sq", [128, D], F32)
    pm = ps("pm", [128, 512], F32)
    ptr = ps("ptr", [128, 128], BF16)

    p.dma("gpsimd", wql[:], wq_d[:], writes=["wql"])
    p.dma("gpsimd", wkl[:], wk_d[:], writes=["wkl"])
    p.dma("gpsimd", wvl[:], wv_d[:], writes=["wvl"])

    for t in range(16):
        p.dma("sync", xt[:], x1_d[t * 128:(t + 1) * 128, :], writes=["xt"])
        p.op("vector", lambda e: e.reduce_sum(mcol[:], xt[:], axis=AX.X),
             reads=["xt"], writes=["mcol"])
        p.op("scalar", lambda e: e.activation(sq[:], xt[:], AF.Square),
             reads=["xt"], writes=["sq"])
        p.op("vector", lambda e: e.reduce_sum(vcol[:, 0:1], sq[:], axis=AX.X),
             reads=["sq"], writes=["vcol"])
        p.op("vector", lambda e: e.tensor_scalar(mcol[:], mcol[:], 1.0 / D, None, ALU.mult),
             reads=["mcol"], writes=["mcol"])
        p.op("scalar", lambda e: e.activation(vcol[:, 1:2], mcol[:], AF.Square),
             reads=["mcol"], writes=["vcol2"])
        p.op("vector", lambda e: e.scalar_tensor_tensor(
            vcol[:, 2:3], vcol[:, 0:1], 1.0 / D, vcol[:, 1:2], ALU.mult, ALU.subtract),
             reads=["vcol", "vcol2"], writes=["vcol3"])
        p.op("vector", lambda e: e.tensor_scalar(vcol[:, 2:3], vcol[:, 2:3], EPS, None, ALU.add),
             reads=["vcol3"], writes=["vcol3"])
        p.op("scalar", lambda e: e.activation(vcol[:, 3:4], vcol[:, 2:3], AF.Ln),
             reads=["vcol3"], writes=["vcol4"])
        p.op("scalar", lambda e: e.activation(vcol[:, 3:4], vcol[:, 3:4], AF.Exp, scale=-0.5),
             reads=["vcol4"], writes=["vcol4"])
        p.op("vector", lambda e: e.tensor_scalar(
            xn32[:], xt[:], mcol[:], vcol[:, 3:4], ALU.subtract, ALU.mult),
             reads=["xt", "mcol", "vcol4"], writes=["xn32"])
        p.dma("sync", xn1_o[t * 128:(t + 1) * 128, :], xn32[:], reads=["xn32"])
        p.op("vector", lambda e: e.tensor_copy(xnb[:], xn32[:]),
             reads=["xn32"], writes=["xnb"])
        for kc in range(8):
            p.op("tensor", lambda e, kc=kc: e.transpose(
                ptr[:], xnb[:, kc * 128:(kc + 1) * 128], idb[:]),
                 reads=["xnb", "c_id_bf"], writes=["ptr"])
            p.op("vector", lambda e, kc=kc, t=t: e.tensor_copy(
                xnf[:, kc * T + t * 128: kc * T + (t + 1) * 128], ptr[:]),
                 reads=["ptr"], writes=["xnf"])
    for (wl, wkey, dst, dkey, scale) in ((wql, "wql", qf, "qf", 0.125),
                                         (wkl, "wkl", kf, "kf", None)):
        for ic in range(8):
            for tb in range(4):
                for kc in range(8):
                    p.op("tensor", lambda e, ic=ic, tb=tb, kc=kc, wl=wl: e.matmul(
                        pm[:], wl[:, kc * D + ic * 128: kc * D + (ic + 1) * 128],
                        xnf[:, kc * T + tb * 512: kc * T + (tb + 1) * 512],
                        start=(kc == 0), stop=(kc == 7)),
                         reads=[wkey, "xnf"], writes=["pm"])
                dsl = dst[:, ic * T + tb * 512: ic * T + (tb + 1) * 512]
                if scale is None:
                    p.op("vector", lambda e, dsl=dsl: e.tensor_copy(dsl, pm[:]),
                         reads=["pm"], writes=[dkey])
                else:
                    p.op("vector", lambda e, dsl=dsl, scale=scale: e.tensor_scalar(
                        dsl, pm[:], scale, None, ALU.mult),
                         reads=["pm"], writes=[dkey])
    for t in range(16):
        for vb in range(2):
            for kc in range(8):
                p.op("tensor", lambda e, t=t, vb=vb, kc=kc: e.matmul(
                    pm[:], xnf[:, kc * T + t * 128: kc * T + (t + 1) * 128],
                    wvl[:, kc * D + vb * 512: kc * D + (vb + 1) * 512],
                    start=(kc == 0), stop=(kc == 7)),
                     reads=["xnf", "wvl"], writes=["pm"])
            p.op("vector", lambda e, t=t, vb=vb: e.tensor_copy(
                vt[:, t * D + vb * 512: t * D + (vb + 1) * 512], pm[:]),
                 reads=["pm"], writes=["vt"])
    p.dma("sync", q_o[:], qf[:], reads=["qf"])
    p.dma("sync", k_o[:], kf[:], reads=["kf"])
    p.dma("sync", v_o[:], vt[:], reads=["vt"])
    with nc.Block() as block:
        p.finalize(block)
    nc.finalize()
    return nc


def host_D(x1, inputs):
    wq = prep_lhsT(np.asarray(inputs["L1_attn_Wq"], np.float32))
    wk = prep_lhsT(np.asarray(inputs["L1_attn_Wk"], np.float32))
    wv = prep_lhsT(np.asarray(inputs["L1_attn_Wv"], np.float32))
    maps = []
    for c in range(N_CORES):
        b_, k = c // 4, c % 4
        maps.append({"x1": np.ascontiguousarray(x1[b_, k * T:(k + 1) * T]),
                     "wq": wq, "wk": wk, "wv": wv})
    return maps


_MIDS = (0.25, 0.75, 1.25, 1.75, 2.5, 3.5, 5.0)
_DELT = (0.5, 0.5, 0.5, 0.5, 1.0, 1.0, 2.0)

def emit_fp4(p, src, dst, bufs, P, n, key, src_key=None):
    nb = n // 16
    sc, th, ax, sg, cb1, cb2, accv, accg = (bufs[x] for x in
        ("sc", "th", "ax", "sg", "cb1", "cb2", "accv", "accg"))
    k_src, k_dst = (src_key or key + "_src"), key + "_dst"
    ksc, kax, ksg = key + "_sc", key + "_ax", key + "_sg"
    kc1, kc2, kav, kag = key + "_c1", key + "_c2", key + "_av", key + "_ag"
    def v3(ap):
        return ap.rearrange("p (a b) -> p a b", b=16)
    p.op("vector", lambda e: e.reduce_max(sc[0:P, 0:nb], v3(src), axis=AX.X,
         apply_absolute_value=True), reads=[k_src], writes=[ksc])
    p.op("vector", lambda e: e.tensor_scalar(sc[0:P, 0:nb], sc[0:P, 0:nb],
         1.0 / 6.0, 1e-12, ALU.mult, ALU.max), reads=[ksc], writes=[ksc])
    p.op("scalar", lambda e: e.activation(ax[0:P, 0:n], src, AF.Abs),
         reads=[k_src], writes=[kax])
    p.op("scalar", lambda e: e.activation(sg[0:P, 0:n], src, AF.Sign),
         reads=[k_src], writes=[ksg])
    for i in range(7):
        p.op("vector", lambda e, i=i: e.tensor_scalar(
            th[0:P, i * nb:(i + 1) * nb], sc[0:P, 0:nb], _MIDS[i], None, ALU.mult),
             reads=[ksc], writes=[key + "_th" + str(i)])
    def cmp(i, dstbuf, dkey):
        p.op("vector", lambda e, i=i, dstbuf=dstbuf: e.tensor_tensor(
            v3(dstbuf[0:P, 0:n]), v3(ax[0:P, 0:n]),
            th[0:P, i * nb:(i + 1) * nb].unsqueeze(2).broadcast_to([P, nb, 16]),
            ALU.is_gt), reads=[kax, key + "_th" + str(i)], writes=[dkey])
    for j in range(3):
        cmp(2 * j, cb1, kc1)
        cmp(2 * j + 1, cb2, kc2)
        acc, kacc = (accv, kav) if j == 2 else (accg, kag)
        if j == 0:
            p.op("gpsimd", lambda e: e.tensor_add(accg[0:P, 0:n], cb1[0:P, 0:n],
                 cb2[0:P, 0:n]), reads=[kc1, kc2], writes=[kag])
        elif j == 1:
            p.op("gpsimd", lambda e: e.tensor_add(accg[0:P, 0:n], accg[0:P, 0:n],
                 cb1[0:P, 0:n]), reads=[kag, kc1], writes=[kag])
            p.op("gpsimd", lambda e: e.tensor_add(accg[0:P, 0:n], accg[0:P, 0:n],
                 cb2[0:P, 0:n]), reads=[kag, kc2], writes=[kag])
        else:
            p.op("gpsimd", lambda e: e.tensor_scalar_mul(accg[0:P, 0:n],
                 accg[0:P, 0:n], 0.5), reads=[kag], writes=[kag])
            p.op("vector", lambda e: e.tensor_add(accv[0:P, 0:n], cb1[0:P, 0:n],
                 cb2[0:P, 0:n]), reads=[kc1, kc2], writes=[kav])
    cmp(6, cb1, kc1)
    p.op("vector", lambda e: e.scalar_tensor_tensor(accv[0:P, 0:n], cb1[0:P, 0:n],
         2.0, accv[0:P, 0:n], ALU.mult, ALU.add), reads=[kc1, kav], writes=[kav])
    p.op("vector", lambda e: e.tensor_add(accv[0:P, 0:n], accv[0:P, 0:n],
         accg[0:P, 0:n]), reads=[kav, kag], writes=[kav])
    p.op("gpsimd", lambda e: e.tensor_mul(accv[0:P, 0:n], accv[0:P, 0:n],
         sg[0:P, 0:n]), reads=[kav, ksg], writes=[kav])
    p.op("vector", lambda e: e.tensor_tensor(
        v3(dst), v3(accv[0:P, 0:n]),
        sc[0:P, 0:nb].unsqueeze(2).broadcast_to([P, nb, 16]), ALU.mult),
         reads=[kav, ksc], writes=[k_dst])


def mk_fp4_bufs(nc, n, tag=""):
    sb = nc.alloc_sbuf_tensor
    return {"sc": sb("q_sc" + tag, [128, n // 16], F32),
            "th": sb("q_th" + tag, [128, 7 * (n // 16)], F32),
            "ax": sb("q_ax" + tag, [128, n], F32),
            "sg": sb("q_sg" + tag, [128, n], BF16),
            "cb1": sb("q_c1" + tag, [128, n], BF16),
            "cb2": sb("q_c2" + tag, [128, n], BF16),
            "accv": sb("q_av" + tag, [128, n], BF16),
            "accg": sb("q_ag" + tag, [128, n], BF16)}


def build_W():
    nc = bass.Bass("TRN2", target_bir_lowering=False)
    es = ExitStack()
    wg_d = nc.dram_tensor("wgs", [352, 1024], F32, kind="ExternalInput").ap()
    wu_d = nc.dram_tensor("wus", [352, 1024], F32, kind="ExternalInput").ap()
    wd_d = nc.dram_tensor("wds", [128, FI], F32, kind="ExternalInput").ap()
    wgq_o = nc.dram_tensor("wgq_o", [352, 1024], BF16, kind="ExternalOutput").ap()
    wuq_o = nc.dram_tensor("wuq_o", [352, 1024], BF16, kind="ExternalOutput").ap()
    wdq_o = nc.dram_tensor("wdq_o", [128, FI], BF16, kind="ExternalOutput").ap()
    sb = nc.alloc_sbuf_tensor
    p = Prog(nc, es)
    bufs = mk_fp4_bufs(nc, FI)
    srcb = sb("srcb", [128, FI], F32)
    dstb = sb("dstb", [128, FI], BF16)
    ji = 0
    for (src_d, dst_d, rows, n) in ((wg_d, wgq_o, 352, 1024), (wu_d, wuq_o, 352, 1024),
                                    (wd_d, wdq_o, 128, FI)):
        r0 = 0
        while r0 < rows:
            P = min(128, rows - r0)
            p.dma("sync", srcb[0:P, 0:n], src_d[r0:r0 + P, :], writes=["q0_src"])
            emit_fp4(p, srcb[0:P, 0:n], dstb[0:P, 0:n], bufs, P, n, "q0")
            p.dma("sync", dst_d[r0:r0 + P, :], dstb[0:P, 0:n], reads=["q0_dst"])
            r0 += P
            ji += 1
    with nc.Block() as block:
        p.finalize(block)
    nc.finalize()
    return nc


def host_W(inputs):
    wg = np.asarray(inputs["L1_ffn_w_gate"], np.float32)
    wu = np.asarray(inputs["L1_ffn_w_up"], np.float32)
    wd = np.asarray(inputs["L1_ffn_w_down"], np.float32)
    maps = []
    for c in range(N_CORES):
        maps.append({"wgs": np.ascontiguousarray(wg[c * 352:(c + 1) * 352]),
                     "wus": np.ascontiguousarray(wu[c * 352:(c + 1) * 352]),
                     "wds": np.ascontiguousarray(wd[c * 128:(c + 1) * 128])})
    return maps


def build_E():
    nc = bass.Bass("TRN2", target_bir_lowering=False)
    es = ExitStack()
    consts = make_consts(nc)
    NKT = 64

    qu_d = [nc.dram_tensor(f"qu{u}", [64, S], BF16, kind="ExternalInput").ap() for u in range(4)]
    ku_d = [nc.dram_tensor(f"ku{u}", [64, S], BF16, kind="ExternalInput").ap() for u in range(4)]
    vu_d = [nc.dram_tensor(f"vu{u}", [128, NKT * AHD], BF16, kind="ExternalInput").ap() for u in range(4)]
    o_o = nc.dram_tensor("o_o", [4, 128, NKT * AHD], BF16, kind="ExternalOutput").ap()

    sb = nc.alloc_sbuf_tensor
    ps = nc.alloc_psum_tensor
    p = Prog(nc, es)
    bm = load_const(p, nc, consts, "bigmask", [128, 896], F32)

    qu = [sb(f"qus{i}", [64, S], BF16) for i in range(2)]
    ku = [sb(f"kus{i}", [64, S], BF16) for i in range(2)]
    vsb = [sb(f"vsbs{i}", [128, NKT * 65], BF16) for i in range(2)]
    pbf = [sb(f"pbf{i}", [128, 512], BF16) for i in range(2)]
    ob = [sb(f"ob{i}", [128, NKT * AHD], BF16) for i in range(2)]
    rc = sb("rc", [128, 1], F32)
    psl = [ps(f"ps{i}", [128, 512], F32) for i in range(3)]
    pacc = [ps(f"pa{i}", [128, 65], F32) for i in range(4)]

    p.op("vector", lambda e: e.memset(vsb[0][:], 1.0), reads=[], writes=["vsb0"])
    p.op("vector", lambda e: e.memset(vsb[1][:], 1.0), reads=[], writes=["vsb1"])
    for u in range(4):
        w = u % 2
        quw, kuw, vsw, obw = qu[w], ku[w], vsb[w], ob[w]
        kq, kk, kv, ko = f"qu{w}", f"ku{w}", f"vsb{w}", f"ob{w}"
        p.dma("sync", quw[:], qu_d[u][:], writes=[kq])
        p.dma("sync", kuw[:], ku_d[u][:], writes=[kk])
        p.dma("gpsimd", vsw[:].rearrange("p (a b) -> p a b", b=65)[:, :, 0:64],
              vu_d[u][:].rearrange("p (a b) -> p a b", b=AHD), writes=[kv])

        def emit_av(kt, nkt, vsw=vsw, kv=kv):
            pb = pbf[kt % 2]
            for qs in range(4):
                p.op("tensor", lambda e, qs=qs, kt=kt, nkt=nkt, pb=pb, vsw=vsw: e.matmul(
                    pacc[qs][:], pb[:, qs * 128:(qs + 1) * 128],
                    vsw[:, kt * 65:(kt + 1) * 65],
                    start=(kt == 0), stop=(kt == nkt - 1)),
                     reads=[f"pbf{kt % 2}", kv], writes=[f"pa{qs}"])

        for qb in range(16):
            nkt = 4 * qb + 4
            for kt in range(nkt):
                sp = psl[kt % 3]
                skey = f"ps{kt % 3}"
                p.op("tensor", lambda e, kt=kt, qb=qb, sp=sp, kuw=kuw, quw=quw: e.matmul(
                    sp[:], kuw[:, kt * 128:(kt + 1) * 128],
                    quw[:, qb * 512:(qb + 1) * 512], start=True, stop=True),
                     reads=[kk, kq], writes=[skey])
                a = kt - 4 * qb
                if a >= 0:
                    off = 384 - 128 * a
                    p.op("vector", lambda e, sp=sp, off=off: e.tensor_tensor(
                        sp[:], sp[:], bm[:, off:off + 512], ALU.add),
                         reads=[skey, "c_bigmask"], writes=[skey])
                p.op("scalar", lambda e, sp=sp, kt=kt: e.activation(
                    pbf[kt % 2][:], sp[:], AF.Exp),
                     reads=[skey], writes=[f"pbf{kt % 2}"])
                if kt >= 1:
                    emit_av(kt - 1, nkt)
            emit_av(nkt - 1, nkt)
            for qs in range(4):
                p.op("vector", lambda e, qs=qs: e.reciprocal(rc[:], pacc[qs][:, 64:65]),
                     reads=[f"pa{qs}"], writes=["rc"])
                ti = qb * 4 + qs
                p.op("vector", lambda e, qs=qs, ti=ti, obw=obw: e.tensor_scalar(
                    obw[:, ti * AHD:(ti + 1) * AHD], pacc[qs][:, 0:64], rc[:],
                    None, ALU.mult), reads=[f"pa{qs}", "rc"], writes=[ko])
        p.dma("sync", o_o[u], obw[:], reads=[ko])
    with nc.Block() as block:
        p.finalize(block)
    nc.finalize()
    return nc


def host_E(q_cores, k_cores, v_cores):
    qg = np.zeros((B, 2 * S, 512), BF)
    qf = np.zeros((B, D, S), BF)
    kf = np.zeros((B, D, S), BF)
    vt = np.zeros((B, S, D), BF)
    for c in range(N_CORES):
        b_, k = c // 4, c % 4
        rows = slice(k * T, (k + 1) * T)
        qf[b_, :, rows] = q_cores[c].reshape(128, 8, T).transpose(1, 0, 2).reshape(D, T)
        kf[b_, :, rows] = k_cores[c].reshape(128, 8, T).transpose(1, 0, 2).reshape(D, T)
        vt[b_, rows, :] = v_cores[c].reshape(128, 16, D).transpose(1, 0, 2).reshape(T, D)
    maps = []
    for c in range(N_CORES):
        m = {}
        for u in range(4):
            bb, hh = divmod(4 * c + u, AH)
            m[f"qu{u}"] = np.ascontiguousarray(qf[bb, hh * AHD:(hh + 1) * AHD, :])
            m[f"ku{u}"] = np.ascontiguousarray(kf[bb, hh * AHD:(hh + 1) * AHD, :])
            m[f"vu{u}"] = np.ascontiguousarray(
                vt[bb, :, hh * AHD:(hh + 1) * AHD].reshape(64, 128, AHD)
                .transpose(1, 0, 2).reshape(128, -1))
        maps.append(m)
    return maps


def build_F1a():
    nc = bass.Bass("TRN2", target_bir_lowering=False)
    es = ExitStack()
    consts = make_consts(nc)
    IB = (512, 512, 512, 512, 512, 256)

    xn1_d = nc.dram_tensor("xn1", [T, D], F32, kind="ExternalInput").ap()
    wg_d = nc.dram_tensor("wgqT", [128, 8 * FI], BF16, kind="ExternalInput").ap()
    wu_d = nc.dram_tensor("wuqT", [128, 8 * FI], BF16, kind="ExternalInput").ap()
    hq_o = nc.dram_tensor("hq_o", [128, 16 * FI], BF16, kind="ExternalOutput").ap()

    sb = nc.alloc_sbuf_tensor
    ps = nc.alloc_psum_tensor
    p = Prog(nc, es)
    idb = load_const(p, nc, consts, "id_bf", [128, 128], BF16)

    wgq = sb("wgq", [128, 8 * FI], BF16)
    wuq = sb("wuq", [128, 8 * FI], BF16)
    bufs_h = mk_fp4_bufs(nc, FI)
    bufs_x = mk_fp4_bufs(nc, D, tag="x")
    xts = sb("xts", [128, D], F32)
    xq = sb("xq", [128, D], BF16)
    xqf = [sb(f"xqf{i}", [128, D], BF16) for i in range(2)]
    hf = [sb(f"hf{i}", [128, FI], F32) for i in range(2)]
    hq = sb("hq", [128, FI], BF16)
    sil = sb("sil", [128, 512], BF16)
    pg = ps("pg", [128, 512], F32)
    pu = ps("pu", [128, 512], F32)
    ptr = [ps(f"ptr{i}", [128, 128], BF16) for i in range(2)]

    p.dma("gpsimd", wgq[:], wg_d[:], writes=["wgq"])
    p.dma("gpsimd", wuq[:], wu_d[:], writes=["wuq"])

    def emit_hq(t):
        hfw, khf = hf[t % 2], f"hf{t % 2}"
        emit_fp4(p, hfw[:], hq[:], bufs_h, 128, FI, "q1", src_key=khf)
        p.dma("sync", hq_o[:, t * FI:(t + 1) * FI], hq[:], reads=["q1_dst"])

    for t in range(16):
        hfw, khf = hf[t % 2], f"hf{t % 2}"
        xqfw, kxf = xqf[t % 2], f"xqf{t % 2}"
        p.dma("sync", xts[:], xn1_d[t * 128:(t + 1) * 128, :], writes=["q0_src"])
        emit_fp4(p, xts[:], xq[:], bufs_x, 128, D, "q0")
        for kc in range(8):
            pt, kpt = ptr[kc % 2], f"ptr{kc % 2}"
            p.op("tensor", lambda e, kc=kc, pt=pt: e.transpose(
                pt[:], xq[:, kc * 128:(kc + 1) * 128], idb[:]),
                 reads=["q0_dst", "c_id_bf"], writes=[kpt])
            p.op("vector", lambda e, kc=kc, pt=pt, xqfw=xqfw: e.tensor_copy(
                xqfw[:, kc * 128:(kc + 1) * 128], pt[:]),
                 reads=[kpt], writes=[kxf])
        if t >= 1:
            emit_hq(t - 1)
        off = 0
        for ib, w in enumerate(IB):
            for kc in range(8):
                p.op("tensor", lambda e, kc=kc, off=off, w=w, xqfw=xqfw: e.matmul(
                    pg[:, 0:w], xqfw[:, kc * 128:(kc + 1) * 128],
                    wgq[:, kc * FI + off: kc * FI + off + w],
                    start=(kc == 0), stop=(kc == 7)),
                     reads=[kxf, "wgq"], writes=["pg"])
            for kc in range(8):
                p.op("tensor", lambda e, kc=kc, off=off, w=w, xqfw=xqfw: e.matmul(
                    pu[:, 0:w], xqfw[:, kc * 128:(kc + 1) * 128],
                    wuq[:, kc * FI + off: kc * FI + off + w],
                    start=(kc == 0), stop=(kc == 7)),
                     reads=[kxf, "wuq"], writes=["pu"])
            p.op("scalar", lambda e, w=w: e.activation(sil[:, 0:w], pg[:, 0:w], AF.Silu),
                 reads=["pg"], writes=["sil"])
            p.op("vector", lambda e, off=off, w=w, hfw=hfw: e.tensor_tensor(
                hfw[:, off:off + w], sil[:, 0:w], pu[:, 0:w], ALU.mult),
                 reads=["sil", "pu"], writes=[khf])
            off += w
    emit_hq(15)
    with nc.Block() as block:
        p.finalize(block)
    nc.finalize()
    return nc


def build_F1b():
    nc = bass.Bass("TRN2", target_bir_lowering=False)
    es = ExitStack()
    NIC = 22
    hq_d = nc.dram_tensor("hqf", [128, NIC * T], BF16, kind="ExternalInput").ap()
    wd_d = nc.dram_tensor("wdqT", [128, NIC * D], BF16, kind="ExternalInput").ap()
    g1_o = nc.dram_tensor("g1_o", [128, 8 * T], BF16, kind="ExternalOutput").ap()
    sb = nc.alloc_sbuf_tensor
    ps = nc.alloc_psum_tensor
    p = Prog(nc, es)
    wdq = sb("wdq", [128, NIC * D], BF16)
    htb = sb("htb", [128, NIC * 512], BF16)
    g1 = sb("g1", [128, 8 * T], BF16)
    pd = ps("pd2", [128, 512], F32)
    p.dma("sync", wdq[:], wd_d[:], writes=["wdq"])
    for tb in range(4):
        p.dma("sync", htb[:].rearrange("p (a b) -> p a b", b=512),
              hq_d[:].rearrange("p (a b) -> p a b", b=T)[:, :, tb * 512:(tb + 1) * 512],
              writes=["htb"])
        for dmc in range(8):
            for ic in range(NIC):
                p.op("tensor", lambda e, dmc=dmc, ic=ic: e.matmul(
                    pd[:], wdq[:, ic * D + dmc * 128: ic * D + (dmc + 1) * 128],
                    htb[:, ic * 512:(ic + 1) * 512],
                    start=(ic == 0), stop=(ic == NIC - 1)),
                     reads=["wdq", "htb"], writes=["pd2"])
            p.op("scalar", lambda e, dmc=dmc, tb=tb: e.activation(
                g1[:, dmc * T + tb * 512: dmc * T + (tb + 1) * 512], pd[:], AF.Silu),
                 reads=["pd2"], writes=["g1"])
    p.dma("sync", g1_o[:], g1[:], reads=["g1"])
    with nc.Block() as block:
        p.finalize(block)
    nc.finalize()
    return nc


def build_F2():
    nc = bass.Bass("TRN2", target_bir_lowering=False)
    es = ExitStack()
    consts = make_consts(nc)

    o_d = nc.dram_tensor("of", [128, 8 * T], BF16, kind="ExternalInput").ap()
    g1_d = nc.dram_tensor("g1f", [128, 8 * T], BF16, kind="ExternalInput").ap()
    x1_d = nc.dram_tensor("x1", [T, D], F32, kind="ExternalInput").ap()
    wo_d = nc.dram_tensor("wo", [128, 8 * D], F32, kind="ExternalInput").ap()
    bo_d = nc.dram_tensor("bo", [128, 8], F32, kind="ExternalInput").ap()
    x2_o = nc.dram_tensor("x2_o", [T, D], F32, kind="ExternalOutput").ap()

    sb = nc.alloc_sbuf_tensor
    ps = nc.alloc_psum_tensor
    p = Prog(nc, es)
    idb = load_const(p, nc, consts, "id_bf", [128, 128], BF16)

    of = sb("of_s", [128, 8 * T], BF16)
    g1f = sb("g1f_s", [128, 8 * T], BF16)
    wol = sb("wol", [128, 8 * D], BF16)
    bos = sb("bos", [128, 8], F32)
    sg = sb("sg", [128, 512], BF16)
    xin = sb("xin", [128, 4 * D], F32)
    pd = ps("pd", [128, 512], F32)
    ptr = ps("ptr", [128, 128], BF16)

    p.dma("sync", of[:], o_d[:], writes=["of"])
    p.dma("sync", g1f[:], g1_d[:], writes=["g1f"])
    p.dma("gpsimd", wol[:], wo_d[:], writes=["wol"])
    p.dma("sync", bos[:], bo_d[:], writes=["bos"])
    for tb in range(4):
        p.dma("sync", xin[:].rearrange("p (a b) -> p a b", b=D),
              x1_d[:].rearrange("(s p) d -> p s d", p=128)[:, tb * 4:(tb + 1) * 4, :],
              writes=["xin"])
        for dmc in range(8):
            for ic in range(8):
                p.op("tensor", lambda e, dmc=dmc, ic=ic, tb=tb: e.matmul(
                    pd[:], wol[:, ic * D + dmc * 128: ic * D + (dmc + 1) * 128],
                    of[:, ic * T + tb * 512: ic * T + (tb + 1) * 512],
                    start=(ic == 0), stop=(ic == 7)),
                     reads=["wol", "of"], writes=["pd"])
            p.op("vector", lambda e, dmc=dmc, tb=tb: e.scalar_tensor_tensor(
                sg[:], pd[:], bos[:, dmc:dmc + 1],
                g1f[:, dmc * T + tb * 512: dmc * T + (tb + 1) * 512],
                ALU.add, ALU.mult), reads=["pd", "bos", "g1f"], writes=["sg"])
            for st in range(4):
                p.op("tensor", lambda e, st=st: e.transpose(
                    ptr[:], sg[:, st * 128:(st + 1) * 128], idb[:]),
                     reads=["sg", "c_id_bf"], writes=["ptr"])
                p.op("vector", lambda e, st=st, dmc=dmc: e.tensor_tensor(
                    xin[:, st * D + dmc * 128: st * D + (dmc + 1) * 128],
                    xin[:, st * D + dmc * 128: st * D + (dmc + 1) * 128],
                    ptr[:], ALU.add),
                     reads=["ptr", "xin"], writes=["xin"])
        p.dma("sync", x2_o[:].rearrange("(s p) d -> p s d", p=128)[:, tb * 4:(tb + 1) * 4, :],
              xin[:].rearrange("p (a b) -> p a b", b=D), reads=["xin"])
    with nc.Block() as block:
        p.finalize(block)
    nc.finalize()
    return nc


def run_layer1(x1, inputs):
    rD = _run(_get("D", build_D), host_D(x1, inputs))
    q_cores = [np.asarray(rD[c]["q_o"]) for c in range(N_CORES)]
    k_cores = [np.asarray(rD[c]["k_o"]) for c in range(N_CORES)]
    v_cores = [np.asarray(rD[c]["v_o"]) for c in range(N_CORES)]
    xn1_cores = [np.asarray(rD[c]["xn1_o"]) for c in range(N_CORES)]

    rW = _run(_get("W", build_W), host_W(inputs))
    wgq = np.concatenate([np.asarray(rW[c]["wgq_o"]) for c in range(N_CORES)], 0)
    wuq = np.concatenate([np.asarray(rW[c]["wuq_o"]) for c in range(N_CORES)], 0)
    wdq = np.concatenate([np.asarray(rW[c]["wdq_o"]) for c in range(N_CORES)], 0)

    rE = _run(_get("E", build_E), host_E(q_cores, k_cores, v_cores))
    og = np.zeros((B, S, D), BF)
    for c in range(N_CORES):
        r = np.asarray(rE[c]["o_o"])
        for u in range(4):
            bb, hh = divmod(4 * c + u, AH)
            og[bb, :, hh * AHD:(hh + 1) * AHD] =                 r[u].reshape(128, 64, AHD).transpose(1, 0, 2).reshape(S, AHD)

    wgqT = prep_lhsT(np.ascontiguousarray(wgq.T)).astype(BF)
    wuqT = prep_lhsT(np.ascontiguousarray(wuq.T)).astype(BF)
    wdqT = prep_lhsT(np.ascontiguousarray(wdq.T)).astype(BF)
    mapsF1a = [{"xn1": xn1_cores[c], "wgqT": wgqT, "wuqT": wuqT}
               for c in range(N_CORES)]
    rF1a = _run(_get("F1a", build_F1a), mapsF1a)
    mapsF1b = []
    for c in range(N_CORES):
        hq = np.asarray(rF1a[c]["hq_o"]).reshape(128, 16, FI).transpose(1, 0, 2).reshape(T, FI)
        mapsF1b.append({"hqf": fm_from(hq), "wdqT": wdqT})
    rF1b = _run(_get("F1b", build_F1b), mapsF1b)

    wo = prep_lhsT(np.asarray(inputs["L1_attn_Wo"], np.float32))
    bo = np.ascontiguousarray(np.asarray(inputs["L1_attn_bo"], np.float32).reshape(8, 128).T)
    mapsF2 = []
    for c in range(N_CORES):
        b_, k = c // 4, c % 4
        mapsF2.append({"of": fm_from(og[b_, k * T:(k + 1) * T]),
                       "g1f": np.asarray(rF1b[c]["g1_o"]),
                       "x1": np.ascontiguousarray(x1[b_, k * T:(k + 1) * T]),
                       "wo": wo, "bo": bo})
    rF2 = _run(_get("F2", build_F2), mapsF2)
    x2 = np.zeros((B, S, D), np.float32)
    for c in range(N_CORES):
        b_, k = c // 4, c % 4
        x2[b_, k * T:(k + 1) * T] = np.asarray(rF2[c]["x2_o"])
    return x2


def kernel(**inputs):
    st = run_layer0(inputs)
    x2 = run_layer1(st["x1"], inputs)
    return x2


# version 03
# speedup vs baseline: 1.3517x; 1.1491x over previous
import numpy as np
import ml_dtypes
from contextlib import ExitStack
import concourse.bass as bass
import concourse.mybir as mybir
from concourse.bass_utils import run_bass_kernel_spmd

F32 = mybir.dt.float32
BF16 = mybir.dt.bfloat16
I32 = mybir.dt.int32
AF = mybir.ActivationFunctionType
ALU = mybir.AluOpType
AX = mybir.AxisListType
BF = ml_dtypes.bfloat16

N_CORES = 8
B, S, D = 2, 8192, 1024
T = S * B // N_CORES
INNER = 2048
HEADS, STATE, HDIM = 8, 64, 256
AH, AHD = 16, 64
FI = 2816
EPS = 1e-5

ENGINES = ("tensor", "vector", "scalar", "gpsimd", "sync")
DMA_POOL = 12


class Prog:
    def __init__(self, nc, es):
        self.nc = nc
        self.ops = []
        self.last_write = {}
        self.readers = {}
        self.sems = {e: es.enter_context(nc.semaphore(f"S_{e}")) for e in ENGINES}
        self.dma_sems = {e: [es.enter_context(nc.semaphore(f"D_{e}_{i}"))
                             for i in range(DMA_POOL)] for e in ENGINES}
        self.dma_count = {e: 0 for e in ENGINES}

    def op(self, engine, builder, reads=(), writes=(), dma=False):
        deps = set()
        for k in reads:
            if k in self.last_write:
                deps.add(self.last_write[k])
        for k in writes:
            if k in self.last_write:
                deps.add(self.last_write[k])
            for r in self.readers.get(k, ()):
                deps.add(r)
        i = len(self.ops)
        o = dict(id=i, engine=engine, builder=builder, deps=deps, dma=dma,
                 inc_no=None)
        if dma:
            n = self.dma_count[engine]
            self.dma_count[engine] = n + 1
            o["dsem"] = self.dma_sems[engine][n % DMA_POOL]
            o["duse"] = n // DMA_POOL + 1
        self.ops.append(o)
        for k in reads:
            self.readers.setdefault(k, []).append(i)
        for k in writes:
            self.last_write[k] = i
            self.readers[k] = []
        return i

    def dma(self, engine, out, in_, reads=(), writes=(), **kw):
        return self.op(engine, lambda e: e.dma_start(out=out, in_=in_, **kw),
                       reads=reads, writes=writes, dma=True)

    def finalize(self, block):
        needed = set()
        for o in self.ops:
            needed.update(o["deps"])
        counts = {e: 0 for e in ENGINES}
        for o in self.ops:
            if not o["dma"] and o["id"] in needed:
                counts[o["engine"]] += 1
                o["inc_no"] = counts[o["engine"]]
        per_engine = {e: [] for e in ENGINES}
        for o in self.ops:
            waits = {}
            for d in o["deps"]:
                p = self.ops[d]
                if (p["engine"] == "tensor" and o["engine"] == "tensor"
                        and not p["dma"] and not o["dma"]):
                    continue
                if p["dma"]:
                    s, v = p["dsem"], 16 * p["duse"]
                else:
                    s, v = self.sems[p["engine"]], p["inc_no"]
                if s.num not in waits or waits[s.num][1] < v:
                    waits[s.num] = (s, v)
            if o["dma"] and o["duse"] > 1:
                s = o["dsem"]
                if s.num not in waits:
                    waits[s.num] = (s, 16 * (o["duse"] - 1))
            o["waits"] = waits
            per_engine[o["engine"]].append(o)

        def mk_body(engine):
            def body(eng):
                satisfied = {}
                for o in per_engine[engine]:
                    for key, (s, v) in sorted(o["waits"].items()):
                        if satisfied.get(key, -1) >= v:
                            continue
                        eng.wait_ge(s, v)
                        satisfied[key] = v
                    inst = o["builder"](eng)
                    if o["dma"]:
                        inst.then_inc(o["dsem"], 16)
                    elif o["inc_no"] is not None:
                        inst.then_inc(self.sems[engine], 1)
                if engine == "sync":
                    for e in ENGINES:
                        n = self.dma_count[e]
                        for i in range(min(n, DMA_POOL)):
                            uses = (n - i - 1) // DMA_POOL + 1
                            eng.wait_ge(self.dma_sems[e][i], 16 * uses)
            return body

        for e in ENGINES:
            getattr(block, e)(mk_body(e))


def make_consts(nc):
    c = {}
    def inl(name, arr):
        c[name] = nc.inline_tensor(np.ascontiguousarray(arr), name=name).ap()
    eye = np.eye(128)
    inl("id_bf", eye.astype(BF))
    inl("id_f32", eye.astype(np.float32))
    inl("ones_row_f32", np.ones((1, 128), np.float32))
    inl("ones_row_bf", np.ones((1, 128), BF))
    inl("ones_col_f32", np.ones((128, 1), np.float32))
    ut = (np.arange(128)[:, None] <= np.arange(128)[None, :]).astype(np.float32)
    inl("ut_f32", ut)
    inl("ones128_f32", np.ones((128, 128), np.float32))
    inl("maskD", np.where(ut > 0, 0.0, -1e30).astype(np.float32))
    kk = np.arange(128)[:, None]
    cc = np.arange(896)[None, :]
    inl("bigmask", np.where(kk + 384 > cc, -1e9, 0.0).astype(np.float32))
    sel = np.zeros((8, 8 * 128), BF)
    for e in range(8):
        sel[e, e * 128:(e + 1) * 128] = 1
    inl("sel8", sel)
    return c


def load_const(p, nc, consts, name, shape, dt):
    t = nc.alloc_sbuf_tensor("c_" + name, shape, dt)
    p.dma("sync", t[:], consts[name][:], writes=["c_" + name])
    return t


def prep_lhsT(wT):
    K, N = wT.shape
    assert K % 128 == 0
    return np.ascontiguousarray(
        wT.reshape(K // 128, 128, N).transpose(1, 0, 2).reshape(128, -1))


def fm_from(tokmajor):
    Tn, F = tokmajor.shape
    return np.ascontiguousarray(
        tokmajor.T.reshape(F // 128, 128, Tn).transpose(1, 0, 2).reshape(128, -1))


def fm_to(arr, Tn, F):
    return np.ascontiguousarray(
        arr.reshape(128, F // 128, Tn).transpose(1, 0, 2).reshape(F, Tn).T)


def build_A():
    nc = bass.Bass("TRN2", target_bir_lowering=False)
    es = ExitStack()
    consts = make_consts(nc)
    NT = 17
    TC = 2176

    xa_d = nc.dram_tensor("xa", [NT * 128, D], F32, kind="ExternalInput").ap()
    wx_d = nc.dram_tensor("wxT", [128, 8 * INNER], F32, kind="ExternalInput").ap()
    wz_d = nc.dram_tensor("wzT", [128, 8 * INNER], F32, kind="ExternalInput").ap()
    wb_d = nc.dram_tensor("wbT", [128, 16 * 512], F32, kind="ExternalInput").ap()
    wc_d = nc.dram_tensor("wcT", [128, 16 * 512], F32, kind="ExternalInput").ap()
    wdt_d = nc.dram_tensor("wdtT", [128, 16 * 8], F32, kind="ExternalInput").ap()
    convw_d = nc.dram_tensor("convw", [128, 64], F32, kind="ExternalInput").ap()
    convb_d = nc.dram_tensor("convb", [128, 16], F32, kind="ExternalInput").ap()
    bdt_d = nc.dram_tensor("bdt", [8, 1], F32, kind="ExternalInput").ap()

    xn_o = nc.dram_tensor("xn_o", [128, 8 * T], BF16, kind="ExternalOutput").ap()
    xlo_o = nc.dram_tensor("xlo_o", [128, 8 * T], BF16, kind="ExternalOutput").ap()
    sz_o = nc.dram_tensor("sz_o", [128, 16 * T], BF16, kind="ExternalOutput").ap()
    xc_o = nc.dram_tensor("xc_o", [128, 16 * T], BF16, kind="ExternalOutput").ap()
    b_o = nc.dram_tensor("b_o", [64, 8 * T], BF16, kind="ExternalOutput").ap()
    c_o = nc.dram_tensor("c_o", [64, 8 * T], BF16, kind="ExternalOutput").ap()
    dt_o = nc.dram_tensor("dt_o", [8, T], F32, kind="ExternalOutput").ap()

    sb = nc.alloc_sbuf_tensor
    ps = nc.alloc_psum_tensor
    p = Prog(nc, es)
    idb = load_const(p, nc, consts, "id_bf", [128, 128], BF16)

    xn = sb("xn", [128, 8 * TC], BF16)
    wbig = sb("wbig", [128, 8 * INNER], BF16)
    xt = [sb(f"xt{i}", [128, D], F32) for i in range(2)]
    xn32 = sb("xn32", [128, D], F32)
    xnb = sb("xnb", [128, D], BF16)
    xlo_t = sb("xlo_t", [128, D], BF16)
    xlo_st = sb("xlo_st", [128, D], BF16)
    mcol = sb("mcol", [128, 1], F32)
    vcol = sb("vcol", [128, 4], F32)
    sq = sb("sq", [128, D], F32)
    big = sb("big", [128, 16 * T], BF16)
    xpb = sb("xpb", [128, 16 * 515], BF16)
    wb2 = wbig[:, 0:16 * 512]
    wc2 = wbig[:, 16 * 512:16 * 1024]
    wdt2 = sb("wdt2", [128, 16 * 8], BF16)
    cw = sb("cw", [128, 64], F32)
    cb = sb("cb", [128, 16], F32)
    bdt = sb("bdt_s", [8, 1], F32)
    bstage = sb("bstage", [64, 8 * 512], BF16)
    cstage = sb("cstage", [64, 8 * 512], BF16)
    dts = sb("dts", [8, T], F32)
    dtt = sb("dtt", [8, 512], F32)
    spa = sb("spa", [8, 512], F32)
    cvt = sb("cvt", [128, 512], F32)

    pml = [ps(f"pm{i}", [128, 512], F32) for i in range(2)]
    ptr = ps("ptr", [128, 128], BF16)
    pb = ps("pb", [64, 512], F32)
    pdt = ps("pdt", [8, 512], F32)

    p.dma("gpsimd", cw[:], convw_d[:], writes=["cw"])
    p.dma("gpsimd", cb[:], convb_d[:], writes=["cb"])
    p.dma("gpsimd", bdt[:], bdt_d[:], writes=["bdt"])

    p.dma("gpsimd", wbig[:], wz_d[:], writes=["wbA", "wbB"])

    for t in range(NT):
        xtw, kxt = xt[t % 2], f"xt{t % 2}"
        p.dma("sync" if t % 2 == 0 else "scalar", xtw[:],
              xa_d[t * 128:(t + 1) * 128, :], writes=[kxt])
        p.op("vector", lambda e, xtw=xtw: e.reduce_sum(mcol[:], xtw[:], axis=AX.X),
             reads=[kxt], writes=["mcol"])
        p.op("scalar", lambda e, xtw=xtw: e.activation(sq[:], xtw[:], AF.Square),
             reads=[kxt], writes=["sq"])
        p.op("vector", lambda e: e.reduce_sum(vcol[:, 0:1], sq[:], axis=AX.X),
             reads=["sq"], writes=["vcol"])
        p.op("vector", lambda e: e.tensor_scalar(mcol[:], mcol[:], 1.0 / D, None, ALU.mult),
             reads=["mcol"], writes=["mcol"])
        p.op("scalar", lambda e: e.activation(vcol[:, 1:2], mcol[:], AF.Square),
             reads=["mcol"], writes=["vcol2"])
        p.op("vector", lambda e: e.scalar_tensor_tensor(
            vcol[:, 2:3], vcol[:, 0:1], 1.0 / D, vcol[:, 1:2], ALU.mult, ALU.subtract),
             reads=["vcol", "vcol2"], writes=["vcol3"])
        p.op("vector", lambda e: e.tensor_scalar(vcol[:, 2:3], vcol[:, 2:3], EPS, None, ALU.add),
             reads=["vcol3"], writes=["vcol3"])
        p.op("scalar", lambda e: e.activation(vcol[:, 3:4], vcol[:, 2:3], AF.Ln),
             reads=["vcol3"], writes=["vcol4"])
        p.op("scalar", lambda e: e.activation(vcol[:, 3:4], vcol[:, 3:4], AF.Exp, scale=-0.5),
             reads=["vcol4"], writes=["vcol4"])
        p.op("vector", lambda e, xtw=xtw: e.tensor_scalar(
            xn32[:], xtw[:], mcol[:], vcol[:, 3:4], ALU.subtract, ALU.mult),
             reads=[kxt, "mcol", "vcol4"], writes=["xn32"])
        p.op("vector", lambda e: e.tensor_copy(xnb[:], xn32[:]),
             reads=["xn32"], writes=["xnb"])
        p.op("vector", lambda e: e.tensor_sub(xlo_t[:], xn32[:], xnb[:]),
             reads=["xn32", "xnb"], writes=["xlo_t"])
        for kc in range(8):
            p.op("tensor", lambda e, kc=kc: e.transpose(
                ptr[:], xnb[:, kc * 128:(kc + 1) * 128], idb[:]),
                 reads=["xnb", "c_id_bf"], writes=["ptr"])
            p.op("vector", lambda e, kc=kc, t=t: e.tensor_copy(
                xn[:, kc * TC + t * 128: kc * TC + (t + 1) * 128], ptr[:]),
                 reads=["ptr"], writes=["xn"])
        if t >= 1:
            for kc in range(8):
                p.op("tensor", lambda e, kc=kc: e.transpose(
                    ptr[:], xlo_t[:, kc * 128:(kc + 1) * 128], idb[:]),
                     reads=["xlo_t", "c_id_bf"], writes=["ptr"])
                p.op("vector", lambda e, kc=kc: e.tensor_copy(
                    xlo_st[:, kc * 128:(kc + 1) * 128], ptr[:]),
                     reads=["ptr"], writes=["xlo_st"])
            p.dma("sync",
                  xlo_o[:].rearrange("p (a b) -> p a b", b=T)[:, :, (t - 1) * 128: t * 128],
                  xlo_st[:].rearrange("p (a b) -> p a b", b=128),
                  reads=["xlo_st"])

    for j in range(4):
        off = 128 + j * 512
        for ic in range(16):
            pm = pml[ic % 2]
            for kc in range(8):
                p.op("tensor", lambda e, ic=ic, kc=kc, off=off, pm=pm: e.matmul(
                    pm[:], wbig[:, kc * INNER + ic * 128: kc * INNER + (ic + 1) * 128],
                    xn[:, kc * TC + off: kc * TC + off + 512],
                    start=(kc == 0), stop=(kc == 7)),
                     reads=["wbA" if kc < 4 else "wbB", "xn"], writes=[f"pm{ic % 2}"])
            p.op("scalar", lambda e, ic=ic, j=j, pm=pm: e.activation(
                big[:, ic * T + j * 512: ic * T + (j + 1) * 512], pm[:], AF.Silu),
                 reads=[f"pm{ic % 2}"], writes=["big"])
    p.dma("sync", sz_o[:], big[:], reads=["big"])

    p.dma("gpsimd", wbig[:], wx_d[:], writes=["wbA", "wbB"])
    for j in range(4):
        off = 128 + j * 512
        for ic in range(16):
            pm = pml[ic % 2]
            for kc in range(8):
                p.op("tensor", lambda e, ic=ic, kc=kc, off=off, pm=pm: e.matmul(
                    pm[:], wbig[:, kc * INNER + ic * 128: kc * INNER + (ic + 1) * 128],
                    xn[:, kc * TC + off: kc * TC + off + 512],
                    start=(kc == 0), stop=(kc == 7)),
                     reads=["wbA" if kc < 4 else "wbB", "xn"], writes=[f"pm{ic % 2}"])
            p.op("vector", lambda e, ic=ic, pm=pm: e.tensor_copy(
                xpb[:, ic * 515 + 3: (ic + 1) * 515], pm[:]),
                 reads=[f"pm{ic % 2}"], writes=["xpb" + str(ic)])
        if j == 0:
            for ic in range(16):
                pm = pml[ic % 2]
                for kc in range(8):
                    p.op("tensor", lambda e, ic=ic, kc=kc, pm=pm: e.matmul(
                        pm[:, 0:3], wbig[:, kc * INNER + ic * 128: kc * INNER + (ic + 1) * 128],
                        xn[:, kc * TC + 125: kc * TC + 128],
                        start=(kc == 0), stop=(kc == 7)),
                         reads=["wbA" if kc < 4 else "wbB", "xn"], writes=[f"pm{ic % 2}"])
                p.op("vector", lambda e, ic=ic, pm=pm: e.tensor_copy(
                    xpb[:, ic * 515: ic * 515 + 3], pm[:, 0:3]),
                     reads=[f"pm{ic % 2}"], writes=["xpb" + str(ic)])
        for ic in range(16):
            p.op("vector", lambda e, ic=ic: e.tensor_scalar(
                cvt[:], xpb[:, ic * 515: ic * 515 + 512], cw[:, ic * 4: ic * 4 + 1],
                None, ALU.mult),
                 reads=["xpb" + str(ic), "cw"], writes=["cvt"])
            for k in range(1, 4):
                p.op("vector", lambda e, ic=ic, k=k: e.scalar_tensor_tensor(
                    cvt[:], xpb[:, ic * 515 + k: ic * 515 + k + 512],
                    cw[:, ic * 4 + k: ic * 4 + k + 1], cvt[:], ALU.mult, ALU.add),
                     reads=["xpb" + str(ic), "cw", "cvt"],
                     writes=["cvt"])
            p.op("scalar", lambda e, ic=ic, j=j: e.activation(
                big[:, ic * T + j * 512: ic * T + (j + 1) * 512], cvt[:],
                AF.Silu, bias=cb[:, ic: ic + 1]),
                 reads=["cvt", "cb"], writes=["big"])
        if j < 3:
            for ic in range(16):
                p.op("vector", lambda e, ic=ic: e.tensor_copy(
                    xpb[:, ic * 515: ic * 515 + 3],
                    xpb[:, ic * 515 + 512: ic * 515 + 515]),
                     reads=["xpb" + str(ic)], writes=["xpb" + str(ic)])
    p.dma("sync", xc_o[:], big[:], reads=["big"])
    p.dma("sync", xn_o[:], xn[:].rearrange("p (a b) -> p a b", b=TC)[:, :, 128:TC],
          reads=["xn"])

    p.dma("gpsimd", wb2, wb_d[:], writes=["wbA", "wb2"])
    p.dma("gpsimd", wc2, wc_d[:], writes=["wbB", "wc2"])
    p.dma("gpsimd", wdt2[:], wdt_d[:], writes=["wdt2"])
    for j in range(4):
        for h in range(8):
            for (wname, wt, dkey) in (("wb2", wb2, "bstage"),
                                      ("wc2", wc2, "cstage")):
                dst = bstage if dkey == "bstage" else cstage
                for kc in range(16):
                    p.op("tensor", lambda e, h=h, kc=kc, j=j, wt=wt: e.matmul(
                        pb[:], wt[:, kc * 512 + h * 64: kc * 512 + (h + 1) * 64],
                        big[:, kc * T + j * 512: kc * T + (j + 1) * 512],
                        start=(kc == 0), stop=(kc == 15)),
                         reads=[wname, "big"], writes=["pb"])
                p.op("vector", lambda e, h=h, dst=dst: e.tensor_copy(
                    dst[:, h * 512: (h + 1) * 512], pb[:]),
                     reads=["pb"], writes=[dkey])
        p.dma("sync", b_o[:].rearrange("p (a b) -> p a b", b=T)[:, :, j * 512:(j + 1) * 512],
              bstage[:].rearrange("p (a b) -> p a b", b=512), reads=["bstage"])
        p.dma("sync", c_o[:].rearrange("p (a b) -> p a b", b=T)[:, :, j * 512:(j + 1) * 512],
              cstage[:].rearrange("p (a b) -> p a b", b=512), reads=["cstage"])
        for kc in range(16):
            p.op("tensor", lambda e, kc=kc, j=j: e.matmul(
                pdt[:], wdt2[:, kc * 8:(kc + 1) * 8],
                big[:, kc * T + j * 512: kc * T + (j + 1) * 512],
                start=(kc == 0), stop=(kc == 15)),
                 reads=["wdt2", "big"], writes=["pdt"])
        p.op("vector", lambda e: e.tensor_scalar(dtt[:], pdt[:], bdt[:], None, ALU.add),
             reads=["pdt", "bdt"], writes=["dtt"])
        p.op("scalar", lambda e: e.activation(spa[:], dtt[:], AF.Abs),
             reads=["dtt"], writes=["spa"])
        p.op("scalar", lambda e: e.activation(spa[:], spa[:], AF.Exp, scale=-1.0),
             reads=["spa"], writes=["spa"])
        p.op("scalar", lambda e: e.activation(spa[:], spa[:], AF.Ln, bias=1.0),
             reads=["spa"], writes=["spa"])
        p.op("vector", lambda e: e.tensor_scalar_max(dtt[:], dtt[:], 0.0),
             reads=["dtt"], writes=["dtt"])
        p.op("vector", lambda e, j=j: e.tensor_add(
            dts[:, j * 512:(j + 1) * 512], dtt[:], spa[:]),
             reads=["dtt", "spa"], writes=["dts"])
    p.dma("sync", dt_o[:], dts[:], reads=["dts"])

    with nc.Block() as block:
        p.finalize(block)
    nc.finalize()
    return nc


def host_A(inputs):
    x = np.asarray(inputs["L0_attn_input_tensor"])
    wx = prep_lhsT(np.ascontiguousarray(np.asarray(inputs["L0_attn_Wx"]).T))
    wz = prep_lhsT(np.ascontiguousarray(np.asarray(inputs["L0_attn_Wz"]).T))
    wb = prep_lhsT(np.ascontiguousarray(np.asarray(inputs["L0_attn_Wb"]).T))
    wc = prep_lhsT(np.ascontiguousarray(np.asarray(inputs["L0_attn_Wc"]).T))
    wdt = prep_lhsT(np.ascontiguousarray(np.asarray(inputs["L0_attn_Wdt"]).T))
    cwf = np.asarray(inputs["L0_attn_conv_w"])[:, 0, :]
    convw = np.ascontiguousarray(cwf.reshape(16, 128, 4).transpose(1, 0, 2).reshape(128, 64))
    convb = np.ascontiguousarray(np.asarray(inputs["L0_attn_conv_b"]).reshape(16, 128).T)
    bdt = np.asarray(inputs["L0_attn_b_dt"]).reshape(8, 1)
    maps = []
    for c in range(N_CORES):
        b_, k = c // 4, c % 4
        xa = np.zeros((17 * 128, D), np.float32)
        if k > 0:
            xa[125:128] = x[b_, k * T - 3: k * T]
        xa[128:] = x[b_, k * T: (k + 1) * T]
        maps.append({"xa": xa, "wxT": wx, "wzT": wz, "wbT": wb, "wcT": wc,
                     "wdtT": wdt, "convw": convw, "convb": convb, "bdt": bdt})
    return maps


def build_B():
    nc = bass.Bass("TRN2", target_bir_lowering=False)
    es = ExitStack()
    consts = make_consts(nc)
    NCH = 64

    xu_d = [nc.dram_tensor(f"xu{u}", [128, NCH * HDIM], BF16, kind="ExternalInput").ap() for u in range(2)]
    bu_d = [nc.dram_tensor(f"bu{u}", [64, S], BF16, kind="ExternalInput").ap() for u in range(2)]
    cu_d = [nc.dram_tensor(f"cu{u}", [64, S], BF16, kind="ExternalInput").ap() for u in range(2)]
    dtu_d = [nc.dram_tensor(f"dtu{u}", [128, NCH], F32, kind="ExternalInput").ap() for u in range(2)]
    la_d = nc.dram_tensor("la", [2, 1], F32, kind="ExternalInput").ap()
    dp_d = nc.dram_tensor("dp", [2, 1], F32, kind="ExternalInput").ap()
    gnw_d = nc.dram_tensor("gnw", [2, HDIM], F32, kind="ExternalInput").ap()
    gnb_d = nc.dram_tensor("gnb", [2, HDIM], F32, kind="ExternalInput").ap()
    y_o = nc.dram_tensor("y_o", [2, 128, NCH * HDIM], BF16, kind="ExternalOutput").ap()

    sb = nc.alloc_sbuf_tensor
    ps = nc.alloc_psum_tensor
    p = Prog(nc, es)
    idb = load_const(p, nc, consts, "id_bf", [128, 128], BF16)
    onr = load_const(p, nc, consts, "ones_row_f32", [1, 128], F32)
    onc = load_const(p, nc, consts, "ones_col_f32", [128, 1], F32)
    ut = load_const(p, nc, consts, "ut_f32", [128, 128], F32)
    on128 = load_const(p, nc, consts, "ones128_f32", [128, 128], F32)
    mkd = load_const(p, nc, consts, "maskD", [128, 128], F32)

    Xu = sb("Xu", [128, NCH * HDIM], BF16)
    Bu = sb("Bu", [64, S], BF16)
    Cu = sb("Cu", [64, S], BF16)
    dtu = sb("dtu", [128, NCH], F32)
    y_sb = sb("y_sb", [128, NCH * HDIM], BF16)
    dta = sb("dta", [128, NCH], F32)
    small = sb("small", [128, 8], F32)
    acol = sb("acol", [128, 1], F32)
    aend = sb("aend", [128, 1], F32)
    arow = sb("arow", [1, 128], F32)
    a_col = sb("a_col", [128, 1], F32)
    d_col = sb("d_col", [128, 1], F32)
    d_sb = sb("d_sb", [128, 128], F32)
    m_bf = sb("m_bf", [128, 128], BF16)
    ea_bf = sb("ea_bf", [64, 128], BF16)
    cbar = sb("cbar", [64, 128], BF16)
    p_f = sb("p_f", [128, 128], F32)
    p_bf = sb("p_bf", [128, 128], BF16)
    f_col = sb("f_col", [128, 1], F32)
    bt_bf = sb("bt_bf", [128, 64], BF16)
    e64 = sb("e64", [64, 1], F32)
    Ssb = sb("Ssb", [64, HDIM], F32)
    S_bf = sb("S_bf", [64, HDIM], BF16)
    sum_cols = sb("sum_cols", [128, NCH], F32)
    sq_cols = sb("sq_cols", [128, NCH], F32)
    sq_scr = sb("sq_scr", [128, HDIM], F32)
    gnw_row = sb("gnw_row", [1, HDIM], F32)
    gnb_row = sb("gnb_row", [1, HDIM], F32)
    k1b = sb("k1b", [128, HDIM], F32)
    k2b = sb("k2b", [128, HDIM], F32)
    t256 = sb("t256", [128, HDIM], F32)

    pc = ps("pc", [128, 4], F32)
    parow = ps("parow", [1, 128], F32)
    pbca = ps("pbca", [128, 128], F32)
    pg = ps("pg", [128, 128], F32)
    py = ps("py", [128, HDIM], F32)
    pbt = ps("pbt", [128, 64], BF16)
    pds = ps("pds", [64, HDIM], F32)

    for u in range(2):
        p.dma("sync", Xu[:], xu_d[u][:], writes=["Xu"])
        p.dma("sync", Bu[:], bu_d[u][:], writes=["Bu"])
        p.dma("sync", Cu[:], cu_d[u][:], writes=["Cu"])
        p.dma("sync", dtu[:], dtu_d[u][:], writes=["dtu"])
        p.dma("sync", small[0:1, 0:1], la_d[u:u + 1, :], writes=["la"])
        p.dma("sync", small[0:1, 1:2], dp_d[u:u + 1, :], writes=["dp"])
        p.dma("sync", gnw_row[:], gnw_d[u:u + 1, :], writes=["gnw_row"])
        p.dma("sync", gnb_row[:], gnb_d[u:u + 1, :], writes=["gnb_row"])
        p.op("scalar", lambda e: e.activation(small[0:1, 2:3], small[0:1, 0:1], AF.Exp),
             reads=["la"], writes=["a11"])
        p.op("vector", lambda e: e.tensor_scalar(small[0:1, 2:3], small[0:1, 2:3], -1.0, None, ALU.mult),
             reads=["a11"], writes=["a11"])
        p.op("tensor", lambda e: e.matmul(pc[:, 0:1], onr[:], small[0:1, 2:3], start=True, stop=True),
             reads=["a11", "c_ones_row_f32"], writes=["pcb"])
        p.op("vector", lambda e: e.tensor_copy(a_col[:], pc[:, 0:1]), reads=["pcb"], writes=["a_col"])
        p.op("tensor", lambda e: e.matmul(pc[:, 1:2], onr[:], small[0:1, 1:2], start=True, stop=True),
             reads=["dp", "c_ones_row_f32"], writes=["pcb"])
        p.op("vector", lambda e: e.tensor_copy(d_col[:], pc[:, 1:2]), reads=["pcb"], writes=["d_col"])
        p.op("vector", lambda e: e.tensor_scalar(dta[:], dtu[:], a_col[:], None, ALU.mult),
             reads=["dtu", "a_col"], writes=["dta"])
        p.op("vector", lambda e: e.memset(Ssb[:], 0.0), reads=[], writes=["Ssb"])

        for i in range(NCH):
            xsl = Xu[:, i * HDIM:(i + 1) * HDIM]
            bsl = Bu[:, i * 128:(i + 1) * 128]
            csl = Cu[:, i * 128:(i + 1) * 128]
            dcol = dtu[:, i:i + 1]
            dtacol = dta[:, i:i + 1]
            p.op("tensor", lambda e, dtacol=dtacol: e.matmul(pc[:, 2:3], ut[:], dtacol, start=True, stop=True),
                 reads=["dta", "c_ut_f32"], writes=["pcb"])
            p.op("vector", lambda e: e.tensor_copy(acol[:], pc[:, 2:3]), reads=["pcb"], writes=["acol"])
            p.op("tensor", lambda e, dtacol=dtacol: e.matmul(pc[:, 3:4], on128[:], dtacol, start=True, stop=True),
                 reads=["dta", "c_ones128_f32"], writes=["pcb"])
            p.op("vector", lambda e: e.tensor_copy(aend[:], pc[:, 3:4]), reads=["pcb"], writes=["aend"])
            p.op("tensor", lambda e, dtacol=dtacol: e.matmul(parow[:], dtacol, ut[:], start=True, stop=True),
                 reads=["dta", "c_ut_f32"], writes=["parow"])
            p.op("vector", lambda e: e.tensor_copy(arow[:], parow[:]), reads=["parow"], writes=["arow"])
            p.op("tensor", lambda e: e.matmul(pbca[:], onr[:], arow[:], start=True, stop=True),
                 reads=["arow", "c_ones_row_f32"], writes=["pbca"])
            p.op("vector", lambda e: e.scalar_tensor_tensor(
                d_sb[:], pbca[:], acol[:], mkd[:], ALU.subtract, ALU.add),
                 reads=["pbca", "acol", "c_maskD"], writes=["d_sb", "bca_tok"])
            p.op("scalar", lambda e: e.activation(m_bf[:], d_sb[:], AF.Exp),
                 reads=["d_sb"], writes=["m_bf"])
            p.op("scalar", lambda e: e.activation(ea_bf[:], pbca[0:64, :], AF.Exp),
                 reads=["pbca", "bca_tok"], writes=["ea_bf"])
            if i > 0:
                p.op("vector", lambda e, csl=csl: e.tensor_tensor(cbar[:], csl, ea_bf[:], ALU.mult),
                     reads=["Cu", "ea_bf"], writes=["cbar"])
            p.op("tensor", lambda e, bsl=bsl, csl=csl: e.matmul(pg[:], bsl, csl, start=True, stop=True),
                 reads=["Bu", "Cu"], writes=["pg"])
            p.op("vector", lambda e: e.tensor_tensor(p_f[:], pg[:], m_bf[:], ALU.mult),
                 reads=["pg", "m_bf"], writes=["p_f"])
            p.op("vector", lambda e, dcol=dcol: e.tensor_scalar(p_bf[:], p_f[:], dcol, None, ALU.mult),
                 reads=["p_f", "dtu"], writes=["p_bf"])
            p.op("tensor", lambda e, xsl=xsl, i=i: e.matmul(py[:], p_bf[:], xsl,
                 start=True, stop=(i == 0)), reads=["p_bf", "Xu"], writes=["py"])
            if i > 0:
                p.op("tensor", lambda e: e.matmul(py[:], cbar[:], S_bf[:], start=False, stop=True),
                     reads=["cbar", "S_bf"], writes=["py"])
            ysl = y_sb[:, i * HDIM:(i + 1) * HDIM]
            p.op("vector", lambda e, xsl=xsl, ysl=ysl: e.scalar_tensor_tensor(
                ysl, xsl, d_col[:], py[:], ALU.mult, ALU.add),
                 reads=["Xu", "d_col", "py"], writes=["y_sb" + str(i % 4)])
            p.op("vector", lambda e, ysl=ysl, i=i: e.reduce_sum(
                sum_cols[:, i:i + 1], ysl, axis=AX.X),
                 reads=["y_sb" + str(i % 4)], writes=["sum_cols"])
            p.op("scalar", lambda e, ysl=ysl, i=i: e.activation(
                sq_scr[:], ysl, AF.Square, accum_out=sq_cols[:, i:i + 1]),
                 reads=["y_sb" + str(i % 4)], writes=["sq_cols", "sq_scr"])
            p.op("vector", lambda e: e.tensor_sub(f_col[:], aend[:], acol[:]),
                 reads=["aend", "acol"], writes=["f_col"])
            p.op("scalar", lambda e: e.activation(f_col[:], f_col[:], AF.Exp),
                 reads=["f_col"], writes=["f_col"])
            p.op("vector", lambda e, dcol=dcol: e.tensor_scalar(f_col[:], f_col[:], dcol, None, ALU.mult),
                 reads=["f_col", "dtu"], writes=["f_col"])
            p.op("tensor", lambda e, bsl=bsl: e.transpose(pbt[:], bsl, idb[0:64, 0:64]),
                 reads=["Bu", "c_id_bf"], writes=["pbt"])
            p.op("vector", lambda e: e.tensor_scalar(bt_bf[:], pbt[:], f_col[:], None, ALU.mult),
                 reads=["pbt", "f_col"], writes=["bt_bf"])
            p.op("tensor", lambda e, xsl=xsl: e.matmul(pds[:], bt_bf[:], xsl, start=True, stop=True),
                 reads=["bt_bf", "Xu"], writes=["pds"])
            p.op("scalar", lambda e: e.activation(e64[:], aend[0:64, :], AF.Exp),
                 reads=["aend"], writes=["e64"])
            p.op("vector", lambda e: e.scalar_tensor_tensor(
                Ssb[:], Ssb[:], e64[:], pds[:], ALU.mult, ALU.add),
                 reads=["Ssb", "e64", "pds"], writes=["Ssb"])
            p.op("vector", lambda e: e.tensor_copy(S_bf[:], Ssb[:]),
                 reads=["Ssb"], writes=["S_bf"])

        p.op("vector", lambda e: e.reduce_sum(acol[:], sum_cols[:], axis=AX.X),
             reads=["sum_cols"], writes=["acol"])
        p.op("vector", lambda e: e.reduce_sum(aend[:], sq_cols[:], axis=AX.X),
             reads=["sq_cols"], writes=["aend"])
        p.op("tensor", lambda e: e.matmul(pc[0:1, 0:1], acol[:], onc[:], start=True, stop=True),
             reads=["acol", "c_ones_col_f32"], writes=["pcb"])
        p.op("vector", lambda e: e.tensor_scalar(small[0:1, 3:4], pc[0:1, 0:1],
             1.0 / (S * HDIM), None, ALU.mult), reads=["pcb"], writes=["mean"])
        p.op("tensor", lambda e: e.matmul(pc[0:1, 1:2], aend[:], onc[:], start=True, stop=True),
             reads=["aend", "c_ones_col_f32"], writes=["pcb"])
        p.op("scalar", lambda e: e.activation(small[0:1, 4:5], small[0:1, 3:4], AF.Square),
             reads=["mean"], writes=["msq"])
        p.op("vector", lambda e: e.scalar_tensor_tensor(
            small[0:1, 5:6], pc[0:1, 1:2], 1.0 / (S * HDIM), small[0:1, 4:5],
            ALU.mult, ALU.subtract), reads=["pcb", "msq"], writes=["var"])
        p.op("vector", lambda e: e.tensor_scalar(small[0:1, 5:6], small[0:1, 5:6],
             EPS, None, ALU.add), reads=["var"], writes=["var"])
        p.op("scalar", lambda e: e.activation(small[0:1, 5:6], small[0:1, 5:6], AF.Ln),
             reads=["var"], writes=["var"])
        p.op("scalar", lambda e: e.activation(small[0:1, 5:6], small[0:1, 5:6], AF.Exp, scale=-0.5),
             reads=["var"], writes=["var"])
        p.op("vector", lambda e: e.tensor_scalar(small[0:1, 6:7], small[0:1, 3:4], -1.0, None, ALU.mult),
             reads=["mean"], writes=["negmean"])
        p.op("vector", lambda e: e.tensor_scalar(gnw_row[:], gnw_row[:], small[0:1, 5:6], None, ALU.mult),
             reads=["gnw_row", "var"], writes=["gnw_row"])
        p.op("vector", lambda e: e.scalar_tensor_tensor(
            gnb_row[:], gnw_row[:], small[0:1, 6:7], gnb_row[:], ALU.mult, ALU.add),
             reads=["gnw_row", "negmean", "gnb_row"], writes=["gnb_row"])
        p.op("tensor", lambda e: e.matmul(py[:], onr[:], gnw_row[:], start=True, stop=True),
             reads=["gnw_row", "c_ones_row_f32"], writes=["py"])
        p.op("vector", lambda e: e.tensor_copy(k1b[:], py[:]), reads=["py"], writes=["k1b"])
        p.op("tensor", lambda e: e.matmul(py[:], onr[:], gnb_row[:], start=True, stop=True),
             reads=["gnb_row", "c_ones_row_f32"], writes=["py"])
        p.op("vector", lambda e: e.tensor_copy(k2b[:], py[:]), reads=["py"], writes=["k2b"])
        for i in range(NCH):
            ysl = y_sb[:, i * HDIM:(i + 1) * HDIM]
            p.op("vector", lambda e, ysl=ysl: e.tensor_tensor(t256[:], ysl, k1b[:], ALU.mult),
                 reads=["y_sb" + str(i % 4), "k1b"], writes=["t256"])
            p.op("vector", lambda e, ysl=ysl: e.tensor_tensor(ysl, t256[:], k2b[:], ALU.add),
                 reads=["t256", "k2b"], writes=["y_sb" + str(i % 4)])
        p.dma("sync", y_o[u], y_sb[:],
              reads=["y_sb0", "y_sb1", "y_sb2", "y_sb3"])
    with nc.Block() as block:
        p.finalize(block)
    nc.finalize()
    return nc


def host_B(xc_g, b_g, c_g, dt_g, inputs):
    la = np.asarray(inputs["L0_attn_log_a"], np.float32)
    dp = np.asarray(inputs["L0_attn_d_param"], np.float32)
    gnw = np.asarray(inputs["L0_attn_gn_w"], np.float32).reshape(HEADS, HDIM)
    gnb = np.asarray(inputs["L0_attn_gn_b"], np.float32).reshape(HEADS, HDIM)
    maps = []
    for c in range(N_CORES):
        m = {}
        us = [divmod(2 * c + u, HEADS) for u in range(2)]
        for u, (bb, hh) in enumerate(us):
            X = np.ascontiguousarray(xc_g[bb, :, hh * HDIM:(hh + 1) * HDIM]
                                     .reshape(64, 128, HDIM).transpose(1, 0, 2).reshape(128, -1))
            m[f"xu{u}"] = X
            m[f"bu{u}"] = np.ascontiguousarray(b_g[bb, :, hh * 64:(hh + 1) * 64].T)
            m[f"cu{u}"] = np.ascontiguousarray(c_g[bb, :, hh * 64:(hh + 1) * 64].T)
            m[f"dtu{u}"] = np.ascontiguousarray(dt_g[bb, :, hh].reshape(64, 128).T)
        m["la"] = np.stack([la[h] for _, h in us]).reshape(2, 1)
        m["dp"] = np.stack([dp[h] for _, h in us]).reshape(2, 1)
        m["gnw"] = np.stack([gnw[h] for _, h in us])
        m["gnb"] = np.stack([gnb[h] for _, h in us])
        maps.append(m)
    return maps


def build_M():
    nc = bass.Bass("TRN2", target_bir_lowering=False)
    es = ExitStack()
    consts = make_consts(nc)

    xn_d = nc.dram_tensor("xn", [128, 8 * T], BF16, kind="ExternalInput").ap()
    xlo_d = nc.dram_tensor("xlo", [128, 8 * T], BF16, kind="ExternalInput").ap()
    gw_d = nc.dram_tensor("gw", [128, 8 * 8], F32, kind="ExternalInput").ap()
    shg_d = nc.dram_tensor("shg", [128, 8 * INNER], F32, kind="ExternalInput").ap()
    shu_d = nc.dram_tensor("shu", [128, 8 * INNER], F32, kind="ExternalInput").ap()
    shd_d = nc.dram_tensor("shd", [128, 16 * D], F32, kind="ExternalInput").ap()
    rtg_d = nc.dram_tensor("rtg", [8, 128, 8 * INNER], F32, kind="ExternalInput").ap()
    rtu_d = nc.dram_tensor("rtu", [8, 128, 8 * INNER], F32, kind="ExternalInput").ap()
    rtd_d = nc.dram_tensor("rtd", [8, 128, 16 * D], F32, kind="ExternalInput").ap()
    g_o = nc.dram_tensor("g_o", [128, 8 * T], BF16, kind="ExternalOutput").ap()

    sb = nc.alloc_sbuf_tensor
    ps = nc.alloc_psum_tensor
    p = Prog(nc, es)
    idb = load_const(p, nc, consts, "id_bf", [128, 128], BF16)
    sel8 = load_const(p, nc, consts, "sel8", [8, 8 * 128], BF16)

    xnf = sb("xnf", [128, 8 * T], BF16)
    gwl = sb("gwl", [128, 8 * 8], BF16)
    gw32 = sb("gw32", [128, 8 * 8], F32)
    gwlo = sb("gwlo", [128, 8 * 8], BF16)
    wg = sb("wg", [128, 8 * INNER], BF16)
    wu = sb("wu", [128, 8 * INNER], BF16)
    wd = sb("wd", [128, 16 * D], BF16)
    xets = sb("xets", [128, 8 * 512], BF16)
    actb = sb("actb", [128, 16 * 512], BF16)
    tb_bf = sb("tb_bf", [128, 512], BF16)
    y_sb = sb("y_sb2", [128, 8 * T], BF16)
    mt = sb("mt", [128, 8], BF16)
    tv = sb("tv", [128, 2], F32)
    mhm = sb("mhm", [8, T], BF16)
    msk = sb("msk", [128, T], BF16)

    plog = ps("plog", [128, 8], F32)
    ptr8 = ps("ptr8", [8, 128], BF16)
    pg = ps("pg", [128, 512], F32)
    pu = ps("pu", [128, 512], F32)
    pd = ps("pd", [128, 512], F32)
    pmsk = ps("pmsk", [128, 512], F32)

    p.dma("sync", xnf[:], xn_d[:], writes=["xnf"])
    p.dma("sync", wg[:], xlo_d[:], writes=["wg"])
    p.dma("gpsimd", gwl[:], gw_d[:], writes=["gwl"])
    p.dma("sync", gw32[:], gw_d[:], writes=["gw32"])
    p.op("vector", lambda e: e.tensor_sub(gwlo[:], gw32[:], gwl[:]),
         reads=["gw32", "gwl"], writes=["gwlo"])

    for t in range(16):
        for pi, (xb_, xkey, wb_, wkey) in enumerate(
                ((xnf, "xnf", gwl, "gwl"), (wg, "wg", gwl, "gwl"),
                 (xnf, "xnf", gwlo, "gwlo"))):
            for kc in range(8):
                p.op("tensor", lambda e, t=t, kc=kc, xb_=xb_, wb_=wb_, pi=pi: e.matmul(
                    plog[:], xb_[:, kc * T + t * 128: kc * T + (t + 1) * 128],
                    wb_[:, kc * 8:(kc + 1) * 8],
                    start=(pi == 0 and kc == 0), stop=(pi == 2 and kc == 7)),
                     reads=[xkey, wkey], writes=["plog"])
        p.op("vector", lambda e: e.reduce_max(tv[:, 0:1], plog[:], axis=AX.X),
             reads=["plog"], writes=["tv0"])
        p.op("scalar", lambda e: e.activation(tv[:, 1:2], tv[:, 0:1], AF.Sigmoid),
             reads=["tv0"], writes=["tv1"])
        p.op("vector", lambda e: e.tensor_scalar(
            mt[:], plog[:], tv[:, 0:1], tv[:, 1:2], ALU.is_equal, ALU.mult),
             reads=["plog", "tv0", "tv1"], writes=["mt"])
        p.op("tensor", lambda e: e.transpose(ptr8[:], mt[:], idb[:]),
             reads=["mt", "c_id_bf"], writes=["ptr8"])
        p.op("vector", lambda e, t=t: e.tensor_copy(mhm[:, t * 128:(t + 1) * 128], ptr8[:]),
             reads=["ptr8"], writes=["mhm"])

    def ffn_branch(first, rhs_fn, pre_tb=None):
        for tb in range(4):
            if pre_tb is not None:
                pre_tb(tb)
            for ic in range(16):
                for kc in range(8):
                    rap, rkey = rhs_fn(kc, tb)
                    p.op("tensor", lambda e, ic=ic, kc=kc, rap=rap: e.matmul(
                        pg[:], wg[:, kc * INNER + ic * 128: kc * INNER + (ic + 1) * 128],
                        rap, start=(kc == 0), stop=(kc == 7)),
                         reads=["wg", rkey], writes=["pg"])
                for kc in range(8):
                    rap, rkey = rhs_fn(kc, tb)
                    p.op("tensor", lambda e, ic=ic, kc=kc, rap=rap: e.matmul(
                        pu[:], wu[:, kc * INNER + ic * 128: kc * INNER + (ic + 1) * 128],
                        rap, start=(kc == 0), stop=(kc == 7)),
                         reads=["wu", rkey], writes=["pu"])
                p.op("scalar", lambda e: e.activation(tb_bf[:], pg[:], AF.Silu),
                     reads=["pg"], writes=["tb_bf"])
                p.op("vector", lambda e, ic=ic: e.tensor_tensor(
                    actb[:, ic * 512:(ic + 1) * 512], tb_bf[:], pu[:], ALU.mult),
                     reads=["tb_bf", "pu"], writes=["actb" + str(ic % 4)])
            for dmc in range(8):
                for ic in range(16):
                    p.op("tensor", lambda e, dmc=dmc, ic=ic: e.matmul(
                        pd[:], wd[:, ic * D + dmc * 128: ic * D + (dmc + 1) * 128],
                        actb[:, ic * 512:(ic + 1) * 512],
                        start=(ic == 0), stop=(ic == 15)),
                         reads=["wd", "actb0", "actb1", "actb2", "actb3"],
                         writes=["pd"])
                ysl = y_sb[:, dmc * T + tb * 512: dmc * T + (tb + 1) * 512]
                if first:
                    p.op("vector", lambda e, ysl=ysl: e.tensor_copy(ysl, pd[:]),
                         reads=["pd"], writes=[f"y{dmc}_{tb}"])
                else:
                    p.op("vector", lambda e, ysl=ysl: e.scalar_tensor_tensor(
                        ysl, pd[:], 1.0, ysl, ALU.mult, ALU.add),
                         reads=["pd", f"y{dmc}_{tb}"], writes=[f"y{dmc}_{tb}"])

    p.dma("gpsimd", wg[:], shg_d[:], writes=["wg"])
    p.dma("gpsimd", wu[:], shu_d[:], writes=["wu"])
    p.dma("gpsimd", wd[:], shd_d[:], writes=["wd"])
    ffn_branch(True, lambda kc, tb: (xnf[:, kc * T + tb * 512: kc * T + (tb + 1) * 512], "xnf"))

    for ex in range(8):
        p.dma("gpsimd", wg[:], rtg_d[ex], writes=["wg"])
        p.dma("gpsimd", wu[:], rtu_d[ex], writes=["wu"])
        p.dma("gpsimd", wd[:], rtd_d[ex], writes=["wd"])
        for tb in range(4):
            p.op("tensor", lambda e, ex=ex, tb=tb: e.matmul(
                pmsk[:], sel8[:, ex * 128:(ex + 1) * 128],
                mhm[:, tb * 512:(tb + 1) * 512], start=True, stop=True),
                 reads=["c_sel8", "mhm"], writes=["pmsk"])
            p.op("vector", lambda e, tb=tb: e.tensor_copy(
                msk[:, tb * 512:(tb + 1) * 512], pmsk[:]),
                 reads=["pmsk"], writes=["msk"])
        def pre_tb(tb):
            for kc in range(8):
                p.op("vector", lambda e, kc=kc, tb=tb: e.tensor_tensor(
                    xets[:, kc * 512:(kc + 1) * 512],
                    xnf[:, kc * T + tb * 512: kc * T + (tb + 1) * 512],
                    msk[:, tb * 512:(tb + 1) * 512], ALU.mult),
                     reads=["xnf", "msk"], writes=["xets"])
        ffn_branch(False, lambda kc, tb: (xets[:, kc * 512:(kc + 1) * 512], "xets"),
                   pre_tb)

    for dmc in range(8):
        p.op("scalar", lambda e, dmc=dmc: e.activation(
            y_sb[:, dmc * T:(dmc + 1) * T], y_sb[:, dmc * T:(dmc + 1) * T], AF.Silu),
             reads=[f"y{dmc}_0", f"y{dmc}_1", f"y{dmc}_2", f"y{dmc}_3"],
             writes=[f"y{dmc}_0", f"y{dmc}_1", f"y{dmc}_2", f"y{dmc}_3"])
        p.dma("sync", g_o[:].rearrange("p (a b) -> p a b", b=T)[:, dmc, :],
              y_sb[:, dmc * T:(dmc + 1) * T],
              reads=[f"y{dmc}_0", f"y{dmc}_1", f"y{dmc}_2", f"y{dmc}_3"])
    with nc.Block() as block:
        p.finalize(block)
    nc.finalize()
    return nc


def host_M(xn_cores, inputs):
    gw = prep_lhsT(np.asarray(inputs["L0_ffn_gate_w"], np.float32))
    shg = prep_lhsT(np.asarray(inputs["L0_ffn_sh_gate"], np.float32))
    shu = prep_lhsT(np.asarray(inputs["L0_ffn_sh_up"], np.float32))
    shd = prep_lhsT(np.asarray(inputs["L0_ffn_sh_down"], np.float32))
    rtg = np.stack([prep_lhsT(np.asarray(inputs["L0_ffn_rt_gate"][e], np.float32)) for e in range(8)])
    rtu = np.stack([prep_lhsT(np.asarray(inputs["L0_ffn_rt_up"][e], np.float32)) for e in range(8)])
    rtd = np.stack([prep_lhsT(np.asarray(inputs["L0_ffn_rt_down"][e], np.float32)) for e in range(8)])
    return [{"xn": xn_cores[c], "gw": gw, "shg": shg, "shu": shu, "shd": shd,
             "rtg": rtg, "rtu": rtu, "rtd": rtd} for c in range(N_CORES)]


def build_C():
    nc = bass.Bass("TRN2", target_bir_lowering=False)
    es = ExitStack()
    consts = make_consts(nc)

    yn_d = nc.dram_tensor("yn", [128, 16 * T], BF16, kind="ExternalInput").ap()
    sz_d = nc.dram_tensor("sz", [128, 16 * T], BF16, kind="ExternalInput").ap()
    g_d = nc.dram_tensor("g", [128, 8 * T], BF16, kind="ExternalInput").ap()
    x_d = nc.dram_tensor("x", [T, D], F32, kind="ExternalInput").ap()
    wo_d = nc.dram_tensor("woT", [128, 16 * D], F32, kind="ExternalInput").ap()
    x1_o = nc.dram_tensor("x1_o", [T, D], F32, kind="ExternalOutput").ap()

    sb = nc.alloc_sbuf_tensor
    ps = nc.alloc_psum_tensor
    p = Prog(nc, es)
    idb = load_const(p, nc, consts, "id_bf", [128, 128], BF16)

    gf = sb("gf", [128, 8 * T], BF16)
    wo = sb("wo", [128, 16 * D], BF16)
    ytb = sb("ytb", [128, 16 * 512], BF16)
    ztb = sb("ztb", [128, 16 * 512], BF16)
    ob = sb("ob", [128, 16 * 512], BF16)
    sg = sb("sg", [128, 512], BF16)
    xin = sb("xin", [128, 4 * D], F32)
    pd = ps("pd", [128, 512], F32)
    ptr = ps("ptr", [128, 128], BF16)

    p.dma("sync", gf[:], g_d[:], writes=["gf"])
    p.dma("gpsimd", wo[:], wo_d[:], writes=["wo"])
    for tb in range(4):
        p.dma("sync", ytb[:].rearrange("p (a b) -> p a b", b=512),
              yn_d[:].rearrange("p (a b) -> p a b", b=T)[:, :, tb * 512:(tb + 1) * 512],
              writes=["ytb"])
        p.dma("sync", ztb[:].rearrange("p (a b) -> p a b", b=512),
              sz_d[:].rearrange("p (a b) -> p a b", b=T)[:, :, tb * 512:(tb + 1) * 512],
              writes=["ztb"])
        p.dma("sync", xin[:].rearrange("p (a b) -> p a b", b=D),
              x_d[:].rearrange("(s p) d -> p s d", p=128)[:, tb * 4:(tb + 1) * 4, :],
              writes=["xin"])
        for ic in range(16):
            p.op("vector", lambda e, ic=ic: e.tensor_tensor(
                ob[:, ic * 512:(ic + 1) * 512], ytb[:, ic * 512:(ic + 1) * 512],
                ztb[:, ic * 512:(ic + 1) * 512], ALU.mult),
                 reads=["ytb", "ztb"], writes=["ob" + str(ic % 4)])
        for dmc in range(8):
            for ic in range(16):
                p.op("tensor", lambda e, dmc=dmc, ic=ic: e.matmul(
                    pd[:], wo[:, ic * D + dmc * 128: ic * D + (dmc + 1) * 128],
                    ob[:, ic * 512:(ic + 1) * 512],
                    start=(ic == 0), stop=(ic == 15)),
                     reads=["wo", "ob0", "ob1", "ob2", "ob3"], writes=["pd"])
            p.op("vector", lambda e, dmc=dmc, tb=tb: e.tensor_tensor(
                sg[:], pd[:], gf[:, dmc * T + tb * 512: dmc * T + (tb + 1) * 512],
                ALU.mult), reads=["pd", "gf"], writes=["sg"])
            for st in range(4):
                p.op("tensor", lambda e, st=st: e.transpose(
                    ptr[:], sg[:, st * 128:(st + 1) * 128], idb[:]),
                     reads=["sg", "c_id_bf"], writes=["ptr"])
                p.op("vector", lambda e, st=st, dmc=dmc: e.tensor_tensor(
                    xin[:, st * D + dmc * 128: st * D + (dmc + 1) * 128],
                    xin[:, st * D + dmc * 128: st * D + (dmc + 1) * 128],
                    ptr[:], ALU.add),
                     reads=["ptr", "xin"], writes=["xin"])
        p.dma("sync", x1_o[:].rearrange("(s p) d -> p s d", p=128)[:, tb * 4:(tb + 1) * 4, :],
              xin[:].rearrange("p (a b) -> p a b", b=D), reads=["xin"])
    with nc.Block() as block:
        p.finalize(block)
    nc.finalize()
    return nc


def host_C(yn_cores, sz_cores, g_cores, inputs):
    woT = prep_lhsT(np.ascontiguousarray(np.asarray(inputs["L0_attn_Wout"], np.float32).T))
    x = np.asarray(inputs["L0_attn_input_tensor"])
    maps = []
    for c in range(N_CORES):
        b_, k = c // 4, c % 4
        maps.append({"yn": yn_cores[c], "sz": sz_cores[c], "g": g_cores[c],
                     "x": np.ascontiguousarray(x[b_, k * T:(k + 1) * T]), "woT": woT})
    return maps


_prog_cache = {}

def _get(name, builder):
    if name not in _prog_cache:
        _prog_cache[name] = builder()
    return _prog_cache[name]


def _run(nc, maps):
    res = run_bass_kernel_spmd(nc, maps, core_ids=list(range(N_CORES)))
    return res.results


def run_layer0(inputs):
    rA = _run(_get("A", build_A), host_A(inputs))
    xc_g = np.zeros((B, S, INNER), BF)
    b_g = np.zeros((B, S, 512), BF)
    c_g = np.zeros((B, S, 512), BF)
    dt_g = np.zeros((B, S, 8), np.float32)
    xn_cores, xlo_cores, sz_cores = [], [], []
    for c in range(N_CORES):
        b_, k = c // 4, c % 4
        rows = slice(k * T, (k + 1) * T)
        r = rA[c]
        xc_g[b_, rows] = fm_to(np.asarray(r["xc_o"]), T, INNER)
        b_g[b_, rows] = np.asarray(r["b_o"]).reshape(64, 8, T).transpose(2, 1, 0).reshape(T, 512)
        c_g[b_, rows] = np.asarray(r["c_o"]).reshape(64, 8, T).transpose(2, 1, 0).reshape(T, 512)
        dt_g[b_, rows] = np.asarray(r["dt_o"]).T
        xn_cores.append(np.asarray(r["xn_o"]))
        xlo_cores.append(np.asarray(r["xlo_o"]))
        sz_cores.append(np.asarray(r["sz_o"]))

    rB = _run(_get("B", build_B), host_B(xc_g, b_g, c_g, dt_g, inputs))
    yn_g = np.zeros((B, S, INNER), BF)
    for c in range(N_CORES):
        r = np.asarray(rB[c]["y_o"])
        for u in range(2):
            bb, hh = divmod(2 * c + u, HEADS)
            yn_g[bb, :, hh * HDIM:(hh + 1) * HDIM] =                 r[u].reshape(128, 64, HDIM).transpose(1, 0, 2).reshape(S, HDIM)

    mapsM = host_M(xn_cores, inputs)
    for c in range(N_CORES):
        mapsM[c]["xlo"] = xlo_cores[c]
    rM = _run(_get("M", build_M), mapsM)
    g_cores = [np.asarray(rM[c]["g_o"]) for c in range(N_CORES)]

    yn_cores = []
    for c in range(N_CORES):
        b_, k = c // 4, c % 4
        yn_cores.append(fm_from(yn_g[b_, k * T:(k + 1) * T]))
    rC = _run(_get("C", build_C), host_C(yn_cores, sz_cores, g_cores, inputs))
    x1 = np.zeros((B, S, D), np.float32)
    for c in range(N_CORES):
        b_, k = c // 4, c % 4
        x1[b_, k * T:(k + 1) * T] = np.asarray(rC[c]["x1_o"])
    return {"x1": x1}


def build_D():
    nc = bass.Bass("TRN2", target_bir_lowering=False)
    es = ExitStack()
    consts = make_consts(nc)

    x1_d = nc.dram_tensor("x1", [T, D], F32, kind="ExternalInput").ap()
    wq_d = nc.dram_tensor("wq", [128, 8 * D], F32, kind="ExternalInput").ap()
    wk_d = nc.dram_tensor("wk", [128, 8 * D], F32, kind="ExternalInput").ap()
    wv_d = nc.dram_tensor("wv", [128, 8 * D], F32, kind="ExternalInput").ap()
    q_o = nc.dram_tensor("q_o", [128, 8 * T], BF16, kind="ExternalOutput").ap()
    k_o = nc.dram_tensor("k_o", [128, 8 * T], BF16, kind="ExternalOutput").ap()
    v_o = nc.dram_tensor("v_o", [128, 16 * D], BF16, kind="ExternalOutput").ap()
    xn1_o = nc.dram_tensor("xn1_o", [T, D], F32, kind="ExternalOutput").ap()

    sb = nc.alloc_sbuf_tensor
    ps = nc.alloc_psum_tensor
    p = Prog(nc, es)
    idb = load_const(p, nc, consts, "id_bf", [128, 128], BF16)

    xnf = sb("xnf", [128, 8 * T], BF16)
    wql = sb("wql", [128, 8 * D], BF16)
    wkl = sb("wkl", [128, 8 * D], BF16)
    wvl = sb("wvl", [128, 8 * D], BF16)
    qf = sb("qf", [128, 8 * T], BF16)
    kf = sb("kf", [128, 8 * T], BF16)
    vt = sb("vt", [128, 16 * D], BF16)
    xt = sb("xt", [128, D], F32)
    xn32 = sb("xn32", [128, D], F32)
    xnb = sb("xnb", [128, D], BF16)
    mcol = sb("mcol", [128, 1], F32)
    vcol = sb("vcol", [128, 4], F32)
    sq = sb("sq", [128, D], F32)
    pm = ps("pm", [128, 512], F32)
    ptr = ps("ptr", [128, 128], BF16)

    p.dma("gpsimd", wql[:], wq_d[:], writes=["wql"])
    p.dma("gpsimd", wkl[:], wk_d[:], writes=["wkl"])
    p.dma("gpsimd", wvl[:], wv_d[:], writes=["wvl"])

    for t in range(16):
        p.dma("sync", xt[:], x1_d[t * 128:(t + 1) * 128, :], writes=["xt"])
        p.op("vector", lambda e: e.reduce_sum(mcol[:], xt[:], axis=AX.X),
             reads=["xt"], writes=["mcol"])
        p.op("scalar", lambda e: e.activation(sq[:], xt[:], AF.Square),
             reads=["xt"], writes=["sq"])
        p.op("vector", lambda e: e.reduce_sum(vcol[:, 0:1], sq[:], axis=AX.X),
             reads=["sq"], writes=["vcol"])
        p.op("vector", lambda e: e.tensor_scalar(mcol[:], mcol[:], 1.0 / D, None, ALU.mult),
             reads=["mcol"], writes=["mcol"])
        p.op("scalar", lambda e: e.activation(vcol[:, 1:2], mcol[:], AF.Square),
             reads=["mcol"], writes=["vcol2"])
        p.op("vector", lambda e: e.scalar_tensor_tensor(
            vcol[:, 2:3], vcol[:, 0:1], 1.0 / D, vcol[:, 1:2], ALU.mult, ALU.subtract),
             reads=["vcol", "vcol2"], writes=["vcol3"])
        p.op("vector", lambda e: e.tensor_scalar(vcol[:, 2:3], vcol[:, 2:3], EPS, None, ALU.add),
             reads=["vcol3"], writes=["vcol3"])
        p.op("scalar", lambda e: e.activation(vcol[:, 3:4], vcol[:, 2:3], AF.Ln),
             reads=["vcol3"], writes=["vcol4"])
        p.op("scalar", lambda e: e.activation(vcol[:, 3:4], vcol[:, 3:4], AF.Exp, scale=-0.5),
             reads=["vcol4"], writes=["vcol4"])
        p.op("vector", lambda e: e.tensor_scalar(
            xn32[:], xt[:], mcol[:], vcol[:, 3:4], ALU.subtract, ALU.mult),
             reads=["xt", "mcol", "vcol4"], writes=["xn32"])
        p.dma("sync", xn1_o[t * 128:(t + 1) * 128, :], xn32[:], reads=["xn32"])
        p.op("vector", lambda e: e.tensor_copy(xnb[:], xn32[:]),
             reads=["xn32"], writes=["xnb"])
        for kc in range(8):
            p.op("tensor", lambda e, kc=kc: e.transpose(
                ptr[:], xnb[:, kc * 128:(kc + 1) * 128], idb[:]),
                 reads=["xnb", "c_id_bf"], writes=["ptr"])
            p.op("vector", lambda e, kc=kc, t=t: e.tensor_copy(
                xnf[:, kc * T + t * 128: kc * T + (t + 1) * 128], ptr[:]),
                 reads=["ptr"], writes=["xnf"])
    for (wl, wkey, dst, dkey, scale) in ((wql, "wql", qf, "qf", 0.125),
                                         (wkl, "wkl", kf, "kf", None)):
        for ic in range(8):
            for tb in range(4):
                for kc in range(8):
                    p.op("tensor", lambda e, ic=ic, tb=tb, kc=kc, wl=wl: e.matmul(
                        pm[:], wl[:, kc * D + ic * 128: kc * D + (ic + 1) * 128],
                        xnf[:, kc * T + tb * 512: kc * T + (tb + 1) * 512],
                        start=(kc == 0), stop=(kc == 7)),
                         reads=[wkey, "xnf"], writes=["pm"])
                dsl = dst[:, ic * T + tb * 512: ic * T + (tb + 1) * 512]
                if scale is None:
                    p.op("vector", lambda e, dsl=dsl: e.tensor_copy(dsl, pm[:]),
                         reads=["pm"], writes=[dkey])
                else:
                    p.op("vector", lambda e, dsl=dsl, scale=scale: e.tensor_scalar(
                        dsl, pm[:], scale, None, ALU.mult),
                         reads=["pm"], writes=[dkey])
    for t in range(16):
        for vb in range(2):
            for kc in range(8):
                p.op("tensor", lambda e, t=t, vb=vb, kc=kc: e.matmul(
                    pm[:], xnf[:, kc * T + t * 128: kc * T + (t + 1) * 128],
                    wvl[:, kc * D + vb * 512: kc * D + (vb + 1) * 512],
                    start=(kc == 0), stop=(kc == 7)),
                     reads=["xnf", "wvl"], writes=["pm"])
            p.op("vector", lambda e, t=t, vb=vb: e.tensor_copy(
                vt[:, t * D + vb * 512: t * D + (vb + 1) * 512], pm[:]),
                 reads=["pm"], writes=["vt"])
    p.dma("sync", q_o[:], qf[:], reads=["qf"])
    p.dma("sync", k_o[:], kf[:], reads=["kf"])
    p.dma("sync", v_o[:], vt[:], reads=["vt"])
    with nc.Block() as block:
        p.finalize(block)
    nc.finalize()
    return nc


def host_D(x1, inputs):
    wq = prep_lhsT(np.asarray(inputs["L1_attn_Wq"], np.float32))
    wk = prep_lhsT(np.asarray(inputs["L1_attn_Wk"], np.float32))
    wv = prep_lhsT(np.asarray(inputs["L1_attn_Wv"], np.float32))
    maps = []
    for c in range(N_CORES):
        b_, k = c // 4, c % 4
        maps.append({"x1": np.ascontiguousarray(x1[b_, k * T:(k + 1) * T]),
                     "wq": wq, "wk": wk, "wv": wv})
    return maps


_MIDS = (0.25, 0.75, 1.25, 1.75, 2.5, 3.5, 5.0)
_DELT = (0.5, 0.5, 0.5, 0.5, 1.0, 1.0, 2.0)

def emit_fp4(p, src, dst, bufs, P, n, key, src_key=None):
    nb = n // 16
    sc, th, ax, sg, cb1, cb2, accv, accg = (bufs[x] for x in
        ("sc", "th", "ax", "sg", "cb1", "cb2", "accv", "accg"))
    k_src, k_dst = (src_key or key + "_src"), key + "_dst"
    ksc, kax, ksg = key + "_sc", key + "_ax", key + "_sg"
    kc1, kc2, kav, kag = key + "_c1", key + "_c2", key + "_av", key + "_ag"
    def v3(ap):
        return ap.rearrange("p (a b) -> p a b", b=16)
    p.op("vector", lambda e: e.reduce_max(sc[0:P, 0:nb], v3(src), axis=AX.X,
         apply_absolute_value=True), reads=[k_src], writes=[ksc])
    p.op("vector", lambda e: e.tensor_scalar(sc[0:P, 0:nb], sc[0:P, 0:nb],
         1.0 / 6.0, 1e-12, ALU.mult, ALU.max), reads=[ksc], writes=[ksc])
    p.op("scalar", lambda e: e.activation(ax[0:P, 0:n], src, AF.Abs),
         reads=[k_src], writes=[kax])
    p.op("scalar", lambda e: e.activation(sg[0:P, 0:n], src, AF.Sign),
         reads=[k_src], writes=[ksg])
    for i in range(7):
        p.op("vector", lambda e, i=i: e.tensor_scalar(
            th[0:P, i * nb:(i + 1) * nb], sc[0:P, 0:nb], _MIDS[i], None, ALU.mult),
             reads=[ksc], writes=[key + "_th" + str(i)])
    def cmp(i, dstbuf, dkey):
        p.op("vector", lambda e, i=i, dstbuf=dstbuf: e.tensor_tensor(
            v3(dstbuf[0:P, 0:n]), v3(ax[0:P, 0:n]),
            th[0:P, i * nb:(i + 1) * nb].unsqueeze(2).broadcast_to([P, nb, 16]),
            ALU.is_gt), reads=[kax, key + "_th" + str(i)], writes=[dkey])
    for j in range(3):
        cmp(2 * j, cb1, kc1)
        cmp(2 * j + 1, cb2, kc2)
        acc, kacc = (accv, kav) if j == 2 else (accg, kag)
        if j == 0:
            p.op("vector", lambda e: e.tensor_add(accg[0:P, 0:n], cb1[0:P, 0:n],
                 cb2[0:P, 0:n]), reads=[kc1, kc2], writes=[kag])
        elif j == 1:
            p.op("vector", lambda e: e.tensor_add(accg[0:P, 0:n], accg[0:P, 0:n],
                 cb1[0:P, 0:n]), reads=[kag, kc1], writes=[kag])
            p.op("vector", lambda e: e.tensor_add(accg[0:P, 0:n], accg[0:P, 0:n],
                 cb2[0:P, 0:n]), reads=[kag, kc2], writes=[kag])
        else:
            p.op("vector", lambda e: e.tensor_scalar(accg[0:P, 0:n],
                 accg[0:P, 0:n], 0.5, None, ALU.mult), reads=[kag], writes=[kag])
            p.op("vector", lambda e: e.tensor_add(accv[0:P, 0:n], cb1[0:P, 0:n],
                 cb2[0:P, 0:n]), reads=[kc1, kc2], writes=[kav])
    cmp(6, cb1, kc1)
    p.op("vector", lambda e: e.scalar_tensor_tensor(accv[0:P, 0:n], cb1[0:P, 0:n],
         2.0, accv[0:P, 0:n], ALU.mult, ALU.add), reads=[kc1, kav], writes=[kav])
    p.op("vector", lambda e: e.tensor_add(accv[0:P, 0:n], accv[0:P, 0:n],
         accg[0:P, 0:n]), reads=[kav, kag], writes=[kav])
    p.op("vector", lambda e: e.tensor_tensor(accv[0:P, 0:n], accv[0:P, 0:n],
         sg[0:P, 0:n], ALU.mult), reads=[kav, ksg], writes=[kav])
    p.op("vector", lambda e: e.tensor_tensor(
        v3(dst), v3(accv[0:P, 0:n]),
        sc[0:P, 0:nb].unsqueeze(2).broadcast_to([P, nb, 16]), ALU.mult),
         reads=[kav, ksc], writes=[k_dst])


def mk_fp4_bufs(nc, n, tag=""):
    sb = nc.alloc_sbuf_tensor
    return {"sc": sb("q_sc" + tag, [128, n // 16], F32),
            "th": sb("q_th" + tag, [128, 7 * (n // 16)], F32),
            "ax": sb("q_ax" + tag, [128, n], F32),
            "sg": sb("q_sg" + tag, [128, n], BF16),
            "cb1": sb("q_c1" + tag, [128, n], BF16),
            "cb2": sb("q_c2" + tag, [128, n], BF16),
            "accv": sb("q_av" + tag, [128, n], BF16),
            "accg": sb("q_ag" + tag, [128, n], BF16)}


def build_W():
    nc = bass.Bass("TRN2", target_bir_lowering=False)
    es = ExitStack()
    wg_d = nc.dram_tensor("wgs", [352, 1024], F32, kind="ExternalInput").ap()
    wu_d = nc.dram_tensor("wus", [352, 1024], F32, kind="ExternalInput").ap()
    wd_d = nc.dram_tensor("wds", [128, FI], F32, kind="ExternalInput").ap()
    wgq_o = nc.dram_tensor("wgq_o", [352, 1024], BF16, kind="ExternalOutput").ap()
    wuq_o = nc.dram_tensor("wuq_o", [352, 1024], BF16, kind="ExternalOutput").ap()
    wdq_o = nc.dram_tensor("wdq_o", [128, FI], BF16, kind="ExternalOutput").ap()
    sb = nc.alloc_sbuf_tensor
    p = Prog(nc, es)
    bufs = mk_fp4_bufs(nc, FI)
    srcb = sb("srcb", [128, FI], F32)
    dstb = sb("dstb", [128, FI], BF16)
    ji = 0
    for (src_d, dst_d, rows, n) in ((wg_d, wgq_o, 352, 1024), (wu_d, wuq_o, 352, 1024),
                                    (wd_d, wdq_o, 128, FI)):
        r0 = 0
        while r0 < rows:
            P = min(128, rows - r0)
            p.dma("sync", srcb[0:P, 0:n], src_d[r0:r0 + P, :], writes=["q0_src"])
            emit_fp4(p, srcb[0:P, 0:n], dstb[0:P, 0:n], bufs, P, n, "q0")
            p.dma("sync", dst_d[r0:r0 + P, :], dstb[0:P, 0:n], reads=["q0_dst"])
            r0 += P
            ji += 1
    with nc.Block() as block:
        p.finalize(block)
    nc.finalize()
    return nc


def host_W(inputs):
    wg = np.asarray(inputs["L1_ffn_w_gate"], np.float32)
    wu = np.asarray(inputs["L1_ffn_w_up"], np.float32)
    wd = np.asarray(inputs["L1_ffn_w_down"], np.float32)
    maps = []
    for c in range(N_CORES):
        maps.append({"wgs": np.ascontiguousarray(wg[c * 352:(c + 1) * 352]),
                     "wus": np.ascontiguousarray(wu[c * 352:(c + 1) * 352]),
                     "wds": np.ascontiguousarray(wd[c * 128:(c + 1) * 128])})
    return maps


def build_E():
    nc = bass.Bass("TRN2", target_bir_lowering=False)
    es = ExitStack()
    consts = make_consts(nc)
    NKT = 64

    qu_d = [nc.dram_tensor(f"qu{u}", [64, S], BF16, kind="ExternalInput").ap() for u in range(4)]
    ku_d = [nc.dram_tensor(f"ku{u}", [64, S], BF16, kind="ExternalInput").ap() for u in range(4)]
    vu_d = [nc.dram_tensor(f"vu{u}", [128, NKT * AHD], BF16, kind="ExternalInput").ap() for u in range(4)]
    o_o = nc.dram_tensor("o_o", [4, 128, NKT * AHD], BF16, kind="ExternalOutput").ap()

    sb = nc.alloc_sbuf_tensor
    ps = nc.alloc_psum_tensor
    p = Prog(nc, es)
    bm = load_const(p, nc, consts, "bigmask", [128, 896], F32)

    qu = [sb(f"qus{i}", [64, S], BF16) for i in range(2)]
    ku = [sb(f"kus{i}", [64, S], BF16) for i in range(2)]
    vsb = [sb(f"vsbs{i}", [128, NKT * 65], BF16) for i in range(2)]
    pbf = [sb(f"pbf{i}", [128, 1024], BF16) for i in range(2)]
    ob = [sb(f"ob{i}", [128, NKT * AHD], BF16) for i in range(2)]
    rc = sb("rc", [128, 1], F32)
    psl = [ps(f"ps{i}", [128, 1024], F32) for i in range(2)]
    pacc = [ps(f"pa{i}", [128, 65], F32) for i in range(4)]

    p.op("vector", lambda e: e.memset(vsb[0][:], 1.0), reads=[], writes=["vsb0"])
    p.op("vector", lambda e: e.memset(vsb[1][:], 1.0), reads=[], writes=["vsb1"])
    for u in range(4):
        w = u % 2
        quw, kuw, vsw, obw = qu[w], ku[w], vsb[w], ob[w]
        kq, kk, kv, ko = f"qu{w}", f"ku{w}", f"vsb{w}", f"ob{w}"
        p.dma("sync", quw[:], qu_d[u][:], writes=[kq])
        p.dma("sync", kuw[:], ku_d[u][:], writes=[kk])
        p.dma("gpsimd", vsw[:].rearrange("p (a b) -> p a b", b=65)[:, :, 0:64],
              vu_d[u][:].rearrange("p (a b) -> p a b", b=AHD), writes=[kv])

        def emit_av(pr, nkt, vsw=vsw, kv=kv):
            pb = pbf[pr % 2]
            for qs in range(4):
                for h in range(2):
                    kt = 2 * pr + h
                    p.op("tensor", lambda e, qs=qs, kt=kt, h=h, nkt=nkt, pb=pb, vsw=vsw: e.matmul(
                        pacc[qs][:], pb[:, h * 512 + qs * 128: h * 512 + (qs + 1) * 128],
                        vsw[:, kt * 65:(kt + 1) * 65],
                        start=(kt == 0), stop=(kt == nkt - 1)),
                         reads=[f"pbf{pr % 2}", kv], writes=[f"pa{qs}"])

        for qb in range(16):
            nkt = 4 * qb + 4
            npr = nkt // 2
            for pr in range(npr):
                pi = pr % 2
                sp = psl[pi]
                skey = f"ps{pi}"
                for h in range(2):
                    kt = 2 * pr + h
                    p.op("tensor", lambda e, kt=kt, qb=qb, h=h, sp=sp, kuw=kuw, quw=quw: e.matmul(
                        sp[:, h * 512:(h + 1) * 512], kuw[:, kt * 128:(kt + 1) * 128],
                        quw[:, qb * 512:(qb + 1) * 512], start=True, stop=True),
                         reads=[kk, kq], writes=[skey])
                    a = kt - 4 * qb
                    if a >= 0:
                        off = 384 - 128 * a
                        p.op("vector", lambda e, sp=sp, h=h, off=off: e.tensor_tensor(
                            sp[:, h * 512:(h + 1) * 512], sp[:, h * 512:(h + 1) * 512],
                            bm[:, off:off + 512], ALU.add),
                             reads=[skey, "c_bigmask"], writes=[skey])
                p.op("scalar", lambda e, sp=sp, pi=pi: e.activation(
                    pbf[pi][:], sp[:], AF.Exp),
                     reads=[skey], writes=[f"pbf{pi}"])
                if pr >= 1:
                    emit_av(pr - 1, nkt)
            emit_av(npr - 1, nkt)
            for qs in range(4):
                p.op("vector", lambda e, qs=qs: e.reciprocal(rc[:], pacc[qs][:, 64:65]),
                     reads=[f"pa{qs}"], writes=["rc"])
                ti = qb * 4 + qs
                p.op("vector", lambda e, qs=qs, ti=ti, obw=obw: e.tensor_scalar(
                    obw[:, ti * AHD:(ti + 1) * AHD], pacc[qs][:, 0:64], rc[:],
                    None, ALU.mult), reads=[f"pa{qs}", "rc"], writes=[ko])
        p.dma("sync", o_o[u], obw[:], reads=[ko])
    with nc.Block() as block:
        p.finalize(block)
    nc.finalize()
    return nc


def host_E(q_cores, k_cores, v_cores):
    qg = np.zeros((B, 2 * S, 512), BF)
    qf = np.zeros((B, D, S), BF)
    kf = np.zeros((B, D, S), BF)
    vt = np.zeros((B, S, D), BF)
    for c in range(N_CORES):
        b_, k = c // 4, c % 4
        rows = slice(k * T, (k + 1) * T)
        qf[b_, :, rows] = q_cores[c].reshape(128, 8, T).transpose(1, 0, 2).reshape(D, T)
        kf[b_, :, rows] = k_cores[c].reshape(128, 8, T).transpose(1, 0, 2).reshape(D, T)
        vt[b_, rows, :] = v_cores[c].reshape(128, 16, D).transpose(1, 0, 2).reshape(T, D)
    maps = []
    for c in range(N_CORES):
        m = {}
        for u in range(4):
            bb, hh = divmod(4 * c + u, AH)
            m[f"qu{u}"] = np.ascontiguousarray(qf[bb, hh * AHD:(hh + 1) * AHD, :])
            m[f"ku{u}"] = np.ascontiguousarray(kf[bb, hh * AHD:(hh + 1) * AHD, :])
            m[f"vu{u}"] = np.ascontiguousarray(
                vt[bb, :, hh * AHD:(hh + 1) * AHD].reshape(64, 128, AHD)
                .transpose(1, 0, 2).reshape(128, -1))
        maps.append(m)
    return maps


def build_F1a():
    nc = bass.Bass("TRN2", target_bir_lowering=False)
    es = ExitStack()
    consts = make_consts(nc)
    IB = (512, 512, 512, 512, 512, 256)

    xn1_d = nc.dram_tensor("xn1", [T, D], F32, kind="ExternalInput").ap()
    wg_d = nc.dram_tensor("wgqT", [128, 8 * FI], BF16, kind="ExternalInput").ap()
    wu_d = nc.dram_tensor("wuqT", [128, 8 * FI], BF16, kind="ExternalInput").ap()
    hq_o = nc.dram_tensor("hq_o", [128, 16 * FI], BF16, kind="ExternalOutput").ap()

    sb = nc.alloc_sbuf_tensor
    ps = nc.alloc_psum_tensor
    p = Prog(nc, es)
    idb = load_const(p, nc, consts, "id_bf", [128, 128], BF16)

    wgq = sb("wgq", [128, 8 * FI], BF16)
    wuq = sb("wuq", [128, 8 * FI], BF16)
    bufs_h = mk_fp4_bufs(nc, FI)
    bufs_x = mk_fp4_bufs(nc, D, tag="x")
    xts = sb("xts", [128, D], F32)
    xq = sb("xq", [128, D], BF16)
    xqf = [sb(f"xqf{i}", [128, D], BF16) for i in range(2)]
    hf = [sb(f"hf{i}", [128, FI], F32) for i in range(2)]
    hq = sb("hq", [128, FI], BF16)
    sil = sb("sil", [128, 512], BF16)
    pg = ps("pg", [128, 512], F32)
    pu = ps("pu", [128, 512], F32)
    ptr = [ps(f"ptr{i}", [128, 128], BF16) for i in range(2)]

    p.dma("gpsimd", wgq[:], wg_d[:], writes=["wgq"])
    p.dma("gpsimd", wuq[:], wu_d[:], writes=["wuq"])

    def emit_hq(t):
        hfw, khf = hf[t % 2], f"hf{t % 2}"
        emit_fp4(p, hfw[:], hq[:], bufs_h, 128, FI, "q1", src_key=khf)
        p.dma("sync", hq_o[:, t * FI:(t + 1) * FI], hq[:], reads=["q1_dst"])

    for t in range(16):
        hfw, khf = hf[t % 2], f"hf{t % 2}"
        xqfw, kxf = xqf[t % 2], f"xqf{t % 2}"
        p.dma("sync", xts[:], xn1_d[t * 128:(t + 1) * 128, :], writes=["q0_src"])
        emit_fp4(p, xts[:], xq[:], bufs_x, 128, D, "q0")
        for kc in range(8):
            pt, kpt = ptr[kc % 2], f"ptr{kc % 2}"
            p.op("tensor", lambda e, kc=kc, pt=pt: e.transpose(
                pt[:], xq[:, kc * 128:(kc + 1) * 128], idb[:]),
                 reads=["q0_dst", "c_id_bf"], writes=[kpt])
            p.op("vector", lambda e, kc=kc, pt=pt, xqfw=xqfw: e.tensor_copy(
                xqfw[:, kc * 128:(kc + 1) * 128], pt[:]),
                 reads=[kpt], writes=[kxf])
        if t >= 1:
            emit_hq(t - 1)
        off = 0
        for ib, w in enumerate(IB):
            for kc in range(8):
                p.op("tensor", lambda e, kc=kc, off=off, w=w, xqfw=xqfw: e.matmul(
                    pg[:, 0:w], xqfw[:, kc * 128:(kc + 1) * 128],
                    wgq[:, kc * FI + off: kc * FI + off + w],
                    start=(kc == 0), stop=(kc == 7)),
                     reads=[kxf, "wgq"], writes=["pg"])
            for kc in range(8):
                p.op("tensor", lambda e, kc=kc, off=off, w=w, xqfw=xqfw: e.matmul(
                    pu[:, 0:w], xqfw[:, kc * 128:(kc + 1) * 128],
                    wuq[:, kc * FI + off: kc * FI + off + w],
                    start=(kc == 0), stop=(kc == 7)),
                     reads=[kxf, "wuq"], writes=["pu"])
            p.op("scalar", lambda e, w=w: e.activation(sil[:, 0:w], pg[:, 0:w], AF.Silu),
                 reads=["pg"], writes=["sil"])
            p.op("vector", lambda e, off=off, w=w, hfw=hfw: e.tensor_tensor(
                hfw[:, off:off + w], sil[:, 0:w], pu[:, 0:w], ALU.mult),
                 reads=["sil", "pu"], writes=[khf])
            off += w
    emit_hq(15)
    with nc.Block() as block:
        p.finalize(block)
    nc.finalize()
    return nc


def build_F1b():
    nc = bass.Bass("TRN2", target_bir_lowering=False)
    es = ExitStack()
    NIC = 22
    hq_d = nc.dram_tensor("hqf", [128, NIC * T], BF16, kind="ExternalInput").ap()
    wd_d = nc.dram_tensor("wdqT", [128, NIC * D], BF16, kind="ExternalInput").ap()
    g1_o = nc.dram_tensor("g1_o", [128, 8 * T], BF16, kind="ExternalOutput").ap()
    sb = nc.alloc_sbuf_tensor
    ps = nc.alloc_psum_tensor
    p = Prog(nc, es)
    wdq = sb("wdq", [128, NIC * D], BF16)
    htb = sb("htb", [128, NIC * 512], BF16)
    g1 = sb("g1", [128, 8 * T], BF16)
    pd = ps("pd2", [128, 512], F32)
    p.dma("sync", wdq[:], wd_d[:], writes=["wdq"])
    for tb in range(4):
        p.dma("sync", htb[:].rearrange("p (a b) -> p a b", b=512),
              hq_d[:].rearrange("p (a b) -> p a b", b=T)[:, :, tb * 512:(tb + 1) * 512],
              writes=["htb"])
        for dmc in range(8):
            for ic in range(NIC):
                p.op("tensor", lambda e, dmc=dmc, ic=ic: e.matmul(
                    pd[:], wdq[:, ic * D + dmc * 128: ic * D + (dmc + 1) * 128],
                    htb[:, ic * 512:(ic + 1) * 512],
                    start=(ic == 0), stop=(ic == NIC - 1)),
                     reads=["wdq", "htb"], writes=["pd2"])
            p.op("scalar", lambda e, dmc=dmc, tb=tb: e.activation(
                g1[:, dmc * T + tb * 512: dmc * T + (tb + 1) * 512], pd[:], AF.Silu),
                 reads=["pd2"], writes=["g1"])
    p.dma("sync", g1_o[:], g1[:], reads=["g1"])
    with nc.Block() as block:
        p.finalize(block)
    nc.finalize()
    return nc


def build_F2():
    nc = bass.Bass("TRN2", target_bir_lowering=False)
    es = ExitStack()
    consts = make_consts(nc)

    o_d = nc.dram_tensor("of", [128, 8 * T], BF16, kind="ExternalInput").ap()
    g1_d = nc.dram_tensor("g1f", [128, 8 * T], BF16, kind="ExternalInput").ap()
    x1_d = nc.dram_tensor("x1", [T, D], F32, kind="ExternalInput").ap()
    wo_d = nc.dram_tensor("wo", [128, 8 * D], F32, kind="ExternalInput").ap()
    bo_d = nc.dram_tensor("bo", [128, 8], F32, kind="ExternalInput").ap()
    x2_o = nc.dram_tensor("x2_o", [T, D], F32, kind="ExternalOutput").ap()

    sb = nc.alloc_sbuf_tensor
    ps = nc.alloc_psum_tensor
    p = Prog(nc, es)
    idb = load_const(p, nc, consts, "id_bf", [128, 128], BF16)

    of = sb("of_s", [128, 8 * T], BF16)
    g1f = sb("g1f_s", [128, 8 * T], BF16)
    wol = sb("wol", [128, 8 * D], BF16)
    bos = sb("bos", [128, 8], F32)
    sg = sb("sg", [128, 512], BF16)
    xin = sb("xin", [128, 4 * D], F32)
    pd = ps("pd", [128, 512], F32)
    ptr = ps("ptr", [128, 128], BF16)

    p.dma("sync", of[:], o_d[:], writes=["of"])
    p.dma("sync", g1f[:], g1_d[:], writes=["g1f"])
    p.dma("gpsimd", wol[:], wo_d[:], writes=["wol"])
    p.dma("sync", bos[:], bo_d[:], writes=["bos"])
    for tb in range(4):
        p.dma("sync", xin[:].rearrange("p (a b) -> p a b", b=D),
              x1_d[:].rearrange("(s p) d -> p s d", p=128)[:, tb * 4:(tb + 1) * 4, :],
              writes=["xin"])
        for dmc in range(8):
            for ic in range(8):
                p.op("tensor", lambda e, dmc=dmc, ic=ic, tb=tb: e.matmul(
                    pd[:], wol[:, ic * D + dmc * 128: ic * D + (dmc + 1) * 128],
                    of[:, ic * T + tb * 512: ic * T + (tb + 1) * 512],
                    start=(ic == 0), stop=(ic == 7)),
                     reads=["wol", "of"], writes=["pd"])
            p.op("vector", lambda e, dmc=dmc, tb=tb: e.scalar_tensor_tensor(
                sg[:], pd[:], bos[:, dmc:dmc + 1],
                g1f[:, dmc * T + tb * 512: dmc * T + (tb + 1) * 512],
                ALU.add, ALU.mult), reads=["pd", "bos", "g1f"], writes=["sg"])
            for st in range(4):
                p.op("tensor", lambda e, st=st: e.transpose(
                    ptr[:], sg[:, st * 128:(st + 1) * 128], idb[:]),
                     reads=["sg", "c_id_bf"], writes=["ptr"])
                p.op("vector", lambda e, st=st, dmc=dmc: e.tensor_tensor(
                    xin[:, st * D + dmc * 128: st * D + (dmc + 1) * 128],
                    xin[:, st * D + dmc * 128: st * D + (dmc + 1) * 128],
                    ptr[:], ALU.add),
                     reads=["ptr", "xin"], writes=["xin"])
        p.dma("sync", x2_o[:].rearrange("(s p) d -> p s d", p=128)[:, tb * 4:(tb + 1) * 4, :],
              xin[:].rearrange("p (a b) -> p a b", b=D), reads=["xin"])
    with nc.Block() as block:
        p.finalize(block)
    nc.finalize()
    return nc


def run_layer1(x1, inputs):
    rD = _run(_get("D", build_D), host_D(x1, inputs))
    q_cores = [np.asarray(rD[c]["q_o"]) for c in range(N_CORES)]
    k_cores = [np.asarray(rD[c]["k_o"]) for c in range(N_CORES)]
    v_cores = [np.asarray(rD[c]["v_o"]) for c in range(N_CORES)]
    xn1_cores = [np.asarray(rD[c]["xn1_o"]) for c in range(N_CORES)]

    rW = _run(_get("W", build_W), host_W(inputs))
    wgq = np.concatenate([np.asarray(rW[c]["wgq_o"]) for c in range(N_CORES)], 0)
    wuq = np.concatenate([np.asarray(rW[c]["wuq_o"]) for c in range(N_CORES)], 0)
    wdq = np.concatenate([np.asarray(rW[c]["wdq_o"]) for c in range(N_CORES)], 0)

    rE = _run(_get("E", build_E), host_E(q_cores, k_cores, v_cores))
    og = np.zeros((B, S, D), BF)
    for c in range(N_CORES):
        r = np.asarray(rE[c]["o_o"])
        for u in range(4):
            bb, hh = divmod(4 * c + u, AH)
            og[bb, :, hh * AHD:(hh + 1) * AHD] =                 r[u].reshape(128, 64, AHD).transpose(1, 0, 2).reshape(S, AHD)

    wgqT = prep_lhsT(np.ascontiguousarray(wgq.T)).astype(BF)
    wuqT = prep_lhsT(np.ascontiguousarray(wuq.T)).astype(BF)
    wdqT = prep_lhsT(np.ascontiguousarray(wdq.T)).astype(BF)
    mapsF1a = [{"xn1": xn1_cores[c], "wgqT": wgqT, "wuqT": wuqT}
               for c in range(N_CORES)]
    rF1a = _run(_get("F1a", build_F1a), mapsF1a)
    mapsF1b = []
    for c in range(N_CORES):
        hq = np.asarray(rF1a[c]["hq_o"]).reshape(128, 16, FI).transpose(1, 0, 2).reshape(T, FI)
        mapsF1b.append({"hqf": fm_from(hq), "wdqT": wdqT})
    rF1b = _run(_get("F1b", build_F1b), mapsF1b)

    wo = prep_lhsT(np.asarray(inputs["L1_attn_Wo"], np.float32))
    bo = np.ascontiguousarray(np.asarray(inputs["L1_attn_bo"], np.float32).reshape(8, 128).T)
    mapsF2 = []
    for c in range(N_CORES):
        b_, k = c // 4, c % 4
        mapsF2.append({"of": fm_from(og[b_, k * T:(k + 1) * T]),
                       "g1f": np.asarray(rF1b[c]["g1_o"]),
                       "x1": np.ascontiguousarray(x1[b_, k * T:(k + 1) * T]),
                       "wo": wo, "bo": bo})
    rF2 = _run(_get("F2", build_F2), mapsF2)
    x2 = np.zeros((B, S, D), np.float32)
    for c in range(N_CORES):
        b_, k = c // 4, c % 4
        x2[b_, k * T:(k + 1) * T] = np.asarray(rF2[c]["x2_o"])
    return x2


def kernel(**inputs):
    st = run_layer0(inputs)
    x2 = run_layer1(st["x1"], inputs)
    return x2
